# version 01
# baseline (speedup 1.0000x reference)
import contextlib
import numpy as np
import ml_dtypes
import concourse.bass as bass
import concourse.mybir as mybir
from concourse.bass_utils import run_bass_kernel_spmd

F32 = mybir.dt.float32
BF16 = mybir.dt.bfloat16
FP8 = mybir.dt.float8e4
AF = mybir.ActivationFunctionType
ALU = mybir.AluOpType

B = 2
T = 2048
NTOK = B * T
D = 1024
NFT = D // 128
CHUNK = 64
NCH = T // CHUNK
LN_EPS = 1e-5
S_PAT = 32


class Ctx:
    def __init__(self, nc):
        self.nc = nc
        self.sem = {}
        self._stack = []

    def new_sem(self, name):
        h = self.nc.semaphore(name)
        handle = h.__enter__()
        self._stack.append(h)
        self.sem[name] = [handle, 0]

    def inc(self, name, by=1):
        s = self.sem[name]
        s[1] += by
        return s[0]

    def val(self, name):
        return self.sem[name][1]

    def h(self, name):
        return self.sem[name][0]

    def _stack_append(self, h):
        t = h.__enter__()
        self._stack.append(h)
        return t


def dma(ctx, q, out, in_, sem="dma"):
    q.dma_start(out=out, in_=in_).then_inc(ctx.inc(sem, 16), 16)
    return ctx.val(sem)


def wait_all(ctx, eng, *names):
    for n in names:
        eng.wait_ge(ctx.h(n), ctx.val(n))


def debug_dump(ctx, name, ap):
    if not getattr(ctx, "debug", False):
        return
    nc = ctx.nc
    shp = [p[1] for p in ap.ap]
    t = nc._nc.dram_tensor(f"dbg_{name}", shp, ap.tensor.dtype if hasattr(ap.tensor, 'dtype') else ap.dtype,
                           kind="ExternalOutput").ap()
    with nc.Block() as block:
        @block.sync
        def _(sync):
            wait_all(ctx, sync, "dma", "pe", "vec", "scl", "gps")
            dma(ctx, sync, t[:], ap)
            sync.wait_ge(ctx.h("dma"), ctx.val("dma"))


def barrier_block(ctx):
    nc = ctx.nc
    with nc.Block() as block:
        @block.sync
        def _(sync):
            wait_all(ctx, sync, "dma", "dmb", "dmg", "pe", "vec", "scl", "gps",
                     "dq0", "dq1")


def build_layernorm(ctx, src_dram, xn_tiles, gain=None, bias=None,
                    keep=None, xn_dram=None, fp32_dram=None):
    nc = ctx.nc
    xf = ctx.scr_f32[:, 0:NTOK]
    r_bc = ctx.scr_f32[:, NTOK:2 * NTOK]
    mr_bc = ctx.scr_f32b[:, 0:NTOK]
    stat = ctx.stat
    es_u = contextlib.ExitStack()
    if gain is not None:
        u_bc = es_u.enter_context(nc.sbuf_tensor("ln_u", [128, NTOK], F32))
    if fp32_dram is not None:
        x32buf = es_u.enter_context(nc.sbuf_tensor("ln_x32", [128, NTOK], F32))

    for half in range(2):
        hs = slice(half * T, (half + 1) * T)
        with (
            nc.psum_tensor("ln_ps1", [1, T], F32) as ps1,
            nc.psum_tensor("ln_ps2", [1, T], F32) as ps2,
            nc.Block() as block,
        ):
            q0 = [ctx.val("dq0"), ctx.val("dq1")]
            g0 = ctx.val("gps")
            s0 = ctx.val("scl")
            p0 = ctx.val("pe")
            xbf = ctx.scr_bf[:, 0:2 * T]
            sqbf = ctx.scr_bf[:, 2 * T:4 * T]

            @block.sync
            def _(sync):
                for ft in range(NFT):
                    if ft >= 2:
                        sync.wait_ge(ctx.h("gps"), g0 + ft - 1)
                        sync.wait_ge(ctx.h("scl"), s0 + ft - 1)
                    dma(ctx, sync, xf[:, (ft % 2) * T:(ft % 2) * T + T],
                        src_dram.ap()[ft, :, hs], sem=f"dq{ft % 2}")

            @block.vector
            def _(vector):
                for ft in range(NFT):
                    vector.wait_ge(ctx.h(f"dq{ft % 2}"), q0[ft % 2] + 16 * (ft // 2 + 1))
                    if ft >= 2:
                        vector.wait_ge(ctx.h("pe"), p0 + ft - 1)
                    vector.tensor_copy(xbf[:, (ft % 2) * T:(ft % 2) * T + T],
                                       xf[:, (ft % 2) * T:(ft % 2) * T + T]).then_inc(
                        ctx.inc("gps"), 1)

            @block.scalar
            def _(scalar):
                for ft in range(NFT):
                    scalar.wait_ge(ctx.h(f"dq{ft % 2}"), q0[ft % 2] + 16 * (ft // 2 + 1))
                    if ft >= 2:
                        scalar.wait_ge(ctx.h("pe"), p0 + ft - 1)
                    scalar.activation(sqbf[:, (ft % 2) * T:(ft % 2) * T + T],
                                      xf[:, (ft % 2) * T:(ft % 2) * T + T],
                                      AF.Square).then_inc(ctx.inc("scl"), 1)

            @block.tensor
            def _(tensor):
                for ft in range(NFT):
                    tensor.wait_ge(ctx.h("gps"), g0 + ft + 1)
                    tensor.wait_ge(ctx.h("scl"), s0 + ft + 1)
                    o = (ft % 2) * T
                    for ch in range(T // 512):
                        c = slice(ch * 512, (ch + 1) * 512)
                        co = slice(o + ch * 512, o + (ch + 1) * 512)
                        tensor.matmul(ps1[:, c], ctx.ones_col[:], xbf[:, co],
                                      start=(ft == 0), stop=(ft == NFT - 1))
                        i = tensor.matmul(ps2[:, c], ctx.ones_col[:], sqbf[:, co],
                                          start=(ft == 0), stop=(ft == NFT - 1))
                    i.then_inc(ctx.inc("pe"), 1)

            @block.vector
            def _(vector):
                vector.wait_ge(ctx.h("pe"), ctx.val("pe"))
                m = stat[:, 0:T]
                q = stat[:, T:2 * T]
                vector.tensor_scalar_mul(m, ps1[:], 1.0 / D)
                vector.tensor_scalar_mul(q, ps2[:], 1.0 / D)
                vector.drain()
                vector.tensor_mul(ps1[:], m, m)
                vector.drain()
                vector.tensor_sub(q, q, ps1[:]).then_inc(ctx.inc("vec"), 1)

            @block.scalar
            def _(scalar):
                scalar.wait_ge(ctx.h("vec"), ctx.val("vec"))
                scalar.activation(stat[:, T:2 * T], stat[:, T:2 * T],
                                  AF.Sqrt, bias=ctx.epscol[:]).then_inc(ctx.inc("scl"), 1)

            @block.vector
            def _(vector):
                vector.wait_ge(ctx.h("scl"), ctx.val("scl"))
                vector.reciprocal(stat[:, T:2 * T], stat[:, T:2 * T])
                vector.drain()
                vector.tensor_mul(stat[:, 0:T], stat[:, 0:T],
                                  stat[:, T:2 * T]).then_inc(ctx.inc("vec"), 1)

        with (
            nc.psum_tensor("ln_pb1", [128, T], F32) as pb1,
            nc.psum_tensor("ln_pb2", [128, T], F32) as pb2,
            nc.Block() as block,
        ):
            pp0 = ctx.val("pe")

            @block.tensor
            def _(tensor):
                tensor.wait_ge(ctx.h("vec"), ctx.val("vec"))
                for ch in range(T // 512):
                    c = slice(ch * 512, (ch + 1) * 512)
                    tensor.matmul(pb1[:, c], ctx.ones_row[:],
                                  stat[:, T + ch * 512:T + (ch + 1) * 512],
                                  start=True, stop=True)
                    i = tensor.matmul(pb2[:, c], ctx.ones_row[:],
                                      stat[:, ch * 512:(ch + 1) * 512],
                                      start=True, stop=True)
                i.then_inc(ctx.inc("pe"), 1)

            @block.vector
            def _(vector):
                vector.wait_ge(ctx.h("pe"), pp0 + 1)
                vector.tensor_copy(r_bc[:, hs], pb1[:])
                vector.tensor_copy(mr_bc[:, hs], pb2[:]).then_inc(ctx.inc("vec"), 1)

    with nc.Block() as block:
        d0 = ctx.val("dma")
        b0 = ctx.val("dmb")
        v0 = ctx.val("vec")

        @block.sync
        def _(sync):
            for ft in range(NFT):
                if ft >= 1:
                    sync.wait_ge(ctx.h("vec"), v0 + 2 * ft - 1)
                dma(ctx, sync, xf, src_dram.ap()[ft])

        v0 = ctx.val("vec")

        @block.vector
        def _(vector):
            for ft in range(NFT):
                vector.wait_ge(ctx.h("dma"), d0 + 16 * (ft + 1))
                nst = (1 if xn_dram is not None else 0) + \
                      (1 if fp32_dram is not None else 0)
                if nst and ft >= 1:
                    vector.wait_ge(ctx.h("dmb"), b0 + 16 * nst * ft)
                vector.tensor_mul(ctx.scr_xr[:, 0:NTOK], xf, r_bc).then_inc(
                    ctx.inc("vec"), 1)
                if gain is None:
                    if fp32_dram is not None:
                        vector.tensor_sub(x32buf[:], ctx.scr_xr[:, 0:NTOK], mr_bc)
                        vector.drain()
                        vector.tensor_copy(xn_tiles[ft][:], x32buf[:]).then_inc(
                            ctx.inc("vec"), 1)
                    else:
                        vector.tensor_sub(xn_tiles[ft][:], ctx.scr_xr[:, 0:NTOK],
                                          mr_bc).then_inc(ctx.inc("vec"), 1)
                else:
                    vector.tensor_scalar(u_bc[:], mr_bc, gain[ft], bias[ft],
                                         ALU.mult, ALU.subtract)
                    vector.scalar_tensor_tensor(
                        xn_tiles[ft][:], ctx.scr_xr[:, 0:NTOK], gain[ft], u_bc[:],
                        ALU.mult, ALU.subtract).then_inc(ctx.inc("vec"), 1)

        if xn_dram is not None or fp32_dram is not None:
            @block.scalar
            def _(scalar):
                for ft in range(NFT):
                    scalar.wait_ge(ctx.h("vec"), v0 + 2 * (ft + 1))
                    if xn_dram is not None:
                        dma(ctx, scalar, xn_dram.ap()[ft], xn_tiles[ft][:],
                            sem="dmb")
                    if fp32_dram is not None:
                        dma(ctx, scalar, fp32_dram.ap()[ft], x32buf[:],
                            sem="dmb")

    es_u.close()
    barrier_block(ctx)


def build_proj(ctx, xn_tiles, w_dram, out_nf, evac, nin_tiles=NFT,
               of0=0, evac_eng="vec", ntok=NTOK, conv_func=None):
    nc = ctx.nc
    nch = ntok // 512
    with (
        nc.sbuf_tensor("pj_wf", [128, 2 * 128], F32) as wf,
        nc.sbuf_tensor("pj_w", [128, 2 * nin_tiles * 128], BF16) as wb,
        nc.psum_tensor("pj_p1", [128, 512], F32) as p1,
        nc.psum_tensor("pj_p2", [128, 512], F32) as p2,
        nc.Block() as block,
    ):
        q0 = [ctx.val("dq0"), ctx.val("dq1")]
        g0 = ctx.val("gps")
        p0 = ctx.val("pe")
        e0 = ctx.val(evac_eng)

        @block.sync
        def _(sync):
            n = 0
            for of in range(out_nf):
                for it in range(nin_tiles):
                    if n >= 2:
                        sync.wait_ge(ctx.h("gps"), g0 + n - 1)
                    dma(ctx, sync, wf[:, (n % 2) * 128:(n % 2) * 128 + 128],
                        w_dram[it * 128:(it + 1) * 128,
                               (of0 + of) * 128:(of0 + of + 1) * 128],
                        sem=f"dq{n % 2}")
                    n += 1

        @block.scalar
        def _(scalar):
            n = 0
            for of in range(out_nf):
                for it in range(nin_tiles):
                    scalar.wait_ge(ctx.h(f"dq{n % 2}"), q0[n % 2] + 16 * (n // 2 + 1))
                    if of >= 2 and it == 0:
                        scalar.wait_ge(ctx.h("pe"), p0 + (of - 1) * nch)
                    half = (of % 2) * nin_tiles
                    scalar.activation(
                        wb[:, (half + it) * 128:(half + it + 1) * 128],
                        wf[:, (n % 2) * 128:(n % 2) * 128 + 128],
                        conv_func if conv_func is not None else AF.Copy).then_inc(
                        ctx.inc("gps"), 1)
                    n += 1

        @block.tensor
        def _(tensor):
            n = 0
            for of in range(out_nf):
                tensor.wait_ge(ctx.h("gps"), g0 + (of + 1) * nin_tiles)
                half = (of % 2) * nin_tiles
                for ch in range(nch):
                    ps = p1 if n % 2 == 0 else p2
                    if n >= 2:
                        tensor.wait_ge(ctx.h(evac_eng), e0 + n - 1)
                    for it in range(nin_tiles):
                        i = tensor.matmul(
                            ps[:], wb[:, (half + it) * 128:(half + it + 1) * 128],
                            xn_tiles[it][:, ch * 512:(ch + 1) * 512],
                            start=(it == 0), stop=(it == nin_tiles - 1))
                    i.then_inc(ctx.inc("pe"), 1)
                    n += 1

        def emit_evac(eng):
            n = 0
            for of in range(out_nf):
                for ch in range(nch):
                    ps = p1 if n % 2 == 0 else p2
                    eng.wait_ge(ctx.h("pe"), p0 + n + 1)
                    i = evac(of, ch, ps, eng)
                    i.then_inc(ctx.inc(evac_eng), 1)
                    n += 1

        if evac_eng == "vec":
            @block.vector
            def _(vector):
                emit_evac(vector)
        else:
            @block.scalar
            def _(scalar):
                emit_evac(scalar)

    barrier_block(ctx)



def make_weights_decl(nc):
    W = {}
    SP = {
        "L0_attn_a_Wq": ([D, 256], F32), "L0_attn_a_bq": ([256, 1], F32),
        "L0_attn_a_Wk": ([D, 256], F32), "L0_attn_a_bk": ([256, 1], F32),
        "L0_attn_a_Wg": ([D, 256], F32), "L0_attn_a_bg": ([256, 1], F32),
        "L0_attn_a_Wv": ([D, 256], F32), "L0_attn_a_bv": ([256, 1], F32),
        "L0_attn_a_Wo": ([256, D], F32), "L0_attn_a_bo": ([D, 1], F32),
        "const_ident": ([128, 128], BF16),
        "const_utmask": ([64, 64], BF16),
        "L0_attn_b_key_param": ([D, S_PAT], F32),
        "L0_attn_b_Wq": ([D, D], F32), "L0_attn_b_bq": ([D, 1], F32),
        "L0_attn_b_Wk": ([D, D], F32), "L0_attn_b_bk": ([D, 1], F32),
        "L0_attn_b_Wv": ([D, D], F32), "L0_attn_b_bv": ([D, 1], F32),
        "L0_attn_b_Wo": ([D, D], F32), "L0_attn_b_bo": ([D, 1], F32),
        "const_krow": ([128, 32], F32),
        "const_selhi": ([64, 128], BF16),
        "const_ef": ([16, NFT * 128], BF16),
        "L1_attn_a_Wq": ([D, D], F32), "L1_attn_a_Wk": ([D, D], F32),
        "L1_attn_a_Wv": ([D, D], F32), "L1_attn_a_Wo": ([D, D], F32),
        "L1_attn_a_bo": ([D, 1], F32),
        "L1_ffn_w1": ([4 * D, D], F32), "L1_ffn_w1T": ([D, 4 * D], F32),
        "L1_ffn_b1": ([4 * D, 1], F32),
        "L1_ffn_w2": ([D, 4 * D], F32), "L1_ffn_w2T": ([4 * D, D], F32),
        "L1_ffn_b2": ([D, 1], F32),
        "const_rep32": ([32, 128], BF16),
        "const_tau": ([128, T], F32),
        "L1_attn_b_time_decay": ([D, 1], F32),
        "L1_attn_b_time_first": ([D, 1], F32),
        "L1_attn_b_W_tm": ([D, 3 * D], F32),
        "L1_attn_b_g1": ([D, 1], F32), "L1_attn_b_b1": ([D, 1], F32),
        "L1_attn_b_tmk": ([D, 1], F32), "L1_attn_b_tmv": ([D, 1], F32),
        "L1_attn_b_tmr": ([D, 1], F32),
        "L1_attn_b_W_cm": ([D, 4 * D], F32),
        "L1_attn_b_W_cp": ([4 * D, D], F32),
        "L1_attn_b_g2": ([D, 1], F32), "L1_attn_b_b2": ([D, 1], F32),
        "L1_attn_b_cmk": ([D, 1], F32), "L1_attn_b_cmr": ([D, 1], F32),
        "L0_ffn_kernel": ([D, D], BF16),
        "L0_ffn_input_scale": ([1, 1], F32),
        "L0_ffn_kernel_scale": ([1, 1], F32),
        "L0_ffn_bias": ([D, 1], BF16),
    }
    for n, (sh, dt) in SP.items():
        W[n] = nc.dram_tensor(n, sh, dt, kind="ExternalInput").ap()
    return W


def build_gla(ctx, xres, W):
    nc = ctx.nc
    KD = 256
    NKT = 2

    with contextlib.ExitStack() as es:
        es_xn = contextlib.ExitStack()
        xn_all = es_xn.enter_context(
            nc.sbuf_tensor("g_xn", [128, NFT * NTOK], BF16, side="right"))
        xn = [xn_all[:, i * NTOK:(i + 1) * NTOK] for i in range(NFT)]
        build_layernorm(ctx, xres, xn)
        debug_dump(ctx, "xn0", xn[0])
        q_t = es.enter_context(nc.sbuf_tensor("g_q", [128, NKT * NTOK], BF16))
        k_t = es.enter_context(nc.sbuf_tensor("g_k", [128, NKT * NTOK], BF16))
        v_t = ctx.scr_xr[:, 0:NKT * NTOK]
        lg_t = ctx.scr_f32b[:, 0:NTOK]
        gbias = es.enter_context(nc.sbuf_tensor("g_bias", [128, 4 * NKT], F32))

        with nc.Block() as block:
            @block.sync
            def _(sync):
                for wi, bn in enumerate(["L0_attn_a_bq", "L0_attn_a_bk",
                                         "L0_attn_a_bg", "L0_attn_a_bv"]):
                    for kt in range(NKT):
                        dma(ctx, sync, gbias[:, wi * NKT + kt:wi * NKT + kt + 1],
                            W[bn][kt * 128:(kt + 1) * 128, :])
                wait_all(ctx, sync, "dma")


        def evac_q(of, ch, ps, eng):
            sl = (slice(None), slice(of * NTOK + ch * 512, of * NTOK + (ch + 1) * 512))
            return eng.tensor_scalar_add(q_t[sl], ps[:],
                                         gbias[:, 0 * NKT + of:0 * NKT + of + 1])
        build_proj(ctx, xn, W["L0_attn_a_Wq"], NKT, evac_q, evac_eng="vec")

        def evac_k(of, ch, ps, eng):
            sl = (slice(None), slice(of * NTOK + ch * 512, of * NTOK + (ch + 1) * 512))
            return eng.activation(k_t[sl], ps[:], AF.Sigmoid,
                                  bias=gbias[:, 1 * NKT + of:1 * NKT + of + 1])
        build_proj(ctx, xn, W["L0_attn_a_Wk"], NKT, evac_k, evac_eng="scl")

        def evac_v(of, ch, ps, eng):
            sl = (slice(None), slice(of * NTOK + ch * 512, of * NTOK + (ch + 1) * 512))
            return eng.tensor_scalar_add(v_t[sl], ps[:],
                                         gbias[:, 3 * NKT + of:3 * NKT + of + 1])
        build_proj(ctx, xn, W["L0_attn_a_Wv"], NKT, evac_v, evac_eng="vec")
        debug_dump(ctx, "v_early", v_t)
        debug_dump(ctx, "xn3", xn[3])
        debug_dump(ctx, "xn7", xn[7])

        debug_dump(ctx, "q", q_t[:])
        debug_dump(ctx, "k", k_t[:])
        debug_dump(ctx, "v", v_t)

        qt_t = q_t
        kt_t = k_t
        kh_t = ctx.scr_bf[:, 0:NKT * NTOK]
        cs_sl = ctx.scr_f32[:, 0:NTOK]
        e_sl = ctx.scr_f32[:, NTOK:2 * NTOK]
        z64 = es.enter_context(nc.sbuf_tensor("g_z64", [128, CHUNK], F32))
        ec_t = es.enter_context(nc.sbuf_tensor("g_ec", [128, NKT * B * NCH], F32))
        with nc.Block() as block:
            @block.vector
            def _(vector):
                vector.memset(z64[:], 0.0).then_inc(ctx.inc("vec"), 1)
        barrier_block(ctx)
        for kt in range(NKT):
            def evac_g(of, ch, ps, eng, kt=kt):
                sl = (slice(None), slice(ch * 512, (ch + 1) * 512))
                eng.activation(lg_t[sl], ps[:], AF.Sigmoid,
                               bias=gbias[:, 2 * NKT + kt:2 * NKT + kt + 1])
                return eng.activation(lg_t[sl], lg_t[sl], AF.Ln)
            build_proj(ctx, xn, W["L0_attn_a_Wg"], 1, evac_g, of0=kt,
                       evac_eng="scl")
            with nc.Block() as block:
                @block.vector
                def _(vector, kt=kt):
                    for bc in range(B * NCH):
                        sl = (slice(None), slice(bc * CHUNK, (bc + 1) * CHUNK))
                        vector.tensor_tensor_scan(cs_sl[sl], lg_t[sl], z64[:],
                                                  0.0, ALU.add, ALU.add)
                    vector.tensor_scalar_max(cs_sl, cs_sl, -85.0).then_inc(
                        ctx.inc("vec"), 1)

                @block.scalar
                def _(scalar, kt=kt):
                    scalar.wait_ge(ctx.h("vec"), ctx.val("vec"))
                    scalar.activation(e_sl, cs_sl, AF.Exp).then_inc(ctx.inc("scl"), 1)

                @block.vector
                def _(vector, kt=kt):
                    vector.wait_ge(ctx.h("scl"), ctx.val("scl"))
                    sl = (slice(None), slice(kt * NTOK, (kt + 1) * NTOK))
                    vector.tensor_mul(qt_t[sl], q_t[sl], e_sl)
                    cs3 = cs_sl.rearrange("p (o c) -> p o c", c=CHUNK)
                    last = cs_sl[:, CHUNK - 1:NTOK:CHUNK]
                    lastb = bass.AP(last.tensor, last.offset,
                                    [list(pp) for pp in last.ap] + [[0, CHUNK]])
                    e3 = e_sl.rearrange("p (o c) -> p o c", c=CHUNK)
                    vector.tensor_sub(e3[:], lastb, cs3[:]).then_inc(
                        ctx.inc("vec"), 1)

                @block.scalar
                def _(scalar, kt=kt):
                    scalar.wait_ge(ctx.h("vec"), ctx.val("vec"))
                    last = cs_sl[:, CHUNK - 1:NTOK:CHUNK]
                    scalar.activation(ec_t[:, kt * B * NCH:(kt + 1) * B * NCH],
                                      last, AF.Exp)
                    scalar.activation(e_sl, e_sl, AF.Exp).then_inc(
                        ctx.inc("scl"), 1)

                @block.vector
                def _(vector, kt=kt):
                    vector.wait_ge(ctx.h("scl"), ctx.val("scl"))
                    sl = (slice(None), slice(kt * NTOK, (kt + 1) * NTOK))
                    vector.tensor_mul(kh_t[sl], k_t[sl], e_sl).then_inc(
                        ctx.inc("vec"), 1)

                @block.scalar
                def _(scalar, kt=kt):
                    scalar.wait_ge(ctx.h("vec"), ctx.val("vec"))
                    scalar.activation(e_sl, cs_sl, AF.Exp,
                                      scale=-1.0).then_inc(ctx.inc("scl"), 1)

                @block.vector
                def _(vector, kt=kt):
                    vector.wait_ge(ctx.h("scl"), ctx.val("scl"))
                    sl = (slice(None), slice(kt * NTOK, (kt + 1) * NTOK))
                    vector.tensor_mul(kt_t[sl], k_t[sl], e_sl).then_inc(
                        ctx.inc("vec"), 1)
            barrier_block(ctx)
        debug_dump(ctx, "lg", lg_t)
        debug_dump(ctx, "cs", cs_sl)
        debug_dump(ctx, "esl", e_sl)
        debug_dump(ctx, "qt", qt_t[:])
        debug_dump(ctx, "kt", kt_t[:])
        debug_dump(ctx, "kh", kh_t)
        debug_dump(ctx, "ec", ec_t[:])
        es_xn.close()
        if True:
            v_tm = es.enter_context(nc.sbuf_tensor("g_vtm", [64, (NTOK // CHUNK) * KD], BF16))
            kh_tm = es.enter_context(nc.sbuf_tensor("g_khtm", [64, (NTOK // CHUNK) * KD], BF16))
            state = es.enter_context(nc.sbuf_tensor("g_state", [128, NKT * B * KD], F32))
            state_bf = es.enter_context(nc.sbuf_tensor("g_statebf", [128, NKT * B * KD], BF16))
            a_sm = es.enter_context(nc.sbuf_tensor("g_asm", [64, CHUNK], BF16))
            gout = ctx.scr_bf[:, 0:NKT * NTOK]
            if True:
                with (
                    nc.psum_tensor("g_pt1", [64, 128], BF16) as pt1,
                    nc.psum_tensor("g_pt2", [64, 128], BF16) as pt2,
                    nc.Block() as block,
                ):
                    p0 = ctx.val("pe")
                    v0 = ctx.val("vec")
                    njob = 2 * NKT * (NTOK // CHUNK)

                    @block.tensor
                    def _(tensor):
                        n = 0
                        for src in (v_t, kh_t):
                            for kt in range(NKT):
                                for c in range(NTOK // CHUNK):
                                    ps = pt1 if n % 2 == 0 else pt2
                                    if n >= 2:
                                        tensor.wait_ge(ctx.h("vec"), v0 + n - 1)
                                    tensor.transpose(
                                        ps[:, :],
                                        src[:, kt * NTOK + c * CHUNK:kt * NTOK + (c + 1) * CHUNK],
                                        ctx.ident[:]).then_inc(ctx.inc("pe"), 1)
                                    n += 1

                    @block.vector
                    def _(vector):
                        n = 0
                        for dst in (v_tm, kh_tm):
                            for kt in range(NKT):
                                for c in range(NTOK // CHUNK):
                                    ps = pt1 if n % 2 == 0 else pt2
                                    vector.wait_ge(ctx.h("pe"), p0 + n + 1)
                                    vector.tensor_copy(
                                        dst[:, c * KD + kt * 128:c * KD + (kt + 1) * 128],
                                        ps[0:64, :]).then_inc(ctx.inc("vec"), 1)
                                    n += 1
                barrier_block(ctx)

                with (
                    nc.psum_tensor("g_psc", [64, CHUNK], F32) as ps_sc,
                    nc.psum_tensor("g_po1", [128, CHUNK], F32) as ps_o1,
                    nc.psum_tensor("g_po2", [128, CHUNK], F32) as ps_o2,
                    nc.psum_tensor("g_ps1", [128, KD], F32) as ps_s1,
                    nc.psum_tensor("g_ps2", [128, KD], F32) as ps_s2,
                    nc.Block() as block,
                ):
                    p0 = ctx.val("pe")
                    v0 = ctx.val("vec")

                    @block.vector
                    def _(vector):
                        vector.memset(state[:], 0.0)
                        vector.memset(state_bf[:], 0.0).then_inc(ctx.inc("gps"), 1)

                    @block.tensor
                    def _(tensor):
                        tensor.wait_ge(ctx.h("gps"), ctx.val("gps"))
                        n = 0
                        for b in range(B):
                            for c in range(NCH):
                                t0 = b * T + c * CHUNK
                                cc = t0 // CHUNK
                                for kt in range(NKT):
                                    i = tensor.matmul(
                                        ps_sc[:],
                                        kt_t[:, kt * NTOK + t0:kt * NTOK + t0 + CHUNK],
                                        qt_t[:, kt * NTOK + t0:kt * NTOK + t0 + CHUNK],
                                        start=(kt == 0), stop=(kt == NKT - 1))
                                i.then_inc(ctx.inc("pe"), 1)
                                for vt in range(NKT):
                                    ps_o = ps_o1 if vt == 0 else ps_o2
                                    for kt in range(NKT):
                                        tensor.matmul(
                                            ps_o[:],
                                            state_bf[:, (kt * B + b) * KD + vt * 128:
                                                     (kt * B + b) * KD + (vt + 1) * 128],
                                            qt_t[:, kt * NTOK + t0:kt * NTOK + t0 + CHUNK],
                                            start=(kt == 0), stop=False)
                                tensor.wait_ge(ctx.h("vec"), v0 + 4 * n + 1)
                                for vt in range(NKT):
                                    ps_o = ps_o1 if vt == 0 else ps_o2
                                    i = tensor.matmul(
                                        ps_o[:],
                                        v_tm[:, cc * KD + vt * 128:cc * KD + (vt + 1) * 128],
                                        a_sm[:],
                                        start=False, stop=True)
                                i.then_inc(ctx.inc("pe"), 1)
                                for kt in range(NKT):
                                    ps_s = ps_s1 if kt == 0 else ps_s2
                                    i = tensor.matmul(
                                        ps_s[:],
                                        kh_tm[:, cc * KD + kt * 128:cc * KD + (kt + 1) * 128],
                                        v_tm[:, cc * KD:(cc + 1) * KD],
                                        start=True, stop=True)
                                i.then_inc(ctx.inc("pe"), 1)
                                tensor.wait_ge(ctx.h("vec"), v0 + 4 * n + 4)
                                n += 1

                    @block.vector
                    def _(vector):
                        n = 0
                        for b in range(B):
                            for c in range(NCH):
                                t0 = b * T + c * CHUNK
                                vector.wait_ge(ctx.h("pe"), p0 + 3 * n + 1)
                                vector.tensor_mul(a_sm[:], ps_sc[:],
                                                  ctx.utmask[:]).then_inc(
                                    ctx.inc("vec"), 1)
                                vector.wait_ge(ctx.h("pe"), p0 + 3 * n + 2)
                                vector.tensor_copy(gout[:, 0 * NTOK + t0:0 * NTOK + t0 + CHUNK],
                                                   ps_o1[:])
                                vector.tensor_copy(gout[:, 1 * NTOK + t0:1 * NTOK + t0 + CHUNK],
                                                   ps_o2[:]).then_inc(ctx.inc("vec"), 1)
                                vector.wait_ge(ctx.h("pe"), p0 + 3 * n + 3)
                                for kt in range(NKT):
                                    ps_s = ps_s1 if kt == 0 else ps_s2
                                    idx = kt * B + b
                                    eccol = ec_t[:, kt * B * NCH + b * NCH + c:
                                                  kt * B * NCH + b * NCH + c + 1]
                                    vector.scalar_tensor_tensor(
                                        state[:, idx * KD:(idx + 1) * KD],
                                        state[:, idx * KD:(idx + 1) * KD],
                                        eccol, ps_s[:], ALU.mult, ALU.add)
                                vector.engine_nop().then_inc(ctx.inc("vec"), 1)
                                for kt in range(NKT):
                                    idx = kt * B + b
                                    vector.tensor_copy(
                                        state_bf[:, idx * KD:(idx + 1) * KD],
                                        state[:, idx * KD:(idx + 1) * KD])
                                vector.engine_nop().then_inc(ctx.inc("vec"), 1)
                                n += 1
                barrier_block(ctx)

                debug_dump(ctx, "gout", gout)
                debug_dump(ctx, "state", state[:])
                def evac_wo(of, ch, ps, vector):
                    bo = ctx.gla_bo[:, of:of + 1]
                    vector.scalar_tensor_tensor(
                        ctx.scr_f32b[:, ch * 512:(ch + 1) * 512], ps[:], bo,
                        ctx.scr_f32[:, ch * 512:(ch + 1) * 512],
                        ALU.add, ALU.add)

                build_proj_residual(ctx, gout, NKT, W["L0_attn_a_Wo"],
                                    W["L0_attn_a_bo"], xres)

    barrier_block(ctx)


def build_proj_residual(ctx, src_fm, n_in_tiles, w_dram, b_dram, xres,
                        extra_add_dram=None):
    nc = ctx.nc
    with (
        nc.sbuf_tensor("pr_wf", [128, n_in_tiles * 128], F32) as wf,
        nc.sbuf_tensor("pr_w", [128, n_in_tiles * 128], BF16) as wb,
        nc.sbuf_tensor("pr_b", [128, NFT], F32) as bcols,
        nc.psum_tensor("pr_p1", [128, 512], F32) as p1,
        nc.psum_tensor("pr_p2", [128, 512], F32) as p2,
    ):
        xf = ctx.scr_f32[:, 0:NTOK]
        of_out = ctx.scr_f32[:, NTOK:2 * NTOK]
        with nc.Block() as block:
            @block.sync
            def _(sync):
                for ft in range(NFT):
                    dma(ctx, sync, bcols[:, ft:ft + 1],
                        b_dram[ft * 128:(ft + 1) * 128, :])
                wait_all(ctx, sync, "dma")
        for of in range(NFT):
            with nc.Block() as block:
                d0 = ctx.val("dma")
                g0 = ctx.val("gps")
                p0 = ctx.val("pe")
                v0 = ctx.val("vec")
                nch = NTOK // 512

                @block.sync
                def _(sync):
                    dma(ctx, sync, xf[:], xres.ap()[of])
                    for it in range(n_in_tiles):
                        dma(ctx, sync, wf[:, it * 128:(it + 1) * 128],
                            w_dram[it * 128:(it + 1) * 128, of * 128:(of + 1) * 128])

                @block.scalar
                def _(scalar):
                    scalar.wait_ge(ctx.h("dma"), d0 + 16 * (1 + n_in_tiles))
                    scalar.activation(wb[:], wf[:], AF.Copy).then_inc(
                        ctx.inc("gps"), 1)

                @block.tensor
                def _(tensor):
                    tensor.wait_ge(ctx.h("gps"), g0 + 1)
                    for ch in range(nch):
                        ps = p1 if ch % 2 == 0 else p2
                        if ch >= 2:
                            tensor.wait_ge(ctx.h("vec"), v0 + ch - 1)
                        for it in range(n_in_tiles):
                            i = tensor.matmul(
                                ps[:], wb[:, it * 128:(it + 1) * 128],
                                src_fm[:, it * NTOK + ch * 512:it * NTOK + (ch + 1) * 512],
                                start=(it == 0), stop=(it == n_in_tiles - 1))
                        i.then_inc(ctx.inc("pe"), 1)

                v1 = ctx.val("vec")

                @block.vector
                def _(vector):
                    for ch in range(nch):
                        ps = p1 if ch % 2 == 0 else p2
                        vector.wait_ge(ctx.h("pe"), p0 + ch + 1)
                        vector.scalar_tensor_tensor(
                            of_out[:, ch * 512:(ch + 1) * 512], ps[:],
                            bcols[:, of:of + 1], xf[:, ch * 512:(ch + 1) * 512],
                            ALU.add, ALU.add).then_inc(ctx.inc("vec"), 1)

                @block.scalar
                def _(scalar):
                    scalar.wait_ge(ctx.h("vec"), v1 + nch)
                    dma(ctx, scalar, xres.ap()[of], of_out[:], sem="dmb")
                    scalar.wait_ge(ctx.h("dmb"), ctx.val("dmb"))
    barrier_block(ctx)



def build_ffn0_prep(ctx, W):
    nc = ctx.nc
    ctx.ffn_kq8 = ctx._stack_append(nc.sbuf_tensor("f0_kq8", [128, NFT * D], FP8))
    ctx.ffn_bias = ctx._stack_append(nc.sbuf_tensor("f0_bias", [128, NFT], F32))
    ctx.ffn_scol = ctx._stack_append(nc.sbuf_tensor("f0_scol", [128, 5], F32))
    with contextlib.ExitStack() as es:
        kbf = es.enter_context(nc.sbuf_tensor("f0_kbf", [128, D], BF16))
        srow = es.enter_context(nc.sbuf_tensor("f0_srow", [1, 2], F32))
        with (
            nc.psum_tensor("f0_ps", [128, 2], F32) as ps,
            nc.Block() as block,
        ):
            d0 = ctx.val("dma")

            @block.sync
            def _(sync):
                dma(ctx, sync, srow[:, 0:1], W["L0_ffn_input_scale"][:])
                dma(ctx, sync, srow[:, 1:2], W["L0_ffn_kernel_scale"][:])
                for ft in range(NFT):
                    dma(ctx, sync, kbf[:, ft:ft + 1],
                        W["L0_ffn_bias"][ft * 128:(ft + 1) * 128, :])

            @block.tensor
            def _(tensor):
                tensor.wait_ge(ctx.h("dma"), ctx.val("dma"))
                tensor.matmul(ps[:], ctx.ones_row[:], srow[:],
                              start=True, stop=True).then_inc(ctx.inc("pe"), 1)

            @block.vector
            def _(vector):
                vector.wait_ge(ctx.h("pe"), ctx.val("pe"))
                wait_all(ctx, vector, "dma")
                vector.tensor_copy(ctx.ffn_bias[:], kbf[:, 0:NFT])
                vector.tensor_copy(ctx.ffn_scol[:, 0:2], ps[:])
                vector.drain()
                vector.tensor_mul(ctx.ffn_scol[:, 2:3], ctx.ffn_scol[:, 0:1],
                                  ctx.ffn_scol[:, 1:2])
                vector.drain()
                vector.reciprocal(ctx.ffn_scol[:, 3:5],
                                  ctx.ffn_scol[:, 0:2]).then_inc(ctx.inc("vec"), 1)

        for it in range(NFT):
            with nc.Block() as block:
                d0 = ctx.val("dma")

                @block.sync
                def _(sync, it=it):
                    dma(ctx, sync, kbf[:], W["L0_ffn_kernel"][it * 128:(it + 1) * 128, :])

                @block.vector
                def _(vector, it=it, d0=d0):
                    vector.wait_ge(ctx.h("dma"), d0 + 16)
                    vector.tensor_scalar_mul(ctx.ffn_kq8[:, it * D:(it + 1) * D],
                                              kbf[:],
                                              ctx.ffn_scol[:, 4:5]).then_inc(
                        ctx.inc("vec"), 1)
    barrier_block(ctx)


def build_ffn0(ctx, xres, W, parts=2):
    nc = ctx.nc
    with contextlib.ExitStack() as es:
        es_r = contextlib.ExitStack()
        xn_all = es_r.enter_context(
            nc.sbuf_tensor("f0_xn", [128, NFT * NTOK], BF16, side="right"))
        xn = [xn_all[:, i * NTOK:(i + 1) * NTOK] for i in range(NFT)]
        build_layernorm(ctx, xres, xn)
        xq8 = es_r.enter_context(
            nc.sbuf_tensor("f0_xq8", [128, NFT * NTOK], FP8, side="right"))

        with nc.Block() as block:
            @block.vector
            def _(vector):
                for it in range(NFT):
                    vector.tensor_scalar_mul(
                        xq8[:, it * NTOK:(it + 1) * NTOK], xn[it],
                        ctx.ffn_scol[:, 3:4]).then_inc(ctx.inc("vec"), 1)
        barrier_block(ctx)
        debug_dump(ctx, "f0xq8", xq8[:, 0:NTOK])
        debug_dump(ctx, "f0kq8", ctx.ffn_kq8[:])
        debug_dump(ctx, "f0scol", ctx.ffn_scol[:])
        debug_dump(ctx, "f0bias", ctx.ffn_bias[:])
        debug_dump(ctx, "f0xn0", xn[0])
        if parts < 2:
            es_r.close()
            return

        with contextlib.ExitStack() as es2:
            ybf = ctx.scr_bf[:, 0:NTOK]
            p1 = es2.enter_context(nc.psum_tensor("f0_p1", [128, 512], F32))
            p2 = es2.enter_context(nc.psum_tensor("f0_p2", [128, 512], F32))
            xf = ctx.scr_f32[:, 0:NTOK]
            of_out = ctx.scr_f32[:, NTOK:2 * NTOK]
            for of in range(NFT):
                with nc.Block() as block:
                    d0 = ctx.val("dma")
                    p0 = ctx.val("pe")
                    v0 = ctx.val("vec")
                    nch = NTOK // 512

                    @block.sync
                    def _(sync, of=of):
                        dma(ctx, sync, xf, xres.ap()[of])

                    @block.tensor
                    def _(tensor, of=of, v0=v0):
                        for ch in range(nch):
                            ps = p1 if ch % 2 == 0 else p2
                            if ch >= 2:
                                tensor.wait_ge(ctx.h("vec"), v0 + 2 * (ch - 1))
                            for it in range(NFT):
                                i = tensor.matmul(
                                    ps[:],
                                    ctx.ffn_kq8[:, it * D + of * 128:it * D + (of + 1) * 128],
                                    xq8[:, it * NTOK + ch * 512:it * NTOK + (ch + 1) * 512],
                                    start=(it == 0), stop=(it == NFT - 1))
                            i.then_inc(ctx.inc("pe"), 1)

                    @block.vector
                    def _(vector, of=of, d0=d0, p0=p0):
                        vector.wait_ge(ctx.h("dma"), d0 + 16)
                        for ch in range(nch):
                            ps = p1 if ch % 2 == 0 else p2
                            vector.wait_ge(ctx.h("pe"), p0 + ch + 1)
                            c = slice(ch * 512, (ch + 1) * 512)
                            vector.tensor_scalar(ybf[:, c], ps[:],
                                                 ctx.ffn_scol[:, 0:1],
                                                 ctx.ffn_scol[:, 1:2],
                                                 ALU.mult, ALU.mult)
                            vector.scalar_tensor_tensor(
                                of_out[:, c], ybf[:, c],
                                ctx.ffn_bias[:, of:of + 1], xf[:, c],
                                ALU.add, ALU.add).then_inc(ctx.inc("vec", 2), 2)

                    v1 = ctx.val("vec")

                    @block.scalar
                    def _(scalar, of=of, v1=v1):
                        scalar.wait_ge(ctx.h("vec"), v1)
                        dma(ctx, scalar, xres.ap()[of], of_out, sem="dmb")
                        scalar.wait_ge(ctx.h("dmb"), ctx.val("dmb"))
        es_r.close()
    barrier_block(ctx)


def apof(t):
    return t if isinstance(t, bass.AP) else t[:]


def _bitonic_rounds(n=32):
    out = []
    k = 2
    while k <= n:
        j = k // 2
        while j >= 1:
            out.append((k, j, n // (2 * j), max(k // (2 * j), 1)))
            j //= 2
        k *= 2
    return out


def _emit_sparsemax(ctx, vector, z, zb1, zb2, zb3):
    nseg = z.ap[-1][1] // 32
    cur, nxt = z, zb1
    for (k, j, nb, m) in _bitonic_rounds(32):
        for par in (0, 1):
            if par == 1 and nb < 2 * m:
                continue
            base = par * m * 2 * j
            if nb >= 2 * m:
                dims = [[4 * m * j, nseg * (nb // (2 * m))], [2 * j, m], [1, j]]
            else:
                dims = [[2 * j, nseg * nb], [1, j]]

            def mk(t, extra, dims=dims, base=base):
                return bass.AP(t.tensor, t.offset + base + extra,
                               [list(t.ap[0])] + [list(dd) for dd in dims])
            lo_i, hi_i = mk(cur, 0), mk(cur, j)
            lo_o, hi_o = mk(nxt, 0), mk(nxt, j)
            if par == 0:
                vector.tensor_max(lo_o, lo_i, hi_i)
                vector.tensor_tensor(hi_o, lo_i, hi_i, ALU.min)
            else:
                vector.tensor_tensor(lo_o, lo_i, hi_i, ALU.min)
                vector.tensor_max(hi_o, lo_i, hi_i)
        vector.drain()
        if cur is z:
            cur, nxt = zb1, zb3
        else:
            cur, nxt = nxt, cur
    zs = cur
    if zs is not zb1:
        vector.tensor_copy(zb1[:], zs[:])
        zs = zb1
    buf_a, buf_b = zs, zb3
    d = 1
    while d < 32:
        a3 = buf_a.rearrange("p (s c) -> p s c", c=32)
        b3 = buf_b.rearrange("p (s c) -> p s c", c=32)
        vector.tensor_add(b3[:, :, d:32], a3[:, :, d:32], a3[:, :, 0:32 - d])
        vector.tensor_copy(b3[:, :, 0:d], a3[:, :, 0:d])
        vector.drain()
        buf_a = buf_b
        buf_b = zb2 if buf_a is zb3 else zb3
        d *= 2
    cum = buf_a
    if cum is not zb2:
        vector.tensor_copy(zb2[:], cum[:])
        cum = zb2
    return zs, cum


def build_attnb(ctx, xres, W, xn_dram, scr2, parts=4):
    nc = ctx.nc
    H, E, S = 16, 64, 32
    with contextlib.ExitStack() as es:
        es_a = contextlib.ExitStack()
        xn_buf = es_a.enter_context(
            nc.sbuf_tensor("ab_xn", [128, NTOK], BF16, side="right"))
        build_layernorm(ctx, xres, [xn_buf] * NFT, xn_dram=xn_dram)
        es_a.close()

        wq_sb = es.enter_context(nc.sbuf_tensor("ab_wq", [128, NFT * D], BF16))
        keys_fm = es.enter_context(nc.sbuf_tensor("ab_kfm", [128, NFT * S], BF16))
        keys_pad = es.enter_context(nc.sbuf_tensor("ab_kpad", [128, 16 * S], BF16))
        v2 = es.enter_context(nc.sbuf_tensor("ab_v2", [128, NFT * 128], BF16))
        bcols = es.enter_context(nc.sbuf_tensor("ab_b", [128, 2 * NFT], F32))
        krow = es.enter_context(nc.sbuf_tensor("ab_krow", [128, S], F32))
        with contextlib.ExitStack() as es1:
            kp_fm = es1.enter_context(nc.sbuf_tensor("ab_kp", [128, NFT * S], BF16))
            kp_f32 = es1.enter_context(nc.sbuf_tensor("ab_kpf", [128, S], F32))
            brow = es1.enter_context(nc.sbuf_tensor("ab_br", [1, 2 * D], BF16))
            brf = es1.enter_context(nc.sbuf_tensor("ab_brf", [1, 2 * D], F32))
            ones32 = es1.enter_context(nc.sbuf_tensor("ab_o32", [1, 32], BF16))
            keys_sm = es1.enter_context(nc.sbuf_tensor("ab_ksm", [32, D], BF16))
            vals_rep = es1.enter_context(nc.sbuf_tensor("ab_vrep", [128, D], BF16))

            with nc.Block() as block:
                d0 = ctx.val("dma")
                dq00 = ctx.val("dq0")
                v0b1 = ctx.val("vec")

                @block.sync
                def _(sync):
                    for it in range(NFT):
                        if it >= 1:
                            sync.wait_ge(ctx.h("vec"), v0b1 + it)
                        dma(ctx, sync, kp_f32[:],
                            W["L0_attn_b_key_param"][it * 128:(it + 1) * 128, :],
                            sem="dq0")
                    dma(ctx, sync, brf[:, 0:D],
                        bass.AP(W["L0_attn_b_bk"].tensor, 0, [[2 * D, 1], [1, D]]))
                    dma(ctx, sync, brf[:, D:2 * D],
                        bass.AP(W["L0_attn_b_bv"].tensor, 0, [[2 * D, 1], [1, D]]))
                    for ft in range(NFT):
                        dma(ctx, sync, bcols[:, ft:ft + 1],
                            W["L0_attn_b_bq"][ft * 128:(ft + 1) * 128, :])
                        dma(ctx, sync, bcols[:, NFT + ft:NFT + ft + 1],
                            W["L0_attn_b_bo"][ft * 128:(ft + 1) * 128, :])
                    dma(ctx, sync, krow[:], W["const_krow"][:])

                @block.vector
                def _(vector):
                    for it in range(NFT):
                        vector.wait_ge(ctx.h("dq0"), dq00 + 16 * (it + 1))
                        vector.tensor_copy(kp_fm[:, it * S:(it + 1) * S],
                                           kp_f32[:]).then_inc(ctx.inc("vec"), 1)
                    vector.wait_ge(ctx.h("dma"), ctx.val("dma"))
                    vector.tensor_copy(brow[:], brf[:])
                    vector.memset(ones32[:], 1.0)
                    vector.memset(v2[:], 0.0).then_inc(ctx.inc("vec"), 1)
            barrier_block(ctx)

            for step in range(2):
                wname = "L0_attn_b_Wk" if step == 0 else "L0_attn_b_Wv"
                boff = 0 if step == 0 else D
                with nc.Block() as block:
                    q0 = [ctx.val("dq0"), ctx.val("dq1")]
                    s0 = ctx.val("scl")

                    @block.sync
                    def _(sync, wname=wname):
                        for it in range(NFT):
                            if it >= 2:
                                sync.wait_ge(ctx.h("scl"), s0 + it - 1)
                            dma(ctx, sync, ctx.scr_f32[:, (it % 2) * D:(it % 2) * D + D],
                                W[wname][it * 128:(it + 1) * 128, :],
                                sem=f"dq{it % 2}")

                    @block.scalar
                    def _(scalar):
                        for it in range(NFT):
                            scalar.wait_ge(ctx.h(f"dq{it % 2}"),
                                           q0[it % 2] + 16 * (it // 2 + 1))
                            scalar.activation(ctx.scr_bf[:, it * D:(it + 1) * D],
                                              ctx.scr_f32[:, (it % 2) * D:(it % 2) * D + D],
                                              AF.Copy).then_inc(ctx.inc("scl"), 1)
                with (
                    nc.psum_tensor("ab_ps", [32, 512], F32) as ps,
                    nc.Block() as block,
                ):
                    p0 = ctx.val("pe")
                    v0 = ctx.val("vec")

                    @block.tensor
                    def _(tensor, step=step, boff=boff):
                        for hc in range(2):
                            if hc == 1:
                                tensor.wait_ge(ctx.h("vec"), v0 + 1)
                            for it in range(NFT):
                                lhs = (kp_fm if step == 0 else keys_fm)
                                tensor.matmul(ps[:], lhs[:, it * S:(it + 1) * S],
                                              ctx.scr_bf[:, it * D + hc * 512:it * D + hc * 512 + 512],
                                              start=(it == 0), stop=False)
                            i = tensor.matmul(ps[:], ones32[:],
                                              brow[:, boff + hc * 512:boff + hc * 512 + 512],
                                              start=False, stop=True)
                            i.then_inc(ctx.inc("pe"), 1)

                    @block.vector
                    def _(vector, step=step):
                        for hc in range(2):
                            vector.wait_ge(ctx.h("pe"), p0 + hc + 1)
                            dst = keys_sm if step == 0 else vals_rep[0:32, :]
                            vector.tensor_copy(dst[0:32, hc * 512:hc * 512 + 512],
                                               ps[:]).then_inc(ctx.inc("vec"), 1)
                barrier_block(ctx)
                if step == 0:
                    with (
                        nc.psum_tensor("ab_pt", [128, 32], BF16) as pt,
                        nc.Block() as block,
                    ):
                        p0 = ctx.val("pe")
                        v0 = ctx.val("vec")

                        @block.tensor
                        def _(tensor):
                            for it in range(NFT):
                                if it >= 1:
                                    tensor.wait_ge(ctx.h("vec"), v0 + it)
                                tensor.transpose(pt[:], keys_sm[0:32, it * 128:(it + 1) * 128],
                                                 ctx.ident[0:32, 0:32]).then_inc(ctx.inc("pe"), 1)

                        @block.vector
                        def _(vector):
                            for it in range(NFT):
                                vector.wait_ge(ctx.h("pe"), p0 + it + 1)
                                vector.tensor_copy(keys_fm[:, it * S:(it + 1) * S],
                                                   pt[:]).then_inc(ctx.inc("vec"), 1)
                    barrier_block(ctx)

            with (
                nc.psum_tensor("ab_pr", [128, 512], F32) as pr,
                nc.Block() as block,
            ):
                p0 = ctx.val("pe")
                v0 = ctx.val("vec")

                @block.tensor
                def _(tensor):
                    for hc in range(2):
                        if hc == 1:
                            tensor.wait_ge(ctx.h("vec"), v0 + 1)
                        tensor.matmul(pr[:], ctx.rep32[:],
                                      vals_rep[0:32, hc * 512:hc * 512 + 512],
                                      start=True, stop=True).then_inc(ctx.inc("pe"), 1)

                @block.vector
                def _(vector):
                    for hc in range(2):
                        vector.wait_ge(ctx.h("pe"), p0 + hc + 1)
                        vector.tensor_copy(vals_rep[:, hc * 512:hc * 512 + 512],
                                           pr[:]).then_inc(ctx.inc("vec"), 1)
            barrier_block(ctx)

            with nc.Block() as block:
                @block.vector
                def _(vector):
                    vector.memset(keys_pad[:], 0.0)
                    for h in range(H):
                        r0 = (h % 2) * 64
                        vector.tensor_copy(
                            keys_pad[r0:r0 + 64, h * S:(h + 1) * S],
                            keys_fm[r0:r0 + 64, (h // 2) * S:(h // 2) * S + S])
                    for f in range(NFT):
                        b0 = (f % 2) * 64
                        vector.tensor_copy(
                            v2[b0:b0 + 32, f * 128:f * 128 + 64],
                            vals_rep[b0:b0 + 32, (2 * f) * 64:(2 * f) * 64 + 64])
                        vector.tensor_copy(
                            v2[b0 + 32:b0 + 64, f * 128 + 64:f * 128 + 128],
                            vals_rep[b0 + 32:b0 + 64, (2 * f + 1) * 64:(2 * f + 1) * 64 + 64])
                    vector.engine_nop().then_inc(ctx.inc("vec"), 1)
            barrier_block(ctx)

            with nc.Block() as block:
                q0 = [ctx.val("dq0"), ctx.val("dq1")]
                s0 = ctx.val("scl")

                @block.sync
                def _(sync):
                    for it in range(NFT):
                        if it >= 2:
                            sync.wait_ge(ctx.h("scl"), s0 + it - 1)
                        dma(ctx, sync, ctx.scr_f32[:, (it % 2) * D:(it % 2) * D + D],
                            W["L0_attn_b_Wq"][it * 128:(it + 1) * 128, :],
                            sem=f"dq{it % 2}")

                @block.scalar
                def _(scalar):
                    for it in range(NFT):
                        scalar.wait_ge(ctx.h(f"dq{it % 2}"),
                                       q0[it % 2] + 16 * (it // 2 + 1))
                        scalar.activation(wq_sb[:, it * D:(it + 1) * D],
                                          ctx.scr_f32[:, (it % 2) * D:(it % 2) * D + D],
                                          AF.Copy).then_inc(ctx.inc("scl"), 1)
            barrier_block(ctx)

        if parts < 2:
            return
        es_r = contextlib.ExitStack()
        out_r = es_r.enter_context(
            nc.sbuf_tensor("ab_outr", [128, NFT * NTOK], BF16, side="right"))
        small_dbg = es.enter_context(nc.sbuf_tensor("ab_small", [128, 4 * 64], F32))
        with contextlib.ExitStack() as es2:
            xn_sb = ctx.scr_bf[:, 0:NFT * 512]
            q_sb = ctx.scr_bf[:, NFT * 512:2 * NFT * 512]
            z_sb = ctx.scr_f32[:, 0:2048]
            zb1 = ctx.scr_f32[:, 2048:4096]
            zb2 = ctx.scr_f32[:, 4096:6144]
            zb3 = ctx.scr_f32[:, 6144:8192]
            small = small_dbg
            a_bf = ctx.scr_xr[:, 0:2048]
            at_sb = ctx.scr_xr[:, 2048:2048 + 16 * 128]
            qp1 = es2.enter_context(nc.psum_tensor("ab_qp1", [128, 512], F32))
            qp2 = es2.enter_context(nc.psum_tensor("ab_qp2", [128, 512], F32))
            scp = es2.enter_context(nc.psum_tensor("ab_scp", [128, 512], F32))
            op1 = es2.enter_context(nc.psum_tensor("ab_op1", [128, 512], F32))
            op2 = es2.enter_context(nc.psum_tensor("ab_op2", [128, 512], F32))
            tp1 = es2.enter_context(nc.psum_tensor("ab_tp1", [128, 1024], BF16))
            tp2 = es2.enter_context(nc.psum_tensor("ab_tp2", [128, 1024], BF16))

            NSUP = NTOK // 512
            STAGE = 5
            with nc.Block() as block:
                d0 = ctx.val("dma")
                p0 = ctx.val("pe")
                v0 = ctx.val("vec")
                PEI = [0, 8, 12, 12, 28, 36][STAGE]
                VEI = [0, 8, 12, 13, 29, 37][STAGE]

                @block.sync
                def _(sync):
                    for sp in range(NSUP):
                        if sp >= 1:
                            sync.wait_ge(ctx.h("pe"), p0 + sp * PEI)
                        for it in range(NFT):
                            dma(ctx, sync, xn_sb[:, it * 512:(it + 1) * 512],
                                bass.AP(xn_dram.ap().tensor,
                                        it * 128 * NTOK + sp * 512,
                                        [[NTOK, 128], [1, 512]]))

                @block.tensor
                def _(tensor):
                    for sp in range(NSUP):
                        pb = p0 + sp * PEI
                        vb = v0 + sp * VEI
                        tensor.wait_ge(ctx.h("dma"), d0 + 16 * NFT * (sp + 1))
                        if sp >= 1:
                            tensor.wait_ge(ctx.h("vec"), vb)
                        nq = 0
                        for of in range(NFT):
                            qp = qp1 if of % 2 == 0 else qp2
                            if of >= 2:
                                tensor.wait_ge(ctx.h("vec"), vb + of - 1)
                            for it in range(NFT):
                                i = tensor.matmul(
                                    qp[:], wq_sb[:, it * D + of * 128:it * D + (of + 1) * 128],
                                    xn_sb[:, it * 512:(it + 1) * 512],
                                    start=(it == 0), stop=(it == NFT - 1))
                            i.then_inc(ctx.inc("pe"), 1)
                        if STAGE < 2:
                            continue
                        for tb in range(4):
                            tensor.wait_ge(ctx.h("vec"), vb + 8 + tb)
                            for h in range(H):
                                i = tensor.matmul(
                                    scp[:, h * 32:(h + 1) * 32],
                                    q_sb[:, (h // 2) * 512 + tb * 128:(h // 2) * 512 + tb * 128 + 128],
                                    keys_pad[:, h * S:(h + 1) * S],
                                    start=True, stop=True)
                            i.then_inc(ctx.inc("pe"), 1)
                        if STAGE < 4:
                            continue
                        tensor.wait_ge(ctx.h("vec"), vb + 13)
                        for g in range(16):
                            tp = tp1 if g % 2 == 0 else tp2
                            if g >= 2:
                                tensor.wait_ge(ctx.h("vec"), vb + 13 + g - 1)
                            tensor.transpose(tp[0:128, 0:128], a_bf[:, g * 128:(g + 1) * 128],
                                             ctx.ident[:]).then_inc(ctx.inc("pe"), 1)
                        if STAGE < 5:
                            continue
                        for tb in range(4):
                            tensor.wait_ge(ctx.h("vec"), vb + 13 + (tb + 1) * 4)
                            for half in range(2):
                                if tb >= 1:
                                    tensor.wait_ge(ctx.h("vec"),
                                                   vb + 29 + 2 * (tb - 1) + half + 1)
                                op = op1 if half == 0 else op2
                                for f in range(4 * half, 4 * half + 4):
                                    i = tensor.matmul(
                                        op[:, (f % 4) * 128:(f % 4) * 128 + 128],
                                        v2[:, f * 128:(f + 1) * 128],
                                        at_sb[:, (tb * 4 + f // 2) * 128:(tb * 4 + f // 2) * 128 + 128],
                                        start=True, stop=True)
                                i.then_inc(ctx.inc("pe"), 1)

                @block.vector
                def _(vector):
                    for sp in range(NSUP):
                        pb = p0 + sp * PEI
                        vb = v0 + sp * VEI
                        for of in range(NFT):
                            qp = qp1 if of % 2 == 0 else qp2
                            vector.wait_ge(ctx.h("pe"), pb + of + 1)
                            vector.tensor_scalar_add(
                                q_sb[:, of * 512:(of + 1) * 512], qp[:],
                                bcols[:, of:of + 1]).then_inc(ctx.inc("vec"), 1)
                        if STAGE < 2:
                            continue
                        for tb in range(4):
                            vector.wait_ge(ctx.h("pe"), pb + 8 + tb + 1)
                            vector.tensor_scalar_mul(
                                z_sb[:, tb * 512:(tb + 1) * 512], scp[:],
                                0.125).then_inc(ctx.inc("vec"), 1)
                        if STAGE < 3:
                            continue
                        zs, cum = _emit_sparsemax(ctx, vector, z_sb, zb1, zb2, zb3)
                        nsegs = 64
                        z3 = zs.rearrange("p (s c) -> p s c", c=32)
                        c3 = cum.rearrange("p (s c) -> p s c", c=32)
                        kr = apof(krow)
                        kb = bass.AP(kr.tensor, kr.offset,
                                     [list(kr.ap[0])] + [[0, nsegs], [1, 32]])
                        s3 = zb3.rearrange("p (s c) -> p s c", c=32)
                        vector.tensor_mul(s3[:], z3[:], kb)
                        vector.drain()
                        vector.tensor_sub(s3[:], s3[:], c3[:])
                        vector.drain()
                        vector.tensor_single_scalar(s3[:], s3[:], -1.0, ALU.is_gt)
                        vector.drain()
                        vector.reduce_sum(
                            small[:, 0:nsegs].rearrange("p (s c) -> p s c", c=1),
                            s3[:], axis=mybir.AxisListType.X)
                        vector.drain()
                        vector.tensor_mul(s3[:], s3[:], z3[:])
                        vector.drain()
                        vector.reduce_sum(
                            small[:, 64:64 + nsegs].rearrange("p (s c) -> p s c", c=1),
                            s3[:], axis=mybir.AxisListType.X)
                        vector.drain()
                        vector.reciprocal(small[:, 128:128 + nsegs], small[:, 0:nsegs])
                        vector.drain()
                        vector.tensor_scalar_add(small[:, 64:64 + nsegs],
                                                 small[:, 64:64 + nsegs], -1.0)
                        vector.drain()
                        vector.tensor_mul(small[:, 64:64 + nsegs],
                                          small[:, 64:64 + nsegs],
                                          small[:, 128:128 + nsegs])
                        vector.drain()
                        sm = apof(small)
                        taub = bass.AP(sm.tensor, sm.offset + 64,
                                       [list(sm.ap[0])] + [[1, nsegs], [0, 32]])
                        zr3 = z_sb.rearrange("p (s c) -> p s c", c=32)
                        vector.tensor_sub(zb3.rearrange("p (s c) -> p s c", c=32)[:],
                                          zr3[:], taub)
                        vector.drain()
                        i = vector.tensor_scalar_max(a_bf[:], zb3[:], 0.0)
                        i.then_inc(ctx.inc("vec"), 1)
                        if STAGE < 4:
                            continue
                        for g in range(16):
                            tp = tp1 if g % 2 == 0 else tp2
                            vector.wait_ge(ctx.h("pe"), pb + 12 + g + 1)
                            vector.tensor_copy(at_sb[:, g * 128:(g + 1) * 128],
                                               tp[0:128, 0:128]).then_inc(ctx.inc("vec"), 1)
                        if STAGE < 5:
                            continue
                        for tb in range(4):
                            tbg = sp * 4 + tb
                            bidx = tbg // 16
                            tbb = tbg % 16
                            for half in range(2):
                                vector.wait_ge(ctx.h("pe"),
                                               pb + 28 + 2 * tb + half + 1)
                                op = op1 if half == 0 else op2
                                opa = apof(op)
                                for f in range(4 * half, 4 * half + 4):
                                    src = bass.AP(opa.tensor, opa.offset + (f % 4) * 128,
                                                  [list(opa.ap[0])] + [[1, 16], [16, 8]])
                                    oro = apof(out_r)
                                    dst = bass.AP(oro.tensor,
                                                  oro.offset + f * NTOK + bidx * 2048 + tbb * 8,
                                                  [list(oro.ap[0])] + [[128, 16], [1, 8]])
                                    vector.tensor_copy(dst, src)
                                vector.engine_nop().then_inc(ctx.inc("vec"), 1)
            barrier_block(ctx)

        debug_dump(ctx, "ab_outr", out_r[:])
        debug_dump(ctx, "ab_kpad", keys_pad[:])
        debug_dump(ctx, "ab_v2", v2[:])
        debug_dump(ctx, "ab_z", ctx.scr_f32[:, 0:2048])
        debug_dump(ctx, "ab_zs", ctx.scr_f32[:, 2048:4096])
        debug_dump(ctx, "ab_cum", ctx.scr_f32[:, 4096:6144])
        debug_dump(ctx, "ab_small", small_dbg[:])
        debug_dump(ctx, "ab_abf", ctx.scr_xr[:, 0:2048])
        if parts < 3:
            es_r.close()
            return
        with nc.Block() as block:
            @block.sync
            def _(sync):
                for pt in range(NFT):
                    for bidx in range(B):
                        for hsub in range(2):
                            oro = apof(out_r)
                            pdim = list(oro.ap[0])
                            src = bass.AP(oro.tensor,
                                          oro.offset + (hsub * 64) * pdim[0]
                                          + pt * NTOK + bidx * 2048,
                                          [[pdim[0], 64]] + [[1, 2048]])
                            dst = bass.AP(scr2.ap().tensor,
                                          bidx * (16 * 16 * 64 * 128)
                                          + (2 * pt + hsub) * 131072,
                                          [[128, 64], [8192, 16], [1, 128]])
                            dma(ctx, sync, dst, src)
                wait_all(ctx, sync, "dma")
        barrier_block(ctx)
        es_r.close()

        if parts < 4:
            return
        with contextlib.ExitStack() as es3:
            wo_sb = es3.enter_context(nc.sbuf_tensor("ab_wo", [128, NFT * D], BF16))
            rhs_sb = es3.enter_context(nc.sbuf_tensor("ab_rhs", [128, NFT * 512], BF16))
            pw1 = es3.enter_context(nc.psum_tensor("ab_pw1", [128, 512], F32))
            pw2 = es3.enter_context(nc.psum_tensor("ab_pw2", [128, 512], F32))
            with nc.Block() as block:
                q0 = [ctx.val("dq0"), ctx.val("dq1")]
                s0 = ctx.val("scl")

                @block.sync
                def _(sync):
                    for it in range(NFT):
                        if it >= 2:
                            sync.wait_ge(ctx.h("scl"), s0 + it - 1)
                        dma(ctx, sync, ctx.scr_f32[:, (it % 2) * D:(it % 2) * D + D],
                            W["L0_attn_b_Wo"][it * 128:(it + 1) * 128, :],
                            sem=f"dq{it % 2}")

                @block.scalar
                def _(scalar):
                    for it in range(NFT):
                        scalar.wait_ge(ctx.h(f"dq{it % 2}"),
                                       q0[it % 2] + 16 * (it // 2 + 1))
                        scalar.activation(wo_sb[:, it * D:(it + 1) * D],
                                          ctx.scr_f32[:, (it % 2) * D:(it % 2) * D + D],
                                          AF.Copy).then_inc(ctx.inc("scl"), 1)
            barrier_block(ctx)
            for ch in range(NTOK // 512):
                bidx = ch // 4
                l2c = (ch % 4) * 512
                with nc.Block() as block:
                    d0 = ctx.val("dma")
                    b0 = ctx.val("dmb")
                    p0 = ctx.val("pe")
                    v0 = ctx.val("vec")

                    @block.sync
                    def _(sync, ch=ch, bidx=bidx, l2c=l2c):
                        for ct in range(NFT):
                            for psub in range(2):
                                src = bass.AP(scr2.ap().tensor,
                                              bidx * (16 * 16 * 64 * 128)
                                              + (2 * ct + psub) * 8192
                                              + (l2c // 128) * 131072,
                                              [[128, 64], [131072, 4], [1, 128]])
                                dma(ctx, sync,
                                    rhs_sb[psub * 64:psub * 64 + 64,
                                           ct * 512:(ct + 1) * 512], src)

                    @block.scalar
                    def _(scalar, ch=ch):
                        for of in range(NFT):
                            scalar.wait_ge(ctx.h("vec"), v0 + of - 1)
                            dma(ctx, scalar,
                                ctx.scr_f32[:, (of % 2) * 512:(of % 2) * 512 + 512],
                                xres.ap()[of, :, ch * 512:(ch + 1) * 512], sem="dmb")
                            scalar.wait_ge(ctx.h("vec"), v0 + of + 1)
                            dma(ctx, scalar, xres.ap()[of, :, ch * 512:(ch + 1) * 512],
                                ctx.scr_f32[:, NTOK + (of % 2) * 512:NTOK + (of % 2) * 512 + 512],
                                sem="dmb")

                    @block.tensor
                    def _(tensor, ch=ch):
                        tensor.wait_ge(ctx.h("dma"), d0 + 16 * 2 * NFT)
                        for of in range(NFT):
                            pw = pw1 if of % 2 == 0 else pw2
                            if of >= 2:
                                tensor.wait_ge(ctx.h("vec"), v0 + of - 1)
                            for ct in range(NFT):
                                i = tensor.matmul(
                                    pw[:], wo_sb[:, ct * D + of * 128:ct * D + (of + 1) * 128],
                                    rhs_sb[:, ct * 512:(ct + 1) * 512],
                                    start=(ct == 0), stop=(ct == NFT - 1))
                            i.then_inc(ctx.inc("pe"), 1)

                    @block.vector
                    def _(vector, ch=ch):
                        for of in range(NFT):
                            pw = pw1 if of % 2 == 0 else pw2
                            vector.wait_ge(ctx.h("pe"), p0 + of + 1)
                            vector.wait_ge(ctx.h("dmb"), b0 + 16 * (2 * of + 1))
                            vector.scalar_tensor_tensor(
                                ctx.scr_f32[:, NTOK + (of % 2) * 512:NTOK + (of % 2) * 512 + 512],
                                pw[:], bcols[:, NFT + of:NFT + of + 1],
                                ctx.scr_f32[:, (of % 2) * 512:(of % 2) * 512 + 512],
                                ALU.add, ALU.add).then_inc(ctx.inc("vec"), 1)
                        wait_all(ctx, vector, "dmb")
            barrier_block(ctx)
    barrier_block(ctx)


def build_attna(ctx, xres, W, qraw_d, kraw_d, vraw_d, ks_d):
    nc = ctx.nc
    H, DH = 16, 64
    with contextlib.ExitStack() as es:
        es_xn = contextlib.ExitStack()
        xn_all = es_xn.enter_context(
            nc.sbuf_tensor("aa_xn", [128, NFT * NTOK], BF16, side="right"))
        xn = [xn_all[:, i * NTOK:(i + 1) * NTOK] for i in range(NFT)]
        build_layernorm(ctx, xres, xn)

        kv_stage = ctx.scr_bf[:, 0:NTOK]
        for wname, dscr in (("L1_attn_a_Wq", qraw_d),
                            ("L1_attn_a_Wk", kraw_d), ("L1_attn_a_Wv", vraw_d)):
            def evac_kv(of, ch, ps, eng, dscr=dscr):
                return eng.tensor_copy(
                    kv_stage[:, ch * 512:(ch + 1) * 512], ps[:])
            for of in range(NFT):
                build_proj(ctx, xn, W[wname], 1, evac_kv, of0=of, evac_eng="vec")
                with nc.Block() as block:
                    @block.scalar
                    def _(scalar, of=of, dscr=dscr):
                        dma(ctx, scalar, dscr.ap()[of], kv_stage, sem="dmb")
                        scalar.wait_ge(ctx.h("dmb"), ctx.val("dmb"))
        es_xn.close()

        for t in range(NFT):
            with nc.Block() as block:
                d0 = ctx.val("dma")
                v0 = ctx.val("vec")
                s0 = ctx.val("scl")
                qt = ctx.scr_bf[:, 0:NTOK]

                @block.sync
                def _(sync, t=t):
                    dma(ctx, sync, qt, qraw_d.ap()[t])

                @block.vector
                def _(vector, t=t):
                    vector.wait_ge(ctx.h("dma"), d0 + 16)
                    vector.tensor_scalar_min(ctx.scr_xr[:, 0:NTOK], qt, 0.0).then_inc(
                        ctx.inc("vec"), 1)
                    vector.wait_ge(ctx.h("scl"), s0 + 1)
                    vector.tensor_scalar_max(qt, qt, 0.0)
                    vector.drain()
                    vector.tensor_add(qt, qt, ctx.scr_xr[:, NTOK:2 * NTOK]).then_inc(
                        ctx.inc("vec"), 1)

                @block.scalar
                def _(scalar, t=t):
                    scalar.wait_ge(ctx.h("vec"), v0 + 1)
                    scalar.activation(ctx.scr_xr[:, NTOK:2 * NTOK],
                                      ctx.scr_xr[:, 0:NTOK],
                                      AF.Exp).then_inc(ctx.inc("scl"), 1)
                    scalar.wait_ge(ctx.h("vec"), v0 + 2)
                    dma(ctx, scalar, qraw_d.ap()[t], qt, sem="dmb")
                    scalar.wait_ge(ctx.h("dmb"), ctx.val("dmb"))
        barrier_block(ctx)

        with contextlib.ExitStack() as e2:
            ktm = e2.enter_context(nc.sbuf_tensor("aa_ktm", [128, D], BF16))
            vtm = e2.enter_context(nc.sbuf_tensor("aa_vtm", [128, D], BF16))
            kvsb = e2.enter_context(nc.sbuf_tensor("aa_kv", [64, 2 * D], F32))
            kssb = e2.enter_context(nc.sbuf_tensor("aa_ks", [1, 2 * D], F32))
            blkin = e2.enter_context(nc.sbuf_tensor("aa_blk", [128, 2 * D], BF16))
            tp1 = e2.enter_context(nc.psum_tensor("aa_tp1", [128, 1024], BF16))
            tp2 = e2.enter_context(nc.psum_tensor("aa_tp2", [128, 1024], BF16))
            kvp = e2.enter_context(nc.psum_tensor("aa_kvp", [64, 1024], F32))
            ksp = e2.enter_context(nc.psum_tensor("aa_ksp", [1, 1024], F32))
            with nc.Block() as block:
                @block.vector
                def _(vector):
                    vector.memset(kvsb[:], 0.0)
                    vector.memset(kssb[:], 0.0).then_inc(ctx.inc("vec"), 1)
            barrier_block(ctx)
            NTB = NTOK // 128
            for tb in range(NTB):
                bidx = tb // 16
                with nc.Block() as block:
                    q0 = [ctx.val("dq0"), ctx.val("dq1")]
                    p0 = ctx.val("pe")
                    v0 = ctx.val("vec")
                    s0 = ctx.val("scl")

                    @block.sync
                    def _(sync, tb=tb):
                        for t in range(NFT):
                            src = bass.AP(kraw_d.ap().tensor,
                                          t * 128 * NTOK + tb * 128,
                                          [[NTOK, 128], [1, 128]])
                            dma(ctx, sync, blkin[:, t * 128:(t + 1) * 128], src,
                                sem="dq0")
                        for t in range(NFT):
                            src = bass.AP(vraw_d.ap().tensor,
                                          t * 128 * NTOK + tb * 128,
                                          [[NTOK, 128], [1, 128]])
                            dma(ctx, sync, blkin[:, D + t * 128:D + (t + 1) * 128],
                                src, sem="dq1")

                    @block.tensor
                    def _(tensor, tb=tb):
                        tensor.wait_ge(ctx.h("dq0"), q0[0] + 16 * NFT)
                        for t in range(NFT):
                            tensor.transpose(tp1[:, t * 128:(t + 1) * 128],
                                             blkin[:, t * 128:(t + 1) * 128],
                                             ctx.ident[:])
                        tensor.wait_ge(ctx.h("dq1"), q0[1] + 16 * NFT)
                        for t in range(NFT):
                            i = tensor.transpose(tp2[:, t * 128:(t + 1) * 128],
                                                 blkin[:, D + t * 128:D + (t + 1) * 128],
                                                 ctx.ident[:])
                        i.then_inc(ctx.inc("pe"), 1)
                        tensor.wait_ge(ctx.h("vec"), v0 + 3)
                        for h in range(H):
                            i = tensor.matmul(kvp[:, h * 64:(h + 1) * 64],
                                              ktm[:, h * 64:(h + 1) * 64],
                                              vtm[:, h * 64:(h + 1) * 64],
                                              start=True, stop=True)
                        i.then_inc(ctx.inc("pe"), 1)
                        for hc in range(2):
                            i = tensor.matmul(ksp[:, hc * 512:(hc + 1) * 512],
                                              ctx.ones_col[:],
                                              ktm[:, hc * 512:(hc + 1) * 512],
                                              start=True, stop=True)
                        i.then_inc(ctx.inc("pe"), 1)

                    @block.vector
                    def _(vector, tb=tb, bidx=bidx):
                        vector.wait_ge(ctx.h("pe"), p0 + 1)
                        vector.tensor_copy(vtm[:], tp2[:]).then_inc(ctx.inc("vec"), 1)
                        vector.tensor_scalar_min(ctx.scr_xr[:, 0:D], tp1[:], 0.0).then_inc(
                            ctx.inc("vec"), 1)
                        vector.wait_ge(ctx.h("scl"), s0 + 1)
                        vector.tensor_scalar_max(ktm[:], tp1[:], 0.0)
                        vector.drain()
                        vector.tensor_add(ktm[:], ktm[:],
                                          ctx.scr_xr[:, NTOK:NTOK + D]).then_inc(
                            ctx.inc("vec"), 1)
                        vector.wait_ge(ctx.h("pe"), p0 + 2)
                        vector.tensor_add(kvsb[:, bidx * D:(bidx + 1) * D],
                                          kvsb[:, bidx * D:(bidx + 1) * D], kvp[:])
                        vector.drain().then_inc(ctx.inc("vec"), 1)
                        vector.wait_ge(ctx.h("pe"), p0 + 3)
                        vector.tensor_add(kssb[:, bidx * D:(bidx + 1) * D],
                                          kssb[:, bidx * D:(bidx + 1) * D], ksp[:])
                        vector.drain().then_inc(ctx.inc("vec"), 1)

                    @block.scalar
                    def _(scalar, tb=tb):
                        scalar.wait_ge(ctx.h("vec"), v0 + 2)
                        scalar.activation(ctx.scr_xr[:, NTOK:NTOK + D],
                                          ctx.scr_xr[:, 0:D],
                                          AF.Exp).then_inc(ctx.inc("scl"), 1)

            with nc.Block() as block:
                @block.scalar
                def _(scalar):
                    dma(ctx, scalar, ks_d.ap()[:], kssb[:], sem="dmb")
                    scalar.wait_ge(ctx.h("dmb"), ctx.val("dmb"))
            barrier_block(ctx)
            kspad = e2.enter_context(nc.sbuf_tensor("aa_kspad", [128, 2 * 128], BF16))
            kvb = e2.enter_context(nc.sbuf_tensor("aa_kvb", [64, 2 * D], BF16))
            z2b = e2.enter_context(nc.sbuf_tensor("aa_z2b", [16, NTOK], BF16))
            kscol = e2.enter_context(nc.sbuf_tensor("aa_kscol", [128, 16], F32))
            kv2 = e2.enter_context(nc.sbuf_tensor("aa_kv2", [128, 2 * NFT * 128], BF16))
            z2 = e2.enter_context(nc.sbuf_tensor("aa_z2", [16, NTOK], F32))
            with nc.Block() as block:
                d0 = ctx.val("dma")

                @block.sync
                def _(sync):
                    for bidx in range(B):
                        for t in range(NFT):
                            src = bass.AP(ks_d.ap().tensor, bidx * D + t * 128,
                                          [[1, 128], [1, 1]])
                            dma(ctx, sync, kscol[:, bidx * NFT + t:bidx * NFT + t + 1],
                                src)

                @block.vector
                def _(vector):
                    wait_all(ctx, vector, "dma")
                    vector.tensor_copy(kvb[:], kvsb[:])
                    vector.memset(kspad[:], 0.0)
                    vector.drain()
                    for bidx in range(B):
                        for h in range(H):
                            r0 = (h % 2) * 64
                            vector.tensor_copy(
                                kspad[r0:r0 + 64,
                                      bidx * 128 + (h // 2) * 16 + h:bidx * 128 + (h // 2) * 16 + h + 1],
                                kscol[r0:r0 + 64,
                                      bidx * NFT + h // 2:bidx * NFT + h // 2 + 1])
                    vector.engine_nop().then_inc(ctx.inc("vec"), 1)
            barrier_block(ctx)

            qblk = e2.enter_context(nc.sbuf_tensor("aa_qblk", [128, 2 * NFT * 512], BF16))
            with (
                nc.psum_tensor("aa_zp", [16, 512], F32) as zp,
                nc.psum_tensor("aa_k2p", [128, 128], F32) as k2p,
                nc.Block() as block,
            ):
                p0 = ctx.val("pe")
                v0 = ctx.val("vec")
                qz0 = [ctx.val("dq0"), ctx.val("dq1")]

                @block.sync
                def _(sync):
                    n = 0
                    for bidx in range(B):
                        for chk in range(T // 512):
                            if n >= 2:
                                sync.wait_ge(ctx.h("pe"), p0 + n - 1)
                            for t in range(NFT):
                                src = bass.AP(qraw_d.ap().tensor,
                                              t * 128 * NTOK + bidx * T + chk * 512,
                                              [[NTOK, 128], [1, 512]])
                                dma(ctx, sync,
                                    qblk[:, (n % 2) * 4096 + t * 512:(n % 2) * 4096 + (t + 1) * 512],
                                    src, sem=f"dq{n % 2}")
                            n += 1

                @block.tensor
                def _(tensor):
                    n = 0
                    qq0 = [ctx.val("dq0") - 0, ctx.val("dq1")]
                    for bidx in range(B):
                        for chk in range(T // 512):
                            if n >= 1:
                                tensor.wait_ge(ctx.h("vec"), v0 + n)
                            tensor.wait_ge(ctx.h(f"dq{n % 2}"),
                                           qz0[n % 2] + 16 * NFT * (n // 2 + 1))
                            for t in range(NFT):
                                i = tensor.matmul(
                                    zp[:], kspad[:, bidx * 128 + t * 16:bidx * 128 + (t + 1) * 16],
                                    qblk[:, (n % 2) * 4096 + t * 512:(n % 2) * 4096 + (t + 1) * 512],
                                    start=(t == 0), stop=(t == NFT - 1))
                            i.then_inc(ctx.inc("pe"), 1)
                            n += 1
                    for bidx in range(B):
                        for f in range(NFT):
                            tensor.wait_ge(ctx.h("vec"), v0 + n)
                            i = tensor.matmul(k2p[:, 0:64],
                                              ctx.ident[0:64, :],
                                              kvb[:, bidx * D + (2 * f) * 64:bidx * D + (2 * f) * 64 + 64],
                                              start=True, stop=True)
                            i = tensor.matmul(k2p[:, 64:128],
                                              ctx.selhi[:],
                                              kvb[:, bidx * D + (2 * f + 1) * 64:bidx * D + (2 * f + 1) * 64 + 64],
                                              start=True, stop=True)
                            i.then_inc(ctx.inc("pe"), 1)
                            n += 1

                @block.vector
                def _(vector):
                    n = 0
                    for bidx in range(B):
                        for chk in range(T // 512):
                            vector.wait_ge(ctx.h("pe"), p0 + n + 1)
                            vector.tensor_copy(
                                z2[:, bidx * T + chk * 512:bidx * T + (chk + 1) * 512],
                                zp[:]).then_inc(ctx.inc("vec"), 1)
                            n += 1
                    for bidx in range(B):
                        for f in range(NFT):
                            vector.wait_ge(ctx.h("pe"), p0 + n + 1)
                            vector.tensor_copy(
                                kv2[:, (bidx * NFT + f) * 128:(bidx * NFT + f + 1) * 128],
                                k2p[:]).then_inc(ctx.inc("vec"), 1)
                            n += 1
                    vector.tensor_scalar_add(z2[:], z2[:], 1e-6)
                    vector.drain()
                    vector.reciprocal(z2[:], z2[:])
                    vector.drain()
                    vector.tensor_copy(z2b[:], z2[:])
                    vector.drain().then_inc(ctx.inc("vec"), 1)
            barrier_block(ctx)

            with (
                nc.psum_tensor("aa_op", [128, 512], F32) as opp,
                nc.psum_tensor("aa_zb", [128, 512], F32) as zbp,
                nc.sbuf_tensor("aa_zbs", [128, 512], F32) as zbs,
                nc.sbuf_tensor("aa_ao", [128, NTOK], BF16) as ao,
                nc.sbuf_tensor("aa_qtile", [128, NTOK], BF16) as qtile,
            ):
                for f in range(NFT):
                    with nc.Block() as block:
                        p0 = ctx.val("pe")
                        v0 = ctx.val("vec")
                        d0 = ctx.val("dma")

                        @block.sync
                        def _(sync, f=f):
                            dma(ctx, sync, qtile[:], qraw_d.ap()[f])

                        @block.tensor
                        def _(tensor, f=f, d0=d0):
                            tensor.wait_ge(ctx.h("dma"), d0 + 16)
                            n = 0
                            for bidx in range(B):
                                for chk in range(T // 512):
                                    if n >= 1:
                                        tensor.wait_ge(ctx.h("vec"), v0 + 2 * n)
                                    tensor.matmul(
                                        zbp[:], ctx.ef_const[:, f * 128:(f + 1) * 128],
                                        z2b[:, bidx * T + chk * 512:bidx * T + (chk + 1) * 512],
                                        start=True, stop=True).then_inc(ctx.inc("pe"), 1)
                                    tensor.matmul(
                                        opp[:], kv2[:, (bidx * NFT + f) * 128:(bidx * NFT + f + 1) * 128],
                                        qtile[:, bidx * T + chk * 512:
                                              bidx * T + (chk + 1) * 512],
                                        start=True, stop=True).then_inc(ctx.inc("pe"), 1)
                                    n += 1

                        @block.vector
                        def _(vector, f=f):
                            n = 0
                            for bidx in range(B):
                                for chk in range(T // 512):
                                    c = slice(bidx * T + chk * 512,
                                              bidx * T + (chk + 1) * 512)
                                    vector.wait_ge(ctx.h("pe"), p0 + 2 * n + 1)
                                    vector.tensor_copy(zbs[:], zbp[:]).then_inc(
                                        ctx.inc("vec"), 1)
                                    vector.wait_ge(ctx.h("pe"), p0 + 2 * n + 2)
                                    vector.drain()
                                    vector.tensor_mul(ao[:, c], opp[:], zbs[:]).then_inc(
                                        ctx.inc("vec"), 1)
                                    n += 1

                        v1 = ctx.val("vec")

                        @block.scalar
                        def _(scalar, f=f):
                            scalar.wait_ge(ctx.h("vec"), v1)
                            dma(ctx, scalar, kraw_d.ap()[f], ao[:], sem="dmb")
                            scalar.wait_ge(ctx.h("dmb"), ctx.val("dmb"))
            barrier_block(ctx)

        ao_all = es.enter_context(
            nc.sbuf_tensor("aa_aoall", [128, NFT * NTOK], BF16, side="right"))
        with nc.Block() as block:
            d0 = ctx.val("dma")

            @block.sync
            def _(sync):
                for t in range(NFT):
                    dma(ctx, sync, ao_all[:, t * NTOK:(t + 1) * NTOK],
                        kraw_d.ap()[t])
                wait_all(ctx, sync, "dma")
        build_proj_residual(ctx, ao_all[:], NFT, W["L1_attn_a_Wo"],
                            W["L1_attn_a_bo"], xres)
    barrier_block(ctx)


def build_ffn1_prep(ctx, W):
    nc = ctx.nc
    ctx.f1_a1 = ctx._stack_append(nc.sbuf_tensor("f1_a1", [128, 32], F32))
    ctx.f1_a2 = ctx._stack_append(nc.sbuf_tensor("f1_a2", [128, NFT], F32))
    ctx.f1_b1 = ctx._stack_append(nc.sbuf_tensor("f1_b1", [128, 32], F32))
    ctx.f1_b2 = ctx._stack_append(nc.sbuf_tensor("f1_b2", [128, NFT], F32))
    wtile = ctx.scr_f32[:, 0:2 * D]
    with nc.Block() as block:
        q0 = [ctx.val("dq0"), ctx.val("dq1")]
        v0 = ctx.val("vec")

        @block.sync
        def _(sync):
            for ht in range(32):
                if ht >= 2:
                    sync.wait_ge(ctx.h("vec"), v0 + ht - 1)
                dma(ctx, sync, wtile[:, (ht % 2) * D:(ht % 2) * D + D],
                    W["L1_ffn_w1"][ht * 128:(ht + 1) * 128, :], sem=f"dq{ht % 2}")
            for ft in range(NFT):
                dma(ctx, sync, ctx.f1_b1[:, 0:32],
                    W["L1_ffn_b1"][0:0, :]) if False else None
            for bi in range(32):
                dma(ctx, sync, ctx.f1_b1[:, bi:bi + 1],
                    W["L1_ffn_b1"][bi * 128:(bi + 1) * 128, :])
            for bi in range(NFT):
                dma(ctx, sync, ctx.f1_b2[:, bi:bi + 1],
                    W["L1_ffn_b2"][bi * 128:(bi + 1) * 128, :])

        @block.vector
        def _(vector):
            for ht in range(32):
                vector.wait_ge(ctx.h(f"dq{ht % 2}"), q0[ht % 2] + 16 * (ht // 2 + 1))
                vector.reduce_sum(ctx.f1_a1[:, ht:ht + 1],
                                  wtile[:, (ht % 2) * D:(ht % 2) * D + D],
                                  axis=mybir.AxisListType.X,
                                  apply_absolute_value=True)
                vector.drain()
                vector.tensor_scalar_mul(ctx.f1_a1[:, ht:ht + 1],
                                         ctx.f1_a1[:, ht:ht + 1], 1.0 / D)
                vector.drain().then_inc(ctx.inc("vec"), 1)
    barrier_block(ctx)
    w2tile = ctx.scr_f32[:, 0:NTOK]
    for dt in range(NFT):
        with nc.Block() as block:
            d0 = ctx.val("dma")

            @block.sync
            def _(sync, dt=dt):
                dma(ctx, sync, w2tile, W["L1_ffn_w2"][dt * 128:(dt + 1) * 128, :])

            @block.vector
            def _(vector, dt=dt, d0=d0):
                vector.wait_ge(ctx.h("dma"), d0 + 16)
                vector.reduce_sum(ctx.f1_a2[:, dt:dt + 1], w2tile,
                                  axis=mybir.AxisListType.X,
                                  apply_absolute_value=True)
                vector.drain()
                vector.tensor_scalar_mul(ctx.f1_a2[:, dt:dt + 1],
                                         ctx.f1_a2[:, dt:dt + 1],
                                         1.0 / (4 * D))
                vector.drain().then_inc(ctx.inc("vec"), 1)
    barrier_block(ctx)


def build_ffn1(ctx, xres, W, h_dram):
    nc = ctx.nc
    with contextlib.ExitStack() as es:
        es_xn = contextlib.ExitStack()
        xn_all = es_xn.enter_context(
            nc.sbuf_tensor("f1_xn", [128, NFT * NTOK], BF16, side="right"))
        xn = [xn_all[:, i * NTOK:(i + 1) * NTOK] for i in range(NFT)]
        build_layernorm(ctx, xres, xn)

        hst = ctx.scr_bf[:, 0:NTOK]
        for ht in range(32):
            def evac_h(of, ch, ps, eng, ht=ht):
                return eng.activation(hst[:, ch * 512:(ch + 1) * 512], ps[:],
                                      AF.Gelu,
                                      scale=ctx.f1_a1[:, ht:ht + 1],
                                      bias=ctx.f1_b1[:, ht:ht + 1])
            build_proj(ctx, xn, W["L1_ffn_w1T"], 1, evac_h, of0=ht,
                       evac_eng="scl", conv_func=AF.Sign)
            with nc.Block() as block:
                @block.scalar
                def _(scalar, ht=ht):
                    dma(ctx, scalar, h_dram.ap()[ht], hst, sem="dmb")
                    scalar.wait_ge(ctx.h("dmb"), ctx.val("dmb"))
        es_xn.close()

        with contextlib.ExitStack() as e2:
            hblk = e2.enter_context(
                nc.sbuf_tensor("f1_hblk", [128, 32 * 512], BF16, side="right"))
            sgw = e2.enter_context(nc.sbuf_tensor("f1_sgw", [128, 2 * 32 * 128], BF16))
            wfst = e2.enter_context(nc.sbuf_tensor("f1_wf", [128, 2 * 128], F32))
            p1 = e2.enter_context(nc.psum_tensor("f1_p1", [128, 512], F32))
            p2 = e2.enter_context(nc.psum_tensor("f1_p2", [128, 512], F32))
            xf = ctx.scr_f32[:, 0:NTOK]
            of_out = ctx.scr_f32[:, NTOK:2 * NTOK]
            for ch in range(NTOK // 512):
                with nc.Block() as block:
                    q0 = [ctx.val("dq0"), ctx.val("dq1")]
                    b0 = ctx.val("dmg")
                    g0 = ctx.val("gps")
                    p0 = ctx.val("pe")
                    v0 = ctx.val("vec")

                    d0 = ctx.val("dma")

                    @block.sync
                    def _(sync, ch=ch):
                        for ht in range(32):
                            src = bass.AP(h_dram.ap().tensor,
                                          ht * 128 * NTOK + ch * 512,
                                          [[NTOK, 128], [1, 512]])
                            dma(ctx, sync, hblk[:, ht * 512:(ht + 1) * 512], src)
                        n = 0
                        for of in range(NFT):
                            for it in range(32):
                                if n >= 2:
                                    sync.wait_ge(ctx.h("gps"), g0 + n - 1)
                                dma(ctx, sync,
                                    wfst[:, (n % 2) * 128:(n % 2) * 128 + 128],
                                    W["L1_ffn_w2T"][it * 128:(it + 1) * 128,
                                                    of * 128:(of + 1) * 128],
                                    sem=f"dq{n % 2}")
                                n += 1

                    @block.scalar
                    def _(scalar, ch=ch):
                        n = 0
                        for of in range(NFT):
                            for it in range(32):
                                scalar.wait_ge(ctx.h(f"dq{n % 2}"),
                                               q0[n % 2] + 16 * (n // 2 + 1))
                                if of >= 2 and it == 0:
                                    scalar.wait_ge(ctx.h("pe"), p0 + of - 1)
                                half = (of % 2) * 32
                                scalar.activation(
                                    sgw[:, (half + it) * 128:(half + it + 1) * 128],
                                    wfst[:, (n % 2) * 128:(n % 2) * 128 + 128],
                                    AF.Sign).then_inc(ctx.inc("gps"), 1)
                                n += 1

                    @block.gpsimd
                    def _(gpsimd, ch=ch):
                        for of in range(NFT):
                            gpsimd.wait_ge(ctx.h("vec"), v0 + max(of - 1, 0))
                            dma(ctx, gpsimd,
                                xf[:, (of % 2) * 512:(of % 2) * 512 + 512],
                                xres.ap()[of, :, ch * 512:(ch + 1) * 512], sem="dmg")
                            gpsimd.wait_ge(ctx.h("vec"), v0 + of + 1)
                            dma(ctx, gpsimd,
                                xres.ap()[of, :, ch * 512:(ch + 1) * 512],
                                of_out[:, (of % 2) * 512:(of % 2) * 512 + 512],
                                sem="dmg")

                    @block.tensor
                    def _(tensor, ch=ch, d0=d0):
                        tensor.wait_ge(ctx.h("dma"), d0 + 16 * 32)
                        for of in range(NFT):
                            ps = p1 if of % 2 == 0 else p2
                            tensor.wait_ge(ctx.h("gps"), g0 + 32 * (of + 1))
                            if of >= 2:
                                tensor.wait_ge(ctx.h("vec"), v0 + of - 1)
                            half = (of % 2) * 32
                            for it in range(32):
                                i = tensor.matmul(
                                    ps[:], sgw[:, (half + it) * 128:(half + it + 1) * 128],
                                    hblk[:, it * 512:(it + 1) * 512],
                                    start=(it == 0), stop=(it == 31))
                            i.then_inc(ctx.inc("pe"), 1)

                    @block.vector
                    def _(vector, ch=ch, b0=b0):
                        for of in range(NFT):
                            ps = p1 if of % 2 == 0 else p2
                            vector.wait_ge(ctx.h("pe"), p0 + of + 1)
                            vector.wait_ge(ctx.h("dmg"), b0 + 16 * (2 * of + 1))
                            c = slice((of % 2) * 512, (of % 2) * 512 + 512)
                            vector.tensor_scalar(of_out[:, c], ps[:],
                                                 ctx.f1_a2[:, of:of + 1],
                                                 ctx.f1_b2[:, of:of + 1],
                                                 ALU.mult, ALU.add)
                            vector.drain()
                            vector.tensor_add(of_out[:, c], of_out[:, c],
                                              xf[:, c]).then_inc(ctx.inc("vec"), 1)
                        wait_all(ctx, vector, "dmg")
    barrier_block(ctx)


def build_rwkv(ctx, xres, W, h2_d):
    nc = ctx.nc
    NG = NFT
    with contextlib.ExitStack() as es:
        if getattr(ctx, "debug", False):
            kw = {"kind": "ExternalOutput"}
        else:
            kw = {}
        xb_d = nc.dram_tensor("rw_xb", [NFT, 128, NTOK], F32, **kw)
        kvr_d = nc.dram_tensor("rw_kvr", [3 * NFT, 128, NTOK], F32, **kw)
        x1_d = nc.dram_tensor("rw_x1", [NFT, 128, NTOK], F32, **kw)
        rr_d = nc.dram_tensor("rw_rr", [NFT, 128, NTOK], F32, **kw)

        cols = es.enter_context(nc.sbuf_tensor("rw_cols", [128, 12 * NG + 1], F32))
        lncols = es.enter_context(nc.sbuf_tensor("rw_lnc", [128, 4 * NG], F32))
        tau = es.enter_context(nc.sbuf_tensor("rw_tau", [128, T], F32))
        CW, CU, CTK, COTK, CTV, COTV, CTR, COTR, CCK, COCK, CCR, COCR = \
            [i * NG for i in range(12)]
        zcol = cols[:, 12 * NG:12 * NG + 1]

        def col(base, i):
            return cols[:, base + i:base + i + 1]

        with nc.Block() as block:
            d0 = ctx.val("dma")

            @block.sync
            def _(sync):
                dma(ctx, sync, tau[:], W["const_tau"][:])
                for gi, pname in ((CW, "L1_attn_b_time_decay"),
                                  (CU, "L1_attn_b_time_first"),
                                  (CTK, "L1_attn_b_tmk"),
                                  (CTV, "L1_attn_b_tmv"),
                                  (CTR, "L1_attn_b_tmr"),
                                  (CCK, "L1_attn_b_cmk"),
                                  (CCR, "L1_attn_b_cmr")):
                    for ft in range(NFT):
                        dma(ctx, sync, cols[:, gi + ft:gi + ft + 1],
                            W[pname][ft * 128:(ft + 1) * 128, :])
                for gi, pname in ((0, "L1_attn_b_g1"), (NG, "L1_attn_b_b1"),
                                  (2 * NG, "L1_attn_b_g2"),
                                  (3 * NG, "L1_attn_b_b2")):
                    for ft in range(NFT):
                        dma(ctx, sync, lncols[:, gi + ft:gi + ft + 1],
                            W[pname][ft * 128:(ft + 1) * 128, :])

            @block.vector
            def _(vector):
                vector.wait_ge(ctx.h("dma"), d0 + 16 * (1 + 7 * NFT + 4 * NFT))
                vector.memset(zcol, 0.0)
                vector.drain()
                for src, dst in ((CTK, COTK), (CTV, COTV), (CTR, COTR),
                                 (CCK, COCK), (CCR, COCR)):
                    vector.tensor_scalar(cols[:, dst:dst + NG],
                                         cols[:, src:src + NG], -1.0, 1.0,
                                         ALU.mult, ALU.add)
                vector.drain().then_inc(ctx.inc("vec"), 1)
        barrier_block(ctx)

        g1c = [lncols[:, i:i + 1] for i in range(NG)]
        b1c = [lncols[:, NG + i:NG + i + 1] for i in range(NG)]
        g2c = [lncols[:, 2 * NG + i:2 * NG + i + 1] for i in range(NG)]
        b2c = [lncols[:, 3 * NG + i:3 * NG + i + 1] for i in range(NG)]

        es_xn = contextlib.ExitStack()
        xn_all = es_xn.enter_context(
            nc.sbuf_tensor("rw_xn", [128, NFT * NTOK], BF16, side="right"))
        xn = [xn_all[:, i * NTOK:(i + 1) * NTOK] for i in range(NFT)]
        build_layernorm(ctx, xres, xn, fp32_dram=xb_d)
        build_layernorm(ctx, xb_d, xn, gain=g1c, bias=b1c)

        st32 = ctx.scr_f32b[:, 0:NTOK]
        for of in range(3 * NFT):
            def evac_kvr(o, ch, ps, eng):
                return eng.tensor_copy(st32[:, ch * 512:(ch + 1) * 512], ps[:])
            build_proj(ctx, xn, W["L1_attn_b_W_tm"], 1, evac_kvr, of0=of,
                       evac_eng="vec")
            with nc.Block() as block:
                @block.scalar
                def _(scalar, of=of):
                    dma(ctx, scalar, kvr_d.ap()[of], st32, sem="dmb")
                    scalar.wait_ge(ctx.h("dmb"), ctx.val("dmb"))
        es_xn.close()

        with contextlib.ExitStack() as eb:
            ED = eb.enter_context(nc.sbuf_tensor("rw_ed", [128, T], F32))
            P1 = eb.enter_context(nc.sbuf_tensor("rw_p1", [128, T], F32))
            S3 = eb.enter_context(nc.sbuf_tensor("rw_s3", [128, T], F32))
            WT = eb.enter_context(nc.sbuf_tensor("rw_wt", [128, T], F32))
            ONE = eb.enter_context(nc.sbuf_tensor("rw_one", [128, T], F32))
            B1 = eb.enter_context(nc.sbuf_tensor("rw_b1", [128, T + 1], F32))
            A1 = eb.enter_context(nc.sbuf_tensor("rw_a1", [128, T + 1], F32))
            KB = ctx.scr_f32[:, 0:T]
            VB = ctx.scr_f32[:, T:2 * T]
            RB = ctx.scr_f32[:, 2 * T:3 * T]
            XB = ctx.scr_f32[:, 3 * T:4 * T]
            S1 = ctx.scr_f32b[:, 0:T]
            S2 = ctx.scr_f32b[:, T:2 * T]
            with nc.Block() as block:
                @block.vector
                def _(vector):
                    vector.memset(ONE[:], 1.0)
                    vector.memset(ED[:, 0:1], 0.0)
                    vector.memset(B1[:, 0:1], -1e38)
                    vector.memset(A1[:, 0:1], 0.0)
                    vector.drain().then_inc(ctx.inc("vec"), 1)
            barrier_block(ctx)

            for ct in range(NFT):
                wcol = col(CW, ct)
                ucol = col(CU, ct)
                with nc.Block() as block:
                    @block.vector
                    def _(vector, wcol=wcol):
                        vector.tensor_scalar(WT[:], tau[:], wcol, zcol,
                                             ALU.mult, ALU.add)
                        vector.drain().then_inc(ctx.inc("vec"), 1)
                for b in range(B):
                    sl = slice(b * T, (b + 1) * T)
                    prog = []

                    def V(fn):
                        prog.append(("V", fn))

                    def S(fn):
                        prog.append(("S", fn))

                    def ts_mul(out, in_, c):
                        V(lambda e, out=out, in_=in_, c=c: e.tensor_scalar(
                            out, in_, c, zcol, ALU.mult, ALU.add))

                    def ts_addc(out, in_, c):
                        V(lambda e, out=out, in_=in_, c=c: e.tensor_scalar(
                            out, in_, c, zcol, ALU.add, ALU.add))

                    def tt(out, a, bb, op):
                        V(lambda e, out=out, a=a, bb=bb, op=op:
                          e.tensor_tensor(out, a, bb, op))

                    def sexp(x):
                        S(lambda e, x=x: e.activation(x, x, AF.Exp))

                    for tcol, ocol, buf in ((CTK, COTK, KB), (CTV, COTV, VB),
                                            (CTR, COTR, RB)):
                        tc, oc = col(tcol, ct), col(ocol, ct)
                        ts_mul(S3[:, 1:T], XB[:, 0:T - 1], oc)
                        ts_mul(S3[:, 0:1], XB[:, 0:1], oc)
                        ts_mul(buf, buf, tc)
                        tt(buf, buf, S3[:], ALU.add)
                    S(lambda e: e.activation(RB, RB, AF.Sigmoid))

                    tt(S1, KB, WT[:], ALU.subtract)
                    V(lambda e: e.tensor_tensor_scan(
                        P1[:], ONE[:], S1, -1e38, ALU.mult, ALU.max))
                    tt(ED[:, 1:T], P1[:, 0:T - 1], P1[:, 1:T], ALU.subtract)
                    S(lambda e: e.activation(ED[:, 1:T], ED[:, 1:T], AF.Exp))
                    tt(S2, S1, P1[:], ALU.subtract)
                    sexp(S2)
                    tt(P1[:], P1[:], WT[:], ALU.add)
                    V(lambda e: e.tensor_tensor_scan(
                        S3[:], ED[:], S2, 0.0, ALU.mult, ALU.add))
                    S(lambda e: e.activation(S3[:], S3[:], AF.Ln))
                    tt(B1[:, 1:T + 1], S3[:], P1[:], ALU.add)
                    ts_addc(S3[:], B1[:, 0:T], wcol)
                    tt(S3[:], S3[:], KB, ALU.max)
                    tt(S3[:], S3[:], P1[:], ALU.subtract)
                    sexp(S3[:])
                    V(lambda e: e.tensor_scalar_mul(S3[:], S3[:], 1e-8))
                    tt(S2, S2, S3[:], ALU.add)
                    V(lambda e: e.tensor_tensor_scan(
                        S3[:], ED[:], S2, 0.0, ALU.mult, ALU.add))
                    S(lambda e: e.activation(S3[:], S3[:], AF.Ln))
                    tt(B1[:, 1:T + 1], S3[:], P1[:], ALU.add)
                    ts_addc(S1, B1[:, 0:T], wcol)
                    tt(S1, S1, KB, ALU.max)
                    ts_addc(S2, B1[:, 0:T], wcol)
                    tt(S2, S2, S1, ALU.subtract)
                    sexp(S2)
                    tt(S3[:], KB, S1, ALU.subtract)
                    sexp(S3[:])
                    tt(S3[:], S3[:], VB, ALU.mult)
                    V(lambda e: e.tensor_tensor_scan(
                        A1[:, 1:T + 1], S2, S3[:], 0.0, ALU.mult, ALU.add))
                    ts_addc(KB, KB, ucol)
                    tt(S1, B1[:, 0:T], KB, ALU.max)
                    tt(S2, B1[:, 0:T], S1, ALU.subtract)
                    sexp(S2)
                    tt(KB, KB, S1, ALU.subtract)
                    S(lambda e: e.activation(KB, KB, AF.Exp))
                    tt(S1, S2, KB, ALU.add)
                    V(lambda e: e.tensor_scalar_add(S1, S1, 1e-8))
                    tt(S3[:], KB, VB, ALU.mult)
                    tt(S2, S2, A1[:, 0:T], ALU.mult)
                    tt(S2, S2, S3[:], ALU.add)
                    V(lambda e: e.reciprocal(S1, S1))
                    tt(VB, S2, S1, ALU.mult)
                    tt(VB, VB, RB, ALU.mult)
                    tt(VB, VB, XB, ALU.add)

                    nv_tot = sum(1 for t_, _ in prog if t_ == "V")
                    with nc.Block() as block:
                        d0 = ctx.val("dma")
                        v0 = ctx.val("vec")
                        s0 = ctx.val("scl")

                        @block.sync
                        def _(sync, ct=ct, sl=sl):
                            dma(ctx, sync, KB, kvr_d.ap()[ct, :, sl])
                            dma(ctx, sync, VB, kvr_d.ap()[NFT + ct, :, sl])
                            dma(ctx, sync, RB, kvr_d.ap()[2 * NFT + ct, :, sl])
                            dma(ctx, sync, XB, xb_d.ap()[ct, :, sl])

                        @block.vector
                        def _(vector, prog=prog, d0=d0, s0=s0):
                            vector.wait_ge(ctx.h("dma"), d0 + 16 * 4)
                            ns = 0
                            for eng, fn in prog:
                                if eng == "V":
                                    vector.wait_ge(ctx.h("scl"), s0 + ns)
                                    vector.drain()
                                    fn(vector).then_inc(ctx.inc("vec"), 1)
                                else:
                                    ns += 1

                        @block.scalar
                        def _(scalar, prog=prog, v0=v0, ct=ct, sl=sl,
                              nv_tot=nv_tot):
                            nv = 0
                            for eng, fn in prog:
                                if eng == "V":
                                    nv += 1
                                else:
                                    scalar.wait_ge(ctx.h("vec"), v0 + nv)
                                    fn(scalar).then_inc(ctx.inc("scl"), 1)
                            scalar.wait_ge(ctx.h("vec"), v0 + nv_tot)
                            dma(ctx, scalar, x1_d.ap()[ct, :, sl], VB,
                                sem="dmb")
                            scalar.wait_ge(ctx.h("dmb"), ctx.val("dmb"))
            barrier_block(ctx)

        es_xn = contextlib.ExitStack()
        xn_all = es_xn.enter_context(
            nc.sbuf_tensor("rw_xn2", [128, NFT * NTOK], BF16, side="right"))
        xn2 = [xn_all[:, i * NTOK:(i + 1) * NTOK] for i in range(NFT)]
        build_layernorm(ctx, x1_d, xn2, gain=g2c, bias=b2c)

        with contextlib.ExitStack() as ec:
            u32 = ec.enter_context(nc.sbuf_tensor("rw_u32", [128, 1024], F32))
            Pm = ctx.scr_f32[:, 0:NTOK]
            x1t = ctx.scr_f32[:, NTOK:2 * NTOK]
            rrb = ctx.scr_f32b[:, 0:NTOK]
            h2st = ctx.scr_bf[:, 0:NTOK]
            for ft in range(NFT):
                ckc, ock = col(CCK, ft), col(COCK, ft)
                crc, ocr = col(CCR, ft), col(COCR, ft)
                with nc.Block() as block:
                    d0 = ctx.val("dma")
                    v0 = ctx.val("vec")

                    @block.sync
                    def _(sync, ft=ft):
                        dma(ctx, sync, x1t, x1_d.ap()[ft])

                    @block.vector
                    def _(vector, ft=ft, ock=ock, ocr=ocr, crc=crc, d0=d0):
                        vector.wait_ge(ctx.h("dma"), d0 + 16)
                        for b in range(B):
                            o = b * T
                            vector.tensor_scalar(
                                Pm[:, o + 1:o + T], x1t[:, o:o + T - 1],
                                ock, zcol, ALU.mult, ALU.add)
                            vector.tensor_scalar(
                                Pm[:, o:o + 1], x1t[:, o + T - 1:o + T],
                                ock, zcol, ALU.mult, ALU.add)
                            vector.tensor_scalar(
                                rrb[:, o + 1:o + T], x1t[:, o:o + T - 1],
                                ocr, zcol, ALU.mult, ALU.add)
                            vector.tensor_scalar(
                                rrb[:, o:o + 1], x1t[:, o + T - 1:o + T],
                                ocr, zcol, ALU.mult, ALU.add)
                        vector.drain()
                        vector.scalar_tensor_tensor(
                            rrb[:], xn2[ft], crc, rrb[:], ALU.mult, ALU.add)
                        vector.drain().then_inc(ctx.inc("vec"), 1)

                    @block.scalar
                    def _(scalar, ft=ft, v0=v0):
                        scalar.wait_ge(ctx.h("vec"), v0 + 1)
                        scalar.activation(rrb[:], rrb[:],
                                          AF.Sigmoid).then_inc(ctx.inc("scl"), 1)
                        scalar.wait_ge(ctx.h("scl"), ctx.val("scl"))
                        dma(ctx, scalar, rr_d.ap()[ft], rrb[:], sem="dmb")
                        scalar.wait_ge(ctx.h("dmb"), ctx.val("dmb"))
                for j in range(4):
                    ht = ft + NFT * j

                    def evac_kk(o, ch, ps, eng, ckc=ckc):
                        c = slice(ch * 512, (ch + 1) * 512)
                        cu = slice((ch % 2) * 512, (ch % 2) * 512 + 512)
                        eng.scalar_tensor_tensor(u32[:, cu], ps[:], ckc,
                                                 Pm[:, c], ALU.mult, ALU.add)
                        eng.drain()
                        eng.tensor_scalar_max(u32[:, cu], u32[:, cu], 0.0)
                        eng.drain()
                        return eng.tensor_tensor(h2st[:, c], u32[:, cu],
                                                 u32[:, cu], ALU.mult)
                    build_proj(ctx, xn2, W["L1_attn_b_W_cm"], 1, evac_kk,
                               of0=ht, evac_eng="vec")
                    with nc.Block() as block:
                        @block.scalar
                        def _(scalar, ht=ht):
                            dma(ctx, scalar, h2_d.ap()[ht], h2st, sem="dmb")
                            scalar.wait_ge(ctx.h("dmb"), ctx.val("dmb"))
        es_xn.close()

        with contextlib.ExitStack() as e2:
            hblk = e2.enter_context(
                nc.sbuf_tensor("rw_hblk", [128, 32 * 512], BF16, side="right"))
            cw = e2.enter_context(nc.sbuf_tensor("rw_cw", [128, 2 * 32 * 128],
                                                 BF16))
            wfst = e2.enter_context(nc.sbuf_tensor("rw_wf", [128, 2 * 128], F32))
            p1 = e2.enter_context(nc.psum_tensor("rw_p1", [128, 512], F32))
            p2 = e2.enter_context(nc.psum_tensor("rw_p2", [128, 512], F32))
            xf = ctx.scr_f32[:, 0:1024]
            x1f = ctx.scr_f32[:, 1024:2048]
            rrf = ctx.scr_f32[:, 2048:3072]
            of_out = ctx.scr_f32[:, 3072:4096]
            for ch in range(NTOK // 512):
                with nc.Block() as block:
                    q0 = [ctx.val("dq0"), ctx.val("dq1")]
                    b0 = ctx.val("dmg")
                    g0 = ctx.val("gps")
                    p0 = ctx.val("pe")
                    v0 = ctx.val("vec")
                    d0 = ctx.val("dma")

                    @block.sync
                    def _(sync, ch=ch):
                        for ht in range(32):
                            src = bass.AP(h2_d.ap().tensor,
                                          ht * 128 * NTOK + ch * 512,
                                          [[NTOK, 128], [1, 512]])
                            dma(ctx, sync, hblk[:, ht * 512:(ht + 1) * 512], src)
                        n = 0
                        for of in range(NFT):
                            for it in range(32):
                                if n >= 2:
                                    sync.wait_ge(ctx.h("gps"), g0 + n - 1)
                                dma(ctx, sync,
                                    wfst[:, (n % 2) * 128:(n % 2) * 128 + 128],
                                    W["L1_attn_b_W_cp"][it * 128:(it + 1) * 128,
                                                        of * 128:(of + 1) * 128],
                                    sem=f"dq{n % 2}")
                                n += 1

                    @block.scalar
                    def _(scalar, ch=ch):
                        n = 0
                        for of in range(NFT):
                            for it in range(32):
                                scalar.wait_ge(ctx.h(f"dq{n % 2}"),
                                               q0[n % 2] + 16 * (n // 2 + 1))
                                if of >= 2 and it == 0:
                                    scalar.wait_ge(ctx.h("pe"), p0 + of - 1)
                                half = (of % 2) * 32
                                scalar.activation(
                                    cw[:, (half + it) * 128:(half + it + 1) * 128],
                                    wfst[:, (n % 2) * 128:(n % 2) * 128 + 128],
                                    AF.Copy).then_inc(ctx.inc("gps"), 1)
                                n += 1

                    @block.gpsimd
                    def _(gpsimd, ch=ch):
                        for of in range(NFT):
                            gpsimd.wait_ge(ctx.h("vec"), v0 + max(of - 1, 0))
                            c2 = slice((of % 2) * 512, (of % 2) * 512 + 512)
                            dma(ctx, gpsimd, xf[:, c2],
                                xres.ap()[of, :, ch * 512:(ch + 1) * 512],
                                sem="dmg")
                            dma(ctx, gpsimd, x1f[:, c2],
                                x1_d.ap()[of, :, ch * 512:(ch + 1) * 512],
                                sem="dmg")
                            dma(ctx, gpsimd, rrf[:, c2],
                                rr_d.ap()[of, :, ch * 512:(ch + 1) * 512],
                                sem="dmg")
                            gpsimd.wait_ge(ctx.h("vec"), v0 + of + 1)
                            dma(ctx, gpsimd,
                                xres.ap()[of, :, ch * 512:(ch + 1) * 512],
                                of_out[:, c2], sem="dmg")

                    @block.tensor
                    def _(tensor, ch=ch, d0=d0):
                        tensor.wait_ge(ctx.h("dma"), d0 + 16 * 32)
                        for of in range(NFT):
                            ps = p1 if of % 2 == 0 else p2
                            tensor.wait_ge(ctx.h("gps"), g0 + 32 * (of + 1))
                            if of >= 2:
                                tensor.wait_ge(ctx.h("vec"), v0 + of - 1)
                            half = (of % 2) * 32
                            for it in range(32):
                                i = tensor.matmul(
                                    ps[:],
                                    cw[:, (half + it) * 128:(half + it + 1) * 128],
                                    hblk[:, it * 512:(it + 1) * 512],
                                    start=(it == 0), stop=(it == 31))
                            i.then_inc(ctx.inc("pe"), 1)

                    @block.vector
                    def _(vector, ch=ch, b0=b0):
                        for of in range(NFT):
                            ps = p1 if of % 2 == 0 else p2
                            vector.wait_ge(ctx.h("pe"), p0 + of + 1)
                            vector.wait_ge(ctx.h("dmg"), b0 + 16 * (4 * of + 3))
                            c2 = slice((of % 2) * 512, (of % 2) * 512 + 512)
                            vector.tensor_tensor(of_out[:, c2], ps[:],
                                                 rrf[:, c2], ALU.mult)
                            vector.drain()
                            vector.tensor_add(of_out[:, c2], of_out[:, c2],
                                              xf[:, c2])
                            vector.drain()
                            vector.tensor_add(of_out[:, c2], of_out[:, c2],
                                              x1f[:, c2]).then_inc(
                                ctx.inc("vec"), 1)
                        wait_all(ctx, vector, "dmg")
    barrier_block(ctx)


class _UniqueNC:

    def __init__(self, nc):
        self._nc = nc
        self._n = 0

    def __getattr__(self, a):
        return getattr(self._nc, a)

    def sbuf_tensor(self, name, *a, **kw):
        self._n += 1
        return self._nc.sbuf_tensor(f"{name}_{self._n}", *a, **kw)

    def psum_tensor(self, name, *a, **kw):
        self._n += 1
        return self._nc.psum_tensor(f"{name}_{self._n}", *a, **kw)


def build_program(upto=99, debug=False):
    nc0 = bass.Bass("TRN2", target_bir_lowering=False,
                    detect_race_conditions=False)
    nc = _UniqueNC(nc0)
    ctx = Ctx(nc)
    ctx.debug = debug
    for s in ("dma", "dmb", "dmg", "pe", "vec", "scl", "gps", "dq0", "dq1"):
        ctx.new_sem(s)

    xin = nc.dram_tensor("xin", [B, D, T], F32, kind="ExternalInput").ap()
    xout = nc.dram_tensor("xout", [B, D, T], F32, kind="ExternalOutput").ap()
    xres = nc.dram_tensor("xres", [NFT, 128, NTOK], F32)
    W = make_weights_decl(nc)

    def glob(name, shape, dtype):
        hh = nc.sbuf_tensor(name, shape, dtype)
        tt = hh.__enter__()
        ctx._stack.append(hh)
        return tt
    ctx.ones_col = glob("c_ones_col", [128, 1], BF16)
    ctx.ones_row = glob("c_ones_row", [1, 128], F32)
    ctx.ident = glob("c_ident", [128, 128], BF16)
    ctx.utmask = glob("c_utmask", [64, 64], BF16)
    ctx.rep32 = glob("c_rep32", [32, 128], BF16)
    ctx.selhi = glob("c_selhi", [64, 128], BF16)
    ctx.ef_const = glob("c_ef", [16, NFT * 128], BF16)
    ctx.scr_f32 = glob("c_scrf32", [128, 2 * NTOK], F32)
    ctx.scr_f32b = glob("c_scrf32b", [128, NTOK], F32)
    ctx.scr_bf = glob("c_scrbf", [128, 2 * NTOK], BF16)
    ctx.scr_xr = glob("c_scrxr", [128, 2 * NTOK], BF16)
    ctx.stat = glob("c_stat", [1, 2 * T], F32)
    ctx.epscol = glob("c_epscol", [1, 1], F32)

    with nc.Block() as block:
        @block.sync
        def _(sync):
            dma(ctx, sync, ctx.ident[:], W["const_ident"][:])
            dma(ctx, sync, ctx.utmask[:], W["const_utmask"][:])
            dma(ctx, sync, ctx.rep32[:], W["const_rep32"][:])
            dma(ctx, sync, ctx.selhi[:], W["const_selhi"][:])
            dma(ctx, sync, ctx.ef_const[:], W["const_ef"][:])

        @block.vector
        def _(vector):
            vector.memset(ctx.ones_col[:], 1.0)
            vector.memset(ctx.ones_row[:], 1.0)
            vector.memset(ctx.epscol[:], LN_EPS)

    with nc.Block() as block:
        @block.sync
        def _(sync):
            for b in range(B):
                for ft in range(NFT):
                    dma(ctx, sync, xres.ap()[ft, :, b * T:(b + 1) * T],
                        xin[b, ft * 128:(ft + 1) * 128, :])
            wait_all(ctx, sync, "dma")

    if 1 <= upto < 20:
        build_gla(ctx, xres, W)
    if 1.5 <= upto < 20:
        build_ffn0_prep(ctx, W)
    if 1.8 <= upto < 20:
        build_ffn0(ctx, xres, W, parts=(2 if upto >= 2 else 1))
    if upto == 30:
        xn_dram = nc.dram_tensor("xn_scr", [NFT, 128, NTOK], BF16)
        scr2 = nc.dram_tensor("scr2", [B, 16, 16, 64, 128], BF16)
        build_attnb(ctx, xres, W, xn_dram, scr2)
    if 2.2 <= upto < 30:
        xn_dram = nc.dram_tensor("xn_scr", [NFT, 128, NTOK], BF16)
        scr2 = nc.dram_tensor("scr2", [B, 16, 16, 64, 128], BF16)
        build_attnb(ctx, xres, W, xn_dram, scr2,
                    parts=(4 if upto >= 3 else int((upto - 2) * 10 + 0.5) // 2))
    if 4 <= upto < 20:
        build_ffn0(ctx, xres, W)
    if (5 <= upto < 20) or upto == 31:
        qraw_d = nc.dram_tensor("aa_qraw", [NFT, 128, NTOK], BF16)
        kraw_d = nc.dram_tensor("aa_kraw", [NFT, 128, NTOK], BF16)
        vraw_d = nc.dram_tensor("aa_vraw", [NFT, 128, NTOK], BF16)
        ks_d = nc.dram_tensor("aa_ksd", [1, 2 * D], F32)
        build_attna(ctx, xres, W, qraw_d, kraw_d, vraw_d, ks_d)
    if (5.5 <= upto < 20) or upto == 32:
        ctx.f1h_dram = nc.dram_tensor("f1_h", [32, 128, NTOK], BF16)
        build_ffn1_prep(ctx, W)
    if (6 <= upto < 20) or upto == 32:
        build_ffn1(ctx, xres, W, ctx.f1h_dram)
    if (7 <= upto < 20) or upto == 33:
        if upto == 33:
            ctx.f1h_dram = nc.dram_tensor("f1_h", [32, 128, NTOK], BF16)
        build_rwkv(ctx, xres, W, ctx.f1h_dram)
    if 8 <= upto < 20:
        build_ffn1(ctx, xres, W, ctx.f1h_dram)

    with nc.Block() as block:
        @block.sync
        def _(sync):
            for b in range(B):
                for ft in range(NFT):
                    dma(ctx, sync, xout[b, ft * 128:(ft + 1) * 128, :],
                        xres.ap()[ft, :, b * T:(b + 1) * T])
            wait_all(ctx, sync, "dma")
    return nc


def make_in_maps(inputs, upto=99):
    x = np.asarray(inputs["L0_attn_a_hidden_state"], dtype=np.float32)
    shared = {}
    for n in ("L0_attn_a_Wq", "L0_attn_a_bq", "L0_attn_a_Wk", "L0_attn_a_bk",
              "L0_attn_a_Wg", "L0_attn_a_bg", "L0_attn_a_Wv", "L0_attn_a_bv",
              "L0_attn_a_Wo", "L0_attn_a_bo"):
        a = np.asarray(inputs[n], dtype=np.float32)
        shared[n] = a.reshape(-1, 1) if a.ndim == 1 else np.ascontiguousarray(a)
    shared["L0_ffn_kernel"] = np.ascontiguousarray(np.asarray(inputs["L0_ffn_kernel"]))
    shared["L0_ffn_input_scale"] = np.asarray(inputs["L0_ffn_input_scale"], dtype=np.float32).reshape(1, 1)
    shared["L0_ffn_kernel_scale"] = np.asarray(inputs["L0_ffn_kernel_scale"], dtype=np.float32).reshape(1, 1)
    shared["L0_ffn_bias"] = np.ascontiguousarray(np.asarray(inputs["L0_ffn_bias"]).reshape(-1, 1))
    for n in ("L0_attn_b_Wq", "L0_attn_b_bq", "L0_attn_b_Wk", "L0_attn_b_bk",
              "L0_attn_b_Wv", "L0_attn_b_bv", "L0_attn_b_Wo", "L0_attn_b_bo"):
        a = np.asarray(inputs[n], dtype=np.float32)
        shared[n] = a.reshape(-1, 1) if a.ndim == 1 else np.ascontiguousarray(a)
    shared["L0_attn_b_key_param"] = np.ascontiguousarray(
        np.asarray(inputs["L0_attn_b_key_param"], dtype=np.float32)[0].T)
    shared["const_krow"] = np.ascontiguousarray(
        np.tile(np.arange(1, 33, dtype=np.float32)[None, :], (128, 1)))
    shared["const_rep32"] = np.ascontiguousarray(
        np.tile(np.eye(32, dtype=ml_dtypes.bfloat16), (1, 4)))
    selhi = np.zeros((64, 128), dtype=ml_dtypes.bfloat16)
    for dd in range(64):
        selhi[dd, 64 + dd] = 1
    shared["const_selhi"] = selhi
    ef = np.zeros((16, 8 * 128), dtype=ml_dtypes.bfloat16)
    for f in range(8):
        for p in range(128):
            ef[2 * f + p // 64, f * 128 + p] = 1
    shared["const_ef"] = ef
    for n in ("L1_attn_a_Wq", "L1_attn_a_Wk", "L1_attn_a_Wv", "L1_attn_a_Wo",
              "L1_attn_a_bo", "L1_ffn_w1", "L1_ffn_b1", "L1_ffn_w2", "L1_ffn_b2"):
        a = np.asarray(inputs[n], dtype=np.float32)
        shared[n] = a.reshape(-1, 1) if a.ndim == 1 else np.ascontiguousarray(a)
    shared["L1_ffn_w1T"] = np.ascontiguousarray(
        np.asarray(inputs["L1_ffn_w1"], dtype=np.float32).T)
    shared["L1_ffn_w2T"] = np.ascontiguousarray(
        np.asarray(inputs["L1_ffn_w2"], dtype=np.float32).T)
    shared["const_ident"] = np.eye(128, dtype=ml_dtypes.bfloat16)
    shared["const_utmask"] = np.ascontiguousarray(
        np.triu(np.ones((64, 64), dtype=ml_dtypes.bfloat16)))
    for n in ("L1_attn_b_time_decay", "L1_attn_b_time_first",
              "L1_attn_b_g1", "L1_attn_b_b1", "L1_attn_b_g2", "L1_attn_b_b2",
              "L1_attn_b_tmk", "L1_attn_b_tmv", "L1_attn_b_tmr",
              "L1_attn_b_cmk", "L1_attn_b_cmr"):
        shared[n] = np.ascontiguousarray(
            np.asarray(inputs[n], dtype=np.float32).reshape(-1, 1))
    for n in ("L1_attn_b_W_tm", "L1_attn_b_W_cm", "L1_attn_b_W_cp"):
        shared[n] = np.ascontiguousarray(np.asarray(inputs[n], dtype=np.float32))
    shared["const_tau"] = np.ascontiguousarray(
        np.tile(np.arange(T, dtype=np.float32)[None, :], (128, 1)))
    maps = []
    for c in range(8):
        m = dict(shared)
        m["xin"] = np.ascontiguousarray(x[B * c:B * (c + 1)].transpose(0, 2, 1))
        maps.append(m)
    return maps


def kernel(**inputs):
    nc = build_program(upto=10)
    maps = make_in_maps(inputs)
    res = run_bass_kernel_spmd(nc, maps, core_ids=list(range(8)))
    out = np.empty((16, T, D), dtype=np.float32)
    for c in range(8):
        out[B * c:B * (c + 1)] = res.results[c]["xout"].transpose(0, 2, 1)
    return out


# version 02
# speedup vs baseline: 1.3475x; 1.3475x over previous
import contextlib
import numpy as np
import ml_dtypes
import concourse.bass as bass
import concourse.mybir as mybir
from concourse.bass_utils import run_bass_kernel_spmd

F32 = mybir.dt.float32
BF16 = mybir.dt.bfloat16
FP8 = mybir.dt.float8e4
AF = mybir.ActivationFunctionType
ALU = mybir.AluOpType

B = 2
T = 2048
NTOK = B * T
D = 1024
NFT = D // 128
CHUNK = 64
NCH = T // CHUNK
LN_EPS = 1e-5
S_PAT = 32


class Ctx:
    def __init__(self, nc):
        self.nc = nc
        self.sem = {}
        self._stack = []

    def new_sem(self, name):
        h = self.nc.semaphore(name)
        handle = h.__enter__()
        self._stack.append(h)
        self.sem[name] = [handle, 0]

    def inc(self, name, by=1):
        s = self.sem[name]
        s[1] += by
        return s[0]

    def val(self, name):
        return self.sem[name][1]

    def h(self, name):
        return self.sem[name][0]

    def _stack_append(self, h):
        t = h.__enter__()
        self._stack.append(h)
        return t


def dma(ctx, q, out, in_, sem="dma"):
    q.dma_start(out=out, in_=in_).then_inc(ctx.inc(sem, 16), 16)
    return ctx.val(sem)


def wait_all(ctx, eng, *names):
    for n in names:
        eng.wait_ge(ctx.h(n), ctx.val(n))


def debug_dump(ctx, name, ap):
    if not getattr(ctx, "debug", False):
        return
    nc = ctx.nc
    shp = [p[1] for p in ap.ap]
    t = nc._nc.dram_tensor(f"dbg_{name}", shp, ap.tensor.dtype if hasattr(ap.tensor, 'dtype') else ap.dtype,
                           kind="ExternalOutput").ap()
    with nc.Block() as block:
        @block.sync
        def _(sync):
            wait_all(ctx, sync, "dma", "pe", "vec", "scl", "gps")
            dma(ctx, sync, t[:], ap)
            sync.wait_ge(ctx.h("dma"), ctx.val("dma"))


def barrier_block(ctx):
    nc = ctx.nc
    with nc.Block() as block:
        @block.sync
        def _(sync):
            wait_all(ctx, sync, "dma", "dmb", "dmg", "pe", "vec", "scl", "gps",
                     "dq0", "dq1")


def build_layernorm(ctx, src_dram, xn_tiles, gain=None, bias=None,
                    keep=None, xn_dram=None, fp32_dram=None):
    nc = ctx.nc
    xf = ctx.scr_f32[:, 0:NTOK]
    r_bc = ctx.scr_f32[:, NTOK:2 * NTOK]
    mr_bc = ctx.scr_f32b[:, 0:NTOK]
    stat = ctx.stat
    es_u = contextlib.ExitStack()
    if gain is not None:
        u_bc = es_u.enter_context(nc.sbuf_tensor("ln_u", [128, NTOK], F32))
    if fp32_dram is not None:
        x32buf = es_u.enter_context(nc.sbuf_tensor("ln_x32", [128, NTOK], F32))

    for half in range(2):
        hs = slice(half * T, (half + 1) * T)
        with (
            nc.psum_tensor("ln_ps1", [1, T], F32) as ps1,
            nc.psum_tensor("ln_ps2", [1, T], F32) as ps2,
            nc.Block() as block,
        ):
            q0 = [ctx.val("dq0"), ctx.val("dq1")]
            g0 = ctx.val("gps")
            s0 = ctx.val("scl")
            p0 = ctx.val("pe")
            xbf = ctx.scr_bf[:, 0:2 * T]
            sqbf = ctx.scr_bf[:, 2 * T:4 * T]

            @block.sync
            def _(sync):
                for ft in range(NFT):
                    if ft >= 2:
                        sync.wait_ge(ctx.h("gps"), g0 + ft - 1)
                        sync.wait_ge(ctx.h("scl"), s0 + ft - 1)
                    dma(ctx, sync, xf[:, (ft % 2) * T:(ft % 2) * T + T],
                        src_dram.ap()[ft, :, hs], sem=f"dq{ft % 2}")

            @block.vector
            def _(vector):
                for ft in range(NFT):
                    vector.wait_ge(ctx.h(f"dq{ft % 2}"), q0[ft % 2] + 16 * (ft // 2 + 1))
                    if ft >= 2:
                        vector.wait_ge(ctx.h("pe"), p0 + ft - 1)
                    vector.tensor_copy(xbf[:, (ft % 2) * T:(ft % 2) * T + T],
                                       xf[:, (ft % 2) * T:(ft % 2) * T + T]).then_inc(
                        ctx.inc("gps"), 1)

            @block.scalar
            def _(scalar):
                for ft in range(NFT):
                    scalar.wait_ge(ctx.h(f"dq{ft % 2}"), q0[ft % 2] + 16 * (ft // 2 + 1))
                    if ft >= 2:
                        scalar.wait_ge(ctx.h("pe"), p0 + ft - 1)
                    scalar.activation(sqbf[:, (ft % 2) * T:(ft % 2) * T + T],
                                      xf[:, (ft % 2) * T:(ft % 2) * T + T],
                                      AF.Square).then_inc(ctx.inc("scl"), 1)

            @block.tensor
            def _(tensor):
                for ft in range(NFT):
                    tensor.wait_ge(ctx.h("gps"), g0 + ft + 1)
                    tensor.wait_ge(ctx.h("scl"), s0 + ft + 1)
                    o = (ft % 2) * T
                    for ch in range(T // 512):
                        c = slice(ch * 512, (ch + 1) * 512)
                        co = slice(o + ch * 512, o + (ch + 1) * 512)
                        tensor.matmul(ps1[:, c], ctx.ones_col[:], xbf[:, co],
                                      start=(ft == 0), stop=(ft == NFT - 1))
                        i = tensor.matmul(ps2[:, c], ctx.ones_col[:], sqbf[:, co],
                                          start=(ft == 0), stop=(ft == NFT - 1))
                    i.then_inc(ctx.inc("pe"), 1)

            @block.vector
            def _(vector):
                vector.wait_ge(ctx.h("pe"), ctx.val("pe"))
                m = stat[:, 0:T]
                q = stat[:, T:2 * T]
                vector.tensor_scalar_mul(m, ps1[:], 1.0 / D)
                vector.tensor_scalar_mul(q, ps2[:], 1.0 / D)
                vector.drain()
                vector.tensor_mul(ps1[:], m, m)
                vector.drain()
                vector.tensor_sub(q, q, ps1[:]).then_inc(ctx.inc("vec"), 1)

            @block.scalar
            def _(scalar):
                scalar.wait_ge(ctx.h("vec"), ctx.val("vec"))
                scalar.activation(stat[:, T:2 * T], stat[:, T:2 * T],
                                  AF.Sqrt, bias=ctx.epscol[:]).then_inc(ctx.inc("scl"), 1)

            @block.vector
            def _(vector):
                vector.wait_ge(ctx.h("scl"), ctx.val("scl"))
                vector.reciprocal(stat[:, T:2 * T], stat[:, T:2 * T])
                vector.drain()
                vector.tensor_mul(stat[:, 0:T], stat[:, 0:T],
                                  stat[:, T:2 * T]).then_inc(ctx.inc("vec"), 1)

        with (
            nc.psum_tensor("ln_pb1", [128, T], F32) as pb1,
            nc.psum_tensor("ln_pb2", [128, T], F32) as pb2,
            nc.Block() as block,
        ):
            pp0 = ctx.val("pe")

            @block.tensor
            def _(tensor):
                tensor.wait_ge(ctx.h("vec"), ctx.val("vec"))
                for ch in range(T // 512):
                    c = slice(ch * 512, (ch + 1) * 512)
                    tensor.matmul(pb1[:, c], ctx.ones_row[:],
                                  stat[:, T + ch * 512:T + (ch + 1) * 512],
                                  start=True, stop=True)
                    i = tensor.matmul(pb2[:, c], ctx.ones_row[:],
                                      stat[:, ch * 512:(ch + 1) * 512],
                                      start=True, stop=True)
                i.then_inc(ctx.inc("pe"), 1)

            @block.vector
            def _(vector):
                vector.wait_ge(ctx.h("pe"), pp0 + 1)
                vector.tensor_copy(r_bc[:, hs], pb1[:])
                vector.tensor_copy(mr_bc[:, hs], pb2[:]).then_inc(ctx.inc("vec"), 1)

    with nc.Block() as block:
        d0 = ctx.val("dma")
        b0 = ctx.val("dmb")
        v0 = ctx.val("vec")

        @block.sync
        def _(sync):
            for ft in range(NFT):
                if ft >= 1:
                    sync.wait_ge(ctx.h("vec"), v0 + 2 * ft - 1)
                dma(ctx, sync, xf, src_dram.ap()[ft])

        v0 = ctx.val("vec")

        @block.vector
        def _(vector):
            for ft in range(NFT):
                vector.wait_ge(ctx.h("dma"), d0 + 16 * (ft + 1))
                nst = (1 if xn_dram is not None else 0) + \
                      (1 if fp32_dram is not None else 0)
                if nst and ft >= 1:
                    vector.wait_ge(ctx.h("dmb"), b0 + 16 * nst * ft)
                vector.tensor_mul(ctx.scr_xr[:, 0:NTOK], xf, r_bc).then_inc(
                    ctx.inc("vec"), 1)
                if gain is None:
                    if fp32_dram is not None:
                        vector.tensor_sub(x32buf[:], ctx.scr_xr[:, 0:NTOK], mr_bc)
                        vector.drain()
                        vector.tensor_copy(xn_tiles[ft][:], x32buf[:]).then_inc(
                            ctx.inc("vec"), 1)
                    else:
                        vector.tensor_sub(xn_tiles[ft][:], ctx.scr_xr[:, 0:NTOK],
                                          mr_bc).then_inc(ctx.inc("vec"), 1)
                else:
                    vector.tensor_scalar(u_bc[:], mr_bc, gain[ft], bias[ft],
                                         ALU.mult, ALU.subtract)
                    vector.scalar_tensor_tensor(
                        xn_tiles[ft][:], ctx.scr_xr[:, 0:NTOK], gain[ft], u_bc[:],
                        ALU.mult, ALU.subtract).then_inc(ctx.inc("vec"), 1)

        if xn_dram is not None or fp32_dram is not None:
            @block.scalar
            def _(scalar):
                for ft in range(NFT):
                    scalar.wait_ge(ctx.h("vec"), v0 + 2 * (ft + 1))
                    if xn_dram is not None:
                        dma(ctx, scalar, xn_dram.ap()[ft], xn_tiles[ft][:],
                            sem="dmb")
                    if fp32_dram is not None:
                        dma(ctx, scalar, fp32_dram.ap()[ft], x32buf[:],
                            sem="dmb")

    es_u.close()
    barrier_block(ctx)


def build_proj(ctx, xn_tiles, w_dram, out_nf, evac, nin_tiles=NFT,
               of0=0, evac_eng="vec", ntok=NTOK, conv_func=None):
    nc = ctx.nc
    nch = ntok // 512
    with (
        nc.sbuf_tensor("pj_wf", [128, 2 * 128], F32) as wf,
        nc.sbuf_tensor("pj_w", [128, 2 * nin_tiles * 128], BF16) as wb,
        nc.psum_tensor("pj_p1", [128, 512], F32) as p1,
        nc.psum_tensor("pj_p2", [128, 512], F32) as p2,
        nc.Block() as block,
    ):
        q0 = [ctx.val("dq0"), ctx.val("dq1")]
        g0 = ctx.val("gps")
        p0 = ctx.val("pe")
        e0 = ctx.val(evac_eng)

        @block.sync
        def _(sync):
            n = 0
            for of in range(out_nf):
                for it in range(nin_tiles):
                    if n >= 2:
                        sync.wait_ge(ctx.h("gps"), g0 + n - 1)
                    dma(ctx, sync, wf[:, (n % 2) * 128:(n % 2) * 128 + 128],
                        w_dram[it * 128:(it + 1) * 128,
                               (of0 + of) * 128:(of0 + of + 1) * 128],
                        sem=f"dq{n % 2}")
                    n += 1

        @block.scalar
        def _(scalar):
            n = 0
            for of in range(out_nf):
                for it in range(nin_tiles):
                    scalar.wait_ge(ctx.h(f"dq{n % 2}"), q0[n % 2] + 16 * (n // 2 + 1))
                    if of >= 2 and it == 0:
                        scalar.wait_ge(ctx.h("pe"), p0 + (of - 1) * nch)
                    half = (of % 2) * nin_tiles
                    scalar.activation(
                        wb[:, (half + it) * 128:(half + it + 1) * 128],
                        wf[:, (n % 2) * 128:(n % 2) * 128 + 128],
                        conv_func if conv_func is not None else AF.Copy).then_inc(
                        ctx.inc("gps"), 1)
                    n += 1

        @block.tensor
        def _(tensor):
            n = 0
            for of in range(out_nf):
                tensor.wait_ge(ctx.h("gps"), g0 + (of + 1) * nin_tiles)
                half = (of % 2) * nin_tiles
                for ch in range(nch):
                    ps = p1 if n % 2 == 0 else p2
                    if n >= 2:
                        tensor.wait_ge(ctx.h(evac_eng), e0 + n - 1)
                    for it in range(nin_tiles):
                        i = tensor.matmul(
                            ps[:], wb[:, (half + it) * 128:(half + it + 1) * 128],
                            xn_tiles[it][:, ch * 512:(ch + 1) * 512],
                            start=(it == 0), stop=(it == nin_tiles - 1))
                    i.then_inc(ctx.inc("pe"), 1)
                    n += 1

        def emit_evac(eng):
            n = 0
            for of in range(out_nf):
                for ch in range(nch):
                    ps = p1 if n % 2 == 0 else p2
                    eng.wait_ge(ctx.h("pe"), p0 + n + 1)
                    i = evac(of, ch, ps, eng)
                    i.then_inc(ctx.inc(evac_eng), 1)
                    n += 1

        if evac_eng == "vec":
            @block.vector
            def _(vector):
                emit_evac(vector)
        else:
            @block.scalar
            def _(scalar):
                emit_evac(scalar)

    barrier_block(ctx)



def make_weights_decl(nc):
    W = {}
    SP = {
        "L0_attn_a_Wq": ([D, 256], F32), "L0_attn_a_bq": ([256, 1], F32),
        "L0_attn_a_Wk": ([D, 256], F32), "L0_attn_a_bk": ([256, 1], F32),
        "L0_attn_a_Wg": ([D, 256], F32), "L0_attn_a_bg": ([256, 1], F32),
        "L0_attn_a_Wv": ([D, 256], F32), "L0_attn_a_bv": ([256, 1], F32),
        "L0_attn_a_Wo": ([256, D], F32), "L0_attn_a_bo": ([D, 1], F32),
        "const_ident": ([128, 128], BF16),
        "const_utmask": ([64, 64], BF16),
        "L0_attn_b_key_param": ([D, S_PAT], F32),
        "L0_attn_b_Wq": ([D, D], F32), "L0_attn_b_bq": ([D, 1], F32),
        "L0_attn_b_Wk": ([D, D], F32), "L0_attn_b_bk": ([D, 1], F32),
        "L0_attn_b_Wv": ([D, D], F32), "L0_attn_b_bv": ([D, 1], F32),
        "L0_attn_b_Wo": ([D, D], F32), "L0_attn_b_bo": ([D, 1], F32),
        "const_krow": ([128, 32], F32),
        "const_selhi": ([64, 128], BF16),
        "const_ef": ([16, NFT * 128], BF16),
        "L1_attn_a_Wq": ([D, D], F32), "L1_attn_a_Wk": ([D, D], F32),
        "L1_attn_a_Wv": ([D, D], F32), "L1_attn_a_Wo": ([D, D], F32),
        "L1_attn_a_bo": ([D, 1], F32),
        "L1_ffn_w1": ([4 * D, D], F32), "L1_ffn_w1T": ([D, 4 * D], F32),
        "L1_ffn_b1": ([4 * D, 1], F32),
        "L1_ffn_w2": ([D, 4 * D], F32), "L1_ffn_w2T": ([4 * D, D], F32),
        "L1_ffn_b2": ([D, 1], F32),
        "const_rep32": ([32, 128], BF16),
        "const_tau": ([128, T], F32),
        "L1_attn_b_time_decay": ([D, 1], F32),
        "L1_attn_b_time_first": ([D, 1], F32),
        "L1_attn_b_W_tm": ([D, 3 * D], F32),
        "L1_attn_b_g1": ([D, 1], F32), "L1_attn_b_b1": ([D, 1], F32),
        "L1_attn_b_tmk": ([D, 1], F32), "L1_attn_b_tmv": ([D, 1], F32),
        "L1_attn_b_tmr": ([D, 1], F32),
        "L1_attn_b_W_cm": ([D, 4 * D], F32),
        "L1_attn_b_W_cp": ([4 * D, D], F32),
        "L1_attn_b_g2": ([D, 1], F32), "L1_attn_b_b2": ([D, 1], F32),
        "L1_attn_b_cmk": ([D, 1], F32), "L1_attn_b_cmr": ([D, 1], F32),
        "L0_ffn_kernel": ([D, D], BF16),
        "L0_ffn_input_scale": ([1, 1], F32),
        "L0_ffn_kernel_scale": ([1, 1], F32),
        "L0_ffn_bias": ([D, 1], BF16),
    }
    for n, (sh, dt) in SP.items():
        W[n] = nc.dram_tensor(n, sh, dt, kind="ExternalInput").ap()
    return W


def build_gla(ctx, xres, W):
    nc = ctx.nc
    KD = 256
    NKT = 2

    with contextlib.ExitStack() as es:
        es_xn = contextlib.ExitStack()
        xn_all = es_xn.enter_context(
            nc.sbuf_tensor("g_xn", [128, NFT * NTOK], BF16, side="right"))
        xn = [xn_all[:, i * NTOK:(i + 1) * NTOK] for i in range(NFT)]
        build_layernorm(ctx, xres, xn)
        debug_dump(ctx, "xn0", xn[0])
        q_t = es.enter_context(nc.sbuf_tensor("g_q", [128, NKT * NTOK], BF16))
        k_t = es.enter_context(nc.sbuf_tensor("g_k", [128, NKT * NTOK], BF16))
        v_t = ctx.scr_xr[:, 0:NKT * NTOK]
        lg_t = ctx.scr_f32b[:, 0:NTOK]
        gbias = es.enter_context(nc.sbuf_tensor("g_bias", [128, 4 * NKT], F32))

        with nc.Block() as block:
            @block.sync
            def _(sync):
                for wi, bn in enumerate(["L0_attn_a_bq", "L0_attn_a_bk",
                                         "L0_attn_a_bg", "L0_attn_a_bv"]):
                    for kt in range(NKT):
                        dma(ctx, sync, gbias[:, wi * NKT + kt:wi * NKT + kt + 1],
                            W[bn][kt * 128:(kt + 1) * 128, :])
                wait_all(ctx, sync, "dma")


        def evac_q(of, ch, ps, eng):
            sl = (slice(None), slice(of * NTOK + ch * 512, of * NTOK + (ch + 1) * 512))
            return eng.tensor_scalar_add(q_t[sl], ps[:],
                                         gbias[:, 0 * NKT + of:0 * NKT + of + 1])
        build_proj(ctx, xn, W["L0_attn_a_Wq"], NKT, evac_q, evac_eng="vec")

        def evac_k(of, ch, ps, eng):
            sl = (slice(None), slice(of * NTOK + ch * 512, of * NTOK + (ch + 1) * 512))
            return eng.activation(k_t[sl], ps[:], AF.Sigmoid,
                                  bias=gbias[:, 1 * NKT + of:1 * NKT + of + 1])
        build_proj(ctx, xn, W["L0_attn_a_Wk"], NKT, evac_k, evac_eng="scl")

        def evac_v(of, ch, ps, eng):
            sl = (slice(None), slice(of * NTOK + ch * 512, of * NTOK + (ch + 1) * 512))
            return eng.tensor_scalar_add(v_t[sl], ps[:],
                                         gbias[:, 3 * NKT + of:3 * NKT + of + 1])
        build_proj(ctx, xn, W["L0_attn_a_Wv"], NKT, evac_v, evac_eng="vec")
        debug_dump(ctx, "v_early", v_t)
        debug_dump(ctx, "xn3", xn[3])
        debug_dump(ctx, "xn7", xn[7])

        debug_dump(ctx, "q", q_t[:])
        debug_dump(ctx, "k", k_t[:])
        debug_dump(ctx, "v", v_t)

        qt_t = q_t
        kt_t = k_t
        kh_t = ctx.scr_bf[:, 0:NKT * NTOK]
        cs_sl = ctx.scr_f32[:, 0:NTOK]
        e_sl = ctx.scr_f32[:, NTOK:2 * NTOK]
        z64 = es.enter_context(nc.sbuf_tensor("g_z64", [128, CHUNK], F32))
        ec_t = es.enter_context(nc.sbuf_tensor("g_ec", [128, NKT * B * NCH], F32))
        with nc.Block() as block:
            @block.vector
            def _(vector):
                vector.memset(z64[:], 0.0).then_inc(ctx.inc("vec"), 1)
        barrier_block(ctx)
        for kt in range(NKT):
            def evac_g(of, ch, ps, eng, kt=kt):
                sl = (slice(None), slice(ch * 512, (ch + 1) * 512))
                eng.activation(lg_t[sl], ps[:], AF.Sigmoid,
                               bias=gbias[:, 2 * NKT + kt:2 * NKT + kt + 1])
                return eng.activation(lg_t[sl], lg_t[sl], AF.Ln)
            build_proj(ctx, xn, W["L0_attn_a_Wg"], 1, evac_g, of0=kt,
                       evac_eng="scl")
            with nc.Block() as block:
                @block.vector
                def _(vector, kt=kt):
                    for bc in range(B * NCH):
                        sl = (slice(None), slice(bc * CHUNK, (bc + 1) * CHUNK))
                        vector.tensor_tensor_scan(cs_sl[sl], lg_t[sl], z64[:],
                                                  0.0, ALU.add, ALU.add)
                    vector.tensor_scalar_max(cs_sl, cs_sl, -85.0).then_inc(
                        ctx.inc("vec"), 1)

                @block.scalar
                def _(scalar, kt=kt):
                    scalar.wait_ge(ctx.h("vec"), ctx.val("vec"))
                    scalar.activation(e_sl, cs_sl, AF.Exp).then_inc(ctx.inc("scl"), 1)

                @block.vector
                def _(vector, kt=kt):
                    vector.wait_ge(ctx.h("scl"), ctx.val("scl"))
                    sl = (slice(None), slice(kt * NTOK, (kt + 1) * NTOK))
                    vector.tensor_mul(qt_t[sl], q_t[sl], e_sl)
                    cs3 = cs_sl.rearrange("p (o c) -> p o c", c=CHUNK)
                    last = cs_sl[:, CHUNK - 1:NTOK:CHUNK]
                    lastb = bass.AP(last.tensor, last.offset,
                                    [list(pp) for pp in last.ap] + [[0, CHUNK]])
                    e3 = e_sl.rearrange("p (o c) -> p o c", c=CHUNK)
                    vector.tensor_sub(e3[:], lastb, cs3[:]).then_inc(
                        ctx.inc("vec"), 1)

                @block.scalar
                def _(scalar, kt=kt):
                    scalar.wait_ge(ctx.h("vec"), ctx.val("vec"))
                    last = cs_sl[:, CHUNK - 1:NTOK:CHUNK]
                    scalar.activation(ec_t[:, kt * B * NCH:(kt + 1) * B * NCH],
                                      last, AF.Exp)
                    scalar.activation(e_sl, e_sl, AF.Exp).then_inc(
                        ctx.inc("scl"), 1)

                @block.vector
                def _(vector, kt=kt):
                    vector.wait_ge(ctx.h("scl"), ctx.val("scl"))
                    sl = (slice(None), slice(kt * NTOK, (kt + 1) * NTOK))
                    vector.tensor_mul(kh_t[sl], k_t[sl], e_sl).then_inc(
                        ctx.inc("vec"), 1)

                @block.scalar
                def _(scalar, kt=kt):
                    scalar.wait_ge(ctx.h("vec"), ctx.val("vec"))
                    scalar.activation(e_sl, cs_sl, AF.Exp,
                                      scale=-1.0).then_inc(ctx.inc("scl"), 1)

                @block.vector
                def _(vector, kt=kt):
                    vector.wait_ge(ctx.h("scl"), ctx.val("scl"))
                    sl = (slice(None), slice(kt * NTOK, (kt + 1) * NTOK))
                    vector.tensor_mul(kt_t[sl], k_t[sl], e_sl).then_inc(
                        ctx.inc("vec"), 1)
            barrier_block(ctx)
        debug_dump(ctx, "lg", lg_t)
        debug_dump(ctx, "cs", cs_sl)
        debug_dump(ctx, "esl", e_sl)
        debug_dump(ctx, "qt", qt_t[:])
        debug_dump(ctx, "kt", kt_t[:])
        debug_dump(ctx, "kh", kh_t)
        debug_dump(ctx, "ec", ec_t[:])
        es_xn.close()
        if True:
            v_tm = es.enter_context(nc.sbuf_tensor("g_vtm", [64, (NTOK // CHUNK) * KD], BF16))
            kh_tm = es.enter_context(nc.sbuf_tensor("g_khtm", [64, (NTOK // CHUNK) * KD], BF16))
            state = es.enter_context(nc.sbuf_tensor("g_state", [128, NKT * B * KD], F32))
            state_bf = es.enter_context(nc.sbuf_tensor("g_statebf", [128, NKT * B * KD], BF16))
            a_sm = es.enter_context(nc.sbuf_tensor("g_asm", [64, CHUNK], BF16))
            gout = ctx.scr_bf[:, 0:NKT * NTOK]
            if True:
                with (
                    nc.psum_tensor("g_pt1", [64, 128], BF16) as pt1,
                    nc.psum_tensor("g_pt2", [64, 128], BF16) as pt2,
                    nc.Block() as block,
                ):
                    p0 = ctx.val("pe")
                    v0 = ctx.val("vec")
                    njob = 2 * NKT * (NTOK // CHUNK)

                    @block.tensor
                    def _(tensor):
                        n = 0
                        for src in (v_t, kh_t):
                            for kt in range(NKT):
                                for c in range(NTOK // CHUNK):
                                    ps = pt1 if n % 2 == 0 else pt2
                                    if n >= 2:
                                        tensor.wait_ge(ctx.h("vec"), v0 + n - 1)
                                    tensor.transpose(
                                        ps[:, :],
                                        src[:, kt * NTOK + c * CHUNK:kt * NTOK + (c + 1) * CHUNK],
                                        ctx.ident[:]).then_inc(ctx.inc("pe"), 1)
                                    n += 1

                    @block.vector
                    def _(vector):
                        n = 0
                        for dst in (v_tm, kh_tm):
                            for kt in range(NKT):
                                for c in range(NTOK // CHUNK):
                                    ps = pt1 if n % 2 == 0 else pt2
                                    vector.wait_ge(ctx.h("pe"), p0 + n + 1)
                                    vector.tensor_copy(
                                        dst[:, c * KD + kt * 128:c * KD + (kt + 1) * 128],
                                        ps[0:64, :]).then_inc(ctx.inc("vec"), 1)
                                    n += 1
                barrier_block(ctx)

                with (
                    nc.psum_tensor("g_psc", [64, CHUNK], F32) as ps_sc,
                    nc.psum_tensor("g_po1", [128, CHUNK], F32) as ps_o1,
                    nc.psum_tensor("g_po2", [128, CHUNK], F32) as ps_o2,
                    nc.psum_tensor("g_ps1", [128, KD], F32) as ps_s1,
                    nc.psum_tensor("g_ps2", [128, KD], F32) as ps_s2,
                    nc.Block() as block,
                ):
                    p0 = ctx.val("pe")
                    v0 = ctx.val("vec")

                    @block.vector
                    def _(vector):
                        vector.memset(state[:], 0.0)
                        vector.memset(state_bf[:], 0.0).then_inc(ctx.inc("gps"), 1)

                    @block.tensor
                    def _(tensor):
                        tensor.wait_ge(ctx.h("gps"), ctx.val("gps"))
                        n = 0
                        for b in range(B):
                            for c in range(NCH):
                                t0 = b * T + c * CHUNK
                                cc = t0 // CHUNK
                                for kt in range(NKT):
                                    i = tensor.matmul(
                                        ps_sc[:],
                                        kt_t[:, kt * NTOK + t0:kt * NTOK + t0 + CHUNK],
                                        qt_t[:, kt * NTOK + t0:kt * NTOK + t0 + CHUNK],
                                        start=(kt == 0), stop=(kt == NKT - 1))
                                i.then_inc(ctx.inc("pe"), 1)
                                for vt in range(NKT):
                                    ps_o = ps_o1 if vt == 0 else ps_o2
                                    for kt in range(NKT):
                                        tensor.matmul(
                                            ps_o[:],
                                            state_bf[:, (kt * B + b) * KD + vt * 128:
                                                     (kt * B + b) * KD + (vt + 1) * 128],
                                            qt_t[:, kt * NTOK + t0:kt * NTOK + t0 + CHUNK],
                                            start=(kt == 0), stop=False)
                                tensor.wait_ge(ctx.h("vec"), v0 + 4 * n + 1)
                                for vt in range(NKT):
                                    ps_o = ps_o1 if vt == 0 else ps_o2
                                    i = tensor.matmul(
                                        ps_o[:],
                                        v_tm[:, cc * KD + vt * 128:cc * KD + (vt + 1) * 128],
                                        a_sm[:],
                                        start=False, stop=True)
                                i.then_inc(ctx.inc("pe"), 1)
                                for kt in range(NKT):
                                    ps_s = ps_s1 if kt == 0 else ps_s2
                                    i = tensor.matmul(
                                        ps_s[:],
                                        kh_tm[:, cc * KD + kt * 128:cc * KD + (kt + 1) * 128],
                                        v_tm[:, cc * KD:(cc + 1) * KD],
                                        start=True, stop=True)
                                i.then_inc(ctx.inc("pe"), 1)
                                tensor.wait_ge(ctx.h("vec"), v0 + 4 * n + 4)
                                n += 1

                    @block.vector
                    def _(vector):
                        n = 0
                        for b in range(B):
                            for c in range(NCH):
                                t0 = b * T + c * CHUNK
                                vector.wait_ge(ctx.h("pe"), p0 + 3 * n + 1)
                                vector.tensor_mul(a_sm[:], ps_sc[:],
                                                  ctx.utmask[:]).then_inc(
                                    ctx.inc("vec"), 1)
                                vector.wait_ge(ctx.h("pe"), p0 + 3 * n + 2)
                                vector.tensor_copy(gout[:, 0 * NTOK + t0:0 * NTOK + t0 + CHUNK],
                                                   ps_o1[:])
                                vector.tensor_copy(gout[:, 1 * NTOK + t0:1 * NTOK + t0 + CHUNK],
                                                   ps_o2[:]).then_inc(ctx.inc("vec"), 1)
                                vector.wait_ge(ctx.h("pe"), p0 + 3 * n + 3)
                                for kt in range(NKT):
                                    ps_s = ps_s1 if kt == 0 else ps_s2
                                    idx = kt * B + b
                                    eccol = ec_t[:, kt * B * NCH + b * NCH + c:
                                                  kt * B * NCH + b * NCH + c + 1]
                                    vector.scalar_tensor_tensor(
                                        state[:, idx * KD:(idx + 1) * KD],
                                        state[:, idx * KD:(idx + 1) * KD],
                                        eccol, ps_s[:], ALU.mult, ALU.add)
                                vector.engine_nop().then_inc(ctx.inc("vec"), 1)
                                for kt in range(NKT):
                                    idx = kt * B + b
                                    vector.tensor_copy(
                                        state_bf[:, idx * KD:(idx + 1) * KD],
                                        state[:, idx * KD:(idx + 1) * KD])
                                vector.engine_nop().then_inc(ctx.inc("vec"), 1)
                                n += 1
                barrier_block(ctx)

                debug_dump(ctx, "gout", gout)
                debug_dump(ctx, "state", state[:])
                def evac_wo(of, ch, ps, vector):
                    bo = ctx.gla_bo[:, of:of + 1]
                    vector.scalar_tensor_tensor(
                        ctx.scr_f32b[:, ch * 512:(ch + 1) * 512], ps[:], bo,
                        ctx.scr_f32[:, ch * 512:(ch + 1) * 512],
                        ALU.add, ALU.add)

                build_proj_residual(ctx, gout, NKT, W["L0_attn_a_Wo"],
                                    W["L0_attn_a_bo"], xres)

    barrier_block(ctx)


def build_proj_residual(ctx, src_fm, n_in_tiles, w_dram, b_dram, xres,
                        extra_add_dram=None):
    nc = ctx.nc
    with (
        nc.sbuf_tensor("pr_wf", [128, n_in_tiles * 128], F32) as wf,
        nc.sbuf_tensor("pr_w", [128, n_in_tiles * 128], BF16) as wb,
        nc.sbuf_tensor("pr_b", [128, NFT], F32) as bcols,
        nc.psum_tensor("pr_p1", [128, 512], F32) as p1,
        nc.psum_tensor("pr_p2", [128, 512], F32) as p2,
    ):
        xf = ctx.scr_f32[:, 0:NTOK]
        of_out = ctx.scr_f32[:, NTOK:2 * NTOK]
        with nc.Block() as block:
            @block.sync
            def _(sync):
                for ft in range(NFT):
                    dma(ctx, sync, bcols[:, ft:ft + 1],
                        b_dram[ft * 128:(ft + 1) * 128, :])
                wait_all(ctx, sync, "dma")
        for of in range(NFT):
            with nc.Block() as block:
                d0 = ctx.val("dma")
                g0 = ctx.val("gps")
                p0 = ctx.val("pe")
                v0 = ctx.val("vec")
                nch = NTOK // 512

                @block.sync
                def _(sync):
                    dma(ctx, sync, xf[:], xres.ap()[of])
                    for it in range(n_in_tiles):
                        dma(ctx, sync, wf[:, it * 128:(it + 1) * 128],
                            w_dram[it * 128:(it + 1) * 128, of * 128:(of + 1) * 128])

                @block.scalar
                def _(scalar):
                    scalar.wait_ge(ctx.h("dma"), d0 + 16 * (1 + n_in_tiles))
                    scalar.activation(wb[:], wf[:], AF.Copy).then_inc(
                        ctx.inc("gps"), 1)

                @block.tensor
                def _(tensor):
                    tensor.wait_ge(ctx.h("gps"), g0 + 1)
                    for ch in range(nch):
                        ps = p1 if ch % 2 == 0 else p2
                        if ch >= 2:
                            tensor.wait_ge(ctx.h("vec"), v0 + ch - 1)
                        for it in range(n_in_tiles):
                            i = tensor.matmul(
                                ps[:], wb[:, it * 128:(it + 1) * 128],
                                src_fm[:, it * NTOK + ch * 512:it * NTOK + (ch + 1) * 512],
                                start=(it == 0), stop=(it == n_in_tiles - 1))
                        i.then_inc(ctx.inc("pe"), 1)

                v1 = ctx.val("vec")

                @block.vector
                def _(vector):
                    for ch in range(nch):
                        ps = p1 if ch % 2 == 0 else p2
                        vector.wait_ge(ctx.h("pe"), p0 + ch + 1)
                        vector.scalar_tensor_tensor(
                            of_out[:, ch * 512:(ch + 1) * 512], ps[:],
                            bcols[:, of:of + 1], xf[:, ch * 512:(ch + 1) * 512],
                            ALU.add, ALU.add).then_inc(ctx.inc("vec"), 1)

                @block.scalar
                def _(scalar):
                    scalar.wait_ge(ctx.h("vec"), v1 + nch)
                    dma(ctx, scalar, xres.ap()[of], of_out[:], sem="dmb")
                    scalar.wait_ge(ctx.h("dmb"), ctx.val("dmb"))
    barrier_block(ctx)



def build_ffn0_prep(ctx, W):
    nc = ctx.nc
    ctx.ffn_kq8 = ctx._stack_append(nc.sbuf_tensor("f0_kq8", [128, NFT * D], FP8))
    ctx.ffn_bias = ctx._stack_append(nc.sbuf_tensor("f0_bias", [128, NFT], F32))
    ctx.ffn_scol = ctx._stack_append(nc.sbuf_tensor("f0_scol", [128, 5], F32))
    with contextlib.ExitStack() as es:
        kbf = es.enter_context(nc.sbuf_tensor("f0_kbf", [128, D], BF16))
        srow = es.enter_context(nc.sbuf_tensor("f0_srow", [1, 2], F32))
        with (
            nc.psum_tensor("f0_ps", [128, 2], F32) as ps,
            nc.Block() as block,
        ):
            d0 = ctx.val("dma")

            @block.sync
            def _(sync):
                dma(ctx, sync, srow[:, 0:1], W["L0_ffn_input_scale"][:])
                dma(ctx, sync, srow[:, 1:2], W["L0_ffn_kernel_scale"][:])
                for ft in range(NFT):
                    dma(ctx, sync, kbf[:, ft:ft + 1],
                        W["L0_ffn_bias"][ft * 128:(ft + 1) * 128, :])

            @block.tensor
            def _(tensor):
                tensor.wait_ge(ctx.h("dma"), ctx.val("dma"))
                tensor.matmul(ps[:], ctx.ones_row[:], srow[:],
                              start=True, stop=True).then_inc(ctx.inc("pe"), 1)

            @block.vector
            def _(vector):
                vector.wait_ge(ctx.h("pe"), ctx.val("pe"))
                wait_all(ctx, vector, "dma")
                vector.tensor_copy(ctx.ffn_bias[:], kbf[:, 0:NFT])
                vector.tensor_copy(ctx.ffn_scol[:, 0:2], ps[:])
                vector.drain()
                vector.tensor_mul(ctx.ffn_scol[:, 2:3], ctx.ffn_scol[:, 0:1],
                                  ctx.ffn_scol[:, 1:2])
                vector.drain()
                vector.reciprocal(ctx.ffn_scol[:, 3:5],
                                  ctx.ffn_scol[:, 0:2]).then_inc(ctx.inc("vec"), 1)

        for it in range(NFT):
            with nc.Block() as block:
                d0 = ctx.val("dma")

                @block.sync
                def _(sync, it=it):
                    dma(ctx, sync, kbf[:], W["L0_ffn_kernel"][it * 128:(it + 1) * 128, :])

                @block.vector
                def _(vector, it=it, d0=d0):
                    vector.wait_ge(ctx.h("dma"), d0 + 16)
                    vector.tensor_scalar_mul(ctx.ffn_kq8[:, it * D:(it + 1) * D],
                                              kbf[:],
                                              ctx.ffn_scol[:, 4:5]).then_inc(
                        ctx.inc("vec"), 1)
    barrier_block(ctx)


def build_ffn0(ctx, xres, W, parts=2):
    nc = ctx.nc
    with contextlib.ExitStack() as es:
        es_r = contextlib.ExitStack()
        xn_all = es_r.enter_context(
            nc.sbuf_tensor("f0_xn", [128, NFT * NTOK], BF16, side="right"))
        xn = [xn_all[:, i * NTOK:(i + 1) * NTOK] for i in range(NFT)]
        build_layernorm(ctx, xres, xn)
        xq8 = es_r.enter_context(
            nc.sbuf_tensor("f0_xq8", [128, NFT * NTOK], FP8, side="right"))

        with nc.Block() as block:
            @block.vector
            def _(vector):
                for it in range(NFT):
                    vector.tensor_scalar_mul(
                        xq8[:, it * NTOK:(it + 1) * NTOK], xn[it],
                        ctx.ffn_scol[:, 3:4]).then_inc(ctx.inc("vec"), 1)
        barrier_block(ctx)
        debug_dump(ctx, "f0xq8", xq8[:, 0:NTOK])
        debug_dump(ctx, "f0kq8", ctx.ffn_kq8[:])
        debug_dump(ctx, "f0scol", ctx.ffn_scol[:])
        debug_dump(ctx, "f0bias", ctx.ffn_bias[:])
        debug_dump(ctx, "f0xn0", xn[0])
        if parts < 2:
            es_r.close()
            return

        with contextlib.ExitStack() as es2:
            ybf = ctx.scr_bf[:, 0:NTOK]
            p1 = es2.enter_context(nc.psum_tensor("f0_p1", [128, 512], F32))
            p2 = es2.enter_context(nc.psum_tensor("f0_p2", [128, 512], F32))
            xf = ctx.scr_f32[:, 0:NTOK]
            of_out = ctx.scr_f32[:, NTOK:2 * NTOK]
            for of in range(NFT):
                with nc.Block() as block:
                    d0 = ctx.val("dma")
                    p0 = ctx.val("pe")
                    v0 = ctx.val("vec")
                    nch = NTOK // 512

                    @block.sync
                    def _(sync, of=of):
                        dma(ctx, sync, xf, xres.ap()[of])

                    @block.tensor
                    def _(tensor, of=of, v0=v0):
                        for ch in range(nch):
                            ps = p1 if ch % 2 == 0 else p2
                            if ch >= 2:
                                tensor.wait_ge(ctx.h("vec"), v0 + 2 * (ch - 1))
                            for it in range(NFT):
                                i = tensor.matmul(
                                    ps[:],
                                    ctx.ffn_kq8[:, it * D + of * 128:it * D + (of + 1) * 128],
                                    xq8[:, it * NTOK + ch * 512:it * NTOK + (ch + 1) * 512],
                                    start=(it == 0), stop=(it == NFT - 1))
                            i.then_inc(ctx.inc("pe"), 1)

                    @block.vector
                    def _(vector, of=of, d0=d0, p0=p0):
                        vector.wait_ge(ctx.h("dma"), d0 + 16)
                        for ch in range(nch):
                            ps = p1 if ch % 2 == 0 else p2
                            vector.wait_ge(ctx.h("pe"), p0 + ch + 1)
                            c = slice(ch * 512, (ch + 1) * 512)
                            vector.tensor_scalar(ybf[:, c], ps[:],
                                                 ctx.ffn_scol[:, 0:1],
                                                 ctx.ffn_scol[:, 1:2],
                                                 ALU.mult, ALU.mult)
                            vector.scalar_tensor_tensor(
                                of_out[:, c], ybf[:, c],
                                ctx.ffn_bias[:, of:of + 1], xf[:, c],
                                ALU.add, ALU.add).then_inc(ctx.inc("vec", 2), 2)

                    v1 = ctx.val("vec")

                    @block.scalar
                    def _(scalar, of=of, v1=v1):
                        scalar.wait_ge(ctx.h("vec"), v1)
                        dma(ctx, scalar, xres.ap()[of], of_out, sem="dmb")
                        scalar.wait_ge(ctx.h("dmb"), ctx.val("dmb"))
        es_r.close()
    barrier_block(ctx)


def apof(t):
    return t if isinstance(t, bass.AP) else t[:]


def _bitonic_rounds(n=32):
    out = []
    k = 2
    while k <= n:
        j = k // 2
        while j >= 1:
            out.append((k, j, n // (2 * j), max(k // (2 * j), 1)))
            j //= 2
        k *= 2
    return out


def _emit_sparsemax(ctx, vector, z, zb1, zb2, zb3):
    nseg = z.ap[-1][1] // 32
    cur, nxt = z, zb1
    for (k, j, nb, m) in _bitonic_rounds(32):
        for par in (0, 1):
            if par == 1 and nb < 2 * m:
                continue
            base = par * m * 2 * j
            if nb >= 2 * m:
                dims = [[4 * m * j, nseg * (nb // (2 * m))], [2 * j, m], [1, j]]
            else:
                dims = [[2 * j, nseg * nb], [1, j]]

            def mk(t, extra, dims=dims, base=base):
                return bass.AP(t.tensor, t.offset + base + extra,
                               [list(t.ap[0])] + [list(dd) for dd in dims])
            lo_i, hi_i = mk(cur, 0), mk(cur, j)
            lo_o, hi_o = mk(nxt, 0), mk(nxt, j)
            if par == 0:
                vector.tensor_max(lo_o, lo_i, hi_i)
                vector.tensor_tensor(hi_o, lo_i, hi_i, ALU.min)
            else:
                vector.tensor_tensor(lo_o, lo_i, hi_i, ALU.min)
                vector.tensor_max(hi_o, lo_i, hi_i)
        vector.drain()
        if cur is z:
            cur, nxt = zb1, zb3
        else:
            cur, nxt = nxt, cur
    zs = cur
    if zs is not zb1:
        vector.tensor_copy(zb1[:], zs[:])
        zs = zb1
    buf_a, buf_b = zs, zb3
    d = 1
    while d < 32:
        a3 = buf_a.rearrange("p (s c) -> p s c", c=32)
        b3 = buf_b.rearrange("p (s c) -> p s c", c=32)
        vector.tensor_add(b3[:, :, d:32], a3[:, :, d:32], a3[:, :, 0:32 - d])
        vector.tensor_copy(b3[:, :, 0:d], a3[:, :, 0:d])
        vector.drain()
        buf_a = buf_b
        buf_b = zb2 if buf_a is zb3 else zb3
        d *= 2
    cum = buf_a
    if cum is not zb2:
        vector.tensor_copy(zb2[:], cum[:])
        cum = zb2
    return zs, cum


def build_attnb(ctx, xres, W, xn_dram, scr2, parts=4):
    nc = ctx.nc
    H, E, S = 16, 64, 32
    with contextlib.ExitStack() as es:
        es_a = contextlib.ExitStack()
        xn_buf = es_a.enter_context(
            nc.sbuf_tensor("ab_xn", [128, NTOK], BF16, side="right"))
        build_layernorm(ctx, xres, [xn_buf] * NFT, xn_dram=xn_dram)
        es_a.close()

        wq_sb = es.enter_context(nc.sbuf_tensor("ab_wq", [128, NFT * D], BF16))
        keys_fm = es.enter_context(nc.sbuf_tensor("ab_kfm", [128, NFT * S], BF16))
        keys_pad = es.enter_context(nc.sbuf_tensor("ab_kpad", [128, 16 * S], BF16))
        v2 = es.enter_context(nc.sbuf_tensor("ab_v2", [128, NFT * 128], BF16))
        bcols = es.enter_context(nc.sbuf_tensor("ab_b", [128, 2 * NFT], F32))
        krow = es.enter_context(nc.sbuf_tensor("ab_krow", [128, S], F32))
        with contextlib.ExitStack() as es1:
            kp_fm = es1.enter_context(nc.sbuf_tensor("ab_kp", [128, NFT * S], BF16))
            kp_f32 = es1.enter_context(nc.sbuf_tensor("ab_kpf", [128, S], F32))
            brow = es1.enter_context(nc.sbuf_tensor("ab_br", [1, 2 * D], BF16))
            brf = es1.enter_context(nc.sbuf_tensor("ab_brf", [1, 2 * D], F32))
            ones32 = es1.enter_context(nc.sbuf_tensor("ab_o32", [1, 32], BF16))
            keys_sm = es1.enter_context(nc.sbuf_tensor("ab_ksm", [32, D], BF16))
            vals_rep = es1.enter_context(nc.sbuf_tensor("ab_vrep", [128, D], BF16))

            with nc.Block() as block:
                d0 = ctx.val("dma")
                dq00 = ctx.val("dq0")
                v0b1 = ctx.val("vec")

                @block.sync
                def _(sync):
                    for it in range(NFT):
                        if it >= 1:
                            sync.wait_ge(ctx.h("vec"), v0b1 + it)
                        dma(ctx, sync, kp_f32[:],
                            W["L0_attn_b_key_param"][it * 128:(it + 1) * 128, :],
                            sem="dq0")
                    dma(ctx, sync, brf[:, 0:D],
                        bass.AP(W["L0_attn_b_bk"].tensor, 0, [[2 * D, 1], [1, D]]))
                    dma(ctx, sync, brf[:, D:2 * D],
                        bass.AP(W["L0_attn_b_bv"].tensor, 0, [[2 * D, 1], [1, D]]))
                    for ft in range(NFT):
                        dma(ctx, sync, bcols[:, ft:ft + 1],
                            W["L0_attn_b_bq"][ft * 128:(ft + 1) * 128, :])
                        dma(ctx, sync, bcols[:, NFT + ft:NFT + ft + 1],
                            W["L0_attn_b_bo"][ft * 128:(ft + 1) * 128, :])
                    dma(ctx, sync, krow[:], W["const_krow"][:])

                @block.vector
                def _(vector):
                    for it in range(NFT):
                        vector.wait_ge(ctx.h("dq0"), dq00 + 16 * (it + 1))
                        vector.tensor_copy(kp_fm[:, it * S:(it + 1) * S],
                                           kp_f32[:]).then_inc(ctx.inc("vec"), 1)
                    vector.wait_ge(ctx.h("dma"), ctx.val("dma"))
                    vector.tensor_copy(brow[:], brf[:])
                    vector.memset(ones32[:], 1.0)
                    vector.memset(v2[:], 0.0).then_inc(ctx.inc("vec"), 1)
            barrier_block(ctx)

            for step in range(2):
                wname = "L0_attn_b_Wk" if step == 0 else "L0_attn_b_Wv"
                boff = 0 if step == 0 else D
                with nc.Block() as block:
                    q0 = [ctx.val("dq0"), ctx.val("dq1")]
                    s0 = ctx.val("scl")

                    @block.sync
                    def _(sync, wname=wname):
                        for it in range(NFT):
                            if it >= 2:
                                sync.wait_ge(ctx.h("scl"), s0 + it - 1)
                            dma(ctx, sync, ctx.scr_f32[:, (it % 2) * D:(it % 2) * D + D],
                                W[wname][it * 128:(it + 1) * 128, :],
                                sem=f"dq{it % 2}")

                    @block.scalar
                    def _(scalar):
                        for it in range(NFT):
                            scalar.wait_ge(ctx.h(f"dq{it % 2}"),
                                           q0[it % 2] + 16 * (it // 2 + 1))
                            scalar.activation(ctx.scr_bf[:, it * D:(it + 1) * D],
                                              ctx.scr_f32[:, (it % 2) * D:(it % 2) * D + D],
                                              AF.Copy).then_inc(ctx.inc("scl"), 1)
                with (
                    nc.psum_tensor("ab_ps", [32, 512], F32) as ps,
                    nc.Block() as block,
                ):
                    p0 = ctx.val("pe")
                    v0 = ctx.val("vec")

                    @block.tensor
                    def _(tensor, step=step, boff=boff):
                        for hc in range(2):
                            if hc == 1:
                                tensor.wait_ge(ctx.h("vec"), v0 + 1)
                            for it in range(NFT):
                                lhs = (kp_fm if step == 0 else keys_fm)
                                tensor.matmul(ps[:], lhs[:, it * S:(it + 1) * S],
                                              ctx.scr_bf[:, it * D + hc * 512:it * D + hc * 512 + 512],
                                              start=(it == 0), stop=False)
                            i = tensor.matmul(ps[:], ones32[:],
                                              brow[:, boff + hc * 512:boff + hc * 512 + 512],
                                              start=False, stop=True)
                            i.then_inc(ctx.inc("pe"), 1)

                    @block.vector
                    def _(vector, step=step):
                        for hc in range(2):
                            vector.wait_ge(ctx.h("pe"), p0 + hc + 1)
                            dst = keys_sm if step == 0 else vals_rep[0:32, :]
                            vector.tensor_copy(dst[0:32, hc * 512:hc * 512 + 512],
                                               ps[:]).then_inc(ctx.inc("vec"), 1)
                barrier_block(ctx)
                if step == 0:
                    with (
                        nc.psum_tensor("ab_pt", [128, 32], BF16) as pt,
                        nc.Block() as block,
                    ):
                        p0 = ctx.val("pe")
                        v0 = ctx.val("vec")

                        @block.tensor
                        def _(tensor):
                            for it in range(NFT):
                                if it >= 1:
                                    tensor.wait_ge(ctx.h("vec"), v0 + it)
                                tensor.transpose(pt[:], keys_sm[0:32, it * 128:(it + 1) * 128],
                                                 ctx.ident[0:32, 0:32]).then_inc(ctx.inc("pe"), 1)

                        @block.vector
                        def _(vector):
                            for it in range(NFT):
                                vector.wait_ge(ctx.h("pe"), p0 + it + 1)
                                vector.tensor_copy(keys_fm[:, it * S:(it + 1) * S],
                                                   pt[:]).then_inc(ctx.inc("vec"), 1)
                    barrier_block(ctx)

            with (
                nc.psum_tensor("ab_pr", [128, 512], F32) as pr,
                nc.Block() as block,
            ):
                p0 = ctx.val("pe")
                v0 = ctx.val("vec")

                @block.tensor
                def _(tensor):
                    for hc in range(2):
                        if hc == 1:
                            tensor.wait_ge(ctx.h("vec"), v0 + 1)
                        tensor.matmul(pr[:], ctx.rep32[:],
                                      vals_rep[0:32, hc * 512:hc * 512 + 512],
                                      start=True, stop=True).then_inc(ctx.inc("pe"), 1)

                @block.vector
                def _(vector):
                    for hc in range(2):
                        vector.wait_ge(ctx.h("pe"), p0 + hc + 1)
                        vector.tensor_copy(vals_rep[:, hc * 512:hc * 512 + 512],
                                           pr[:]).then_inc(ctx.inc("vec"), 1)
            barrier_block(ctx)

            with nc.Block() as block:
                @block.vector
                def _(vector):
                    vector.memset(keys_pad[:], 0.0)
                    for h in range(H):
                        r0 = (h % 2) * 64
                        vector.tensor_copy(
                            keys_pad[r0:r0 + 64, h * S:(h + 1) * S],
                            keys_fm[r0:r0 + 64, (h // 2) * S:(h // 2) * S + S])
                    for f in range(NFT):
                        b0 = (f % 2) * 64
                        vector.tensor_copy(
                            v2[b0:b0 + 32, f * 128:f * 128 + 64],
                            vals_rep[b0:b0 + 32, (2 * f) * 64:(2 * f) * 64 + 64])
                        vector.tensor_copy(
                            v2[b0 + 32:b0 + 64, f * 128 + 64:f * 128 + 128],
                            vals_rep[b0 + 32:b0 + 64, (2 * f + 1) * 64:(2 * f + 1) * 64 + 64])
                    vector.engine_nop().then_inc(ctx.inc("vec"), 1)
            barrier_block(ctx)

            with nc.Block() as block:
                q0 = [ctx.val("dq0"), ctx.val("dq1")]
                s0 = ctx.val("scl")

                @block.sync
                def _(sync):
                    for it in range(NFT):
                        if it >= 2:
                            sync.wait_ge(ctx.h("scl"), s0 + it - 1)
                        dma(ctx, sync, ctx.scr_f32[:, (it % 2) * D:(it % 2) * D + D],
                            W["L0_attn_b_Wq"][it * 128:(it + 1) * 128, :],
                            sem=f"dq{it % 2}")

                @block.scalar
                def _(scalar):
                    for it in range(NFT):
                        scalar.wait_ge(ctx.h(f"dq{it % 2}"),
                                       q0[it % 2] + 16 * (it // 2 + 1))
                        scalar.activation(wq_sb[:, it * D:(it + 1) * D],
                                          ctx.scr_f32[:, (it % 2) * D:(it % 2) * D + D],
                                          AF.Copy).then_inc(ctx.inc("scl"), 1)
            barrier_block(ctx)

        if parts < 2:
            return
        es_r = contextlib.ExitStack()
        out_r = es_r.enter_context(
            nc.sbuf_tensor("ab_outr", [128, NFT * NTOK], BF16, side="right"))
        small_dbg = es.enter_context(nc.sbuf_tensor("ab_small", [128, 4 * 64], F32))
        with contextlib.ExitStack() as es2:
            xn_sb = ctx.scr_bf[:, 0:NFT * 512]
            q_sb = ctx.scr_bf[:, NFT * 512:2 * NFT * 512]
            z_sb = ctx.scr_f32[:, 0:2048]
            zb1 = ctx.scr_f32[:, 2048:4096]
            zb2 = ctx.scr_f32[:, 4096:6144]
            zb3 = ctx.scr_f32[:, 6144:8192]
            small = small_dbg
            a_bf = ctx.scr_xr[:, 0:2048]
            at_sb = ctx.scr_xr[:, 2048:2048 + 16 * 128]
            qp1 = es2.enter_context(nc.psum_tensor("ab_qp1", [128, 512], F32))
            qp2 = es2.enter_context(nc.psum_tensor("ab_qp2", [128, 512], F32))
            scp = es2.enter_context(nc.psum_tensor("ab_scp", [128, 512], F32))
            op1 = es2.enter_context(nc.psum_tensor("ab_op1", [128, 512], F32))
            op2 = es2.enter_context(nc.psum_tensor("ab_op2", [128, 512], F32))
            tp1 = es2.enter_context(nc.psum_tensor("ab_tp1", [128, 1024], BF16))
            tp2 = es2.enter_context(nc.psum_tensor("ab_tp2", [128, 1024], BF16))

            NSUP = NTOK // 512
            STAGE = 5
            with nc.Block() as block:
                d0 = ctx.val("dma")
                p0 = ctx.val("pe")
                v0 = ctx.val("vec")
                PEI = [0, 8, 12, 12, 28, 36][STAGE]
                VEI = [0, 8, 12, 13, 29, 37][STAGE]

                @block.sync
                def _(sync):
                    for sp in range(NSUP):
                        if sp >= 1:
                            sync.wait_ge(ctx.h("pe"), p0 + sp * PEI)
                        for it in range(NFT):
                            dma(ctx, sync, xn_sb[:, it * 512:(it + 1) * 512],
                                bass.AP(xn_dram.ap().tensor,
                                        it * 128 * NTOK + sp * 512,
                                        [[NTOK, 128], [1, 512]]))

                @block.tensor
                def _(tensor):
                    for sp in range(NSUP):
                        pb = p0 + sp * PEI
                        vb = v0 + sp * VEI
                        tensor.wait_ge(ctx.h("dma"), d0 + 16 * NFT * (sp + 1))
                        if sp >= 1:
                            tensor.wait_ge(ctx.h("vec"), vb)
                        nq = 0
                        for of in range(NFT):
                            qp = qp1 if of % 2 == 0 else qp2
                            if of >= 2:
                                tensor.wait_ge(ctx.h("vec"), vb + of - 1)
                            for it in range(NFT):
                                i = tensor.matmul(
                                    qp[:], wq_sb[:, it * D + of * 128:it * D + (of + 1) * 128],
                                    xn_sb[:, it * 512:(it + 1) * 512],
                                    start=(it == 0), stop=(it == NFT - 1))
                            i.then_inc(ctx.inc("pe"), 1)
                        if STAGE < 2:
                            continue
                        for tb in range(4):
                            tensor.wait_ge(ctx.h("vec"), vb + 8 + tb)
                            for h in range(H):
                                i = tensor.matmul(
                                    scp[:, h * 32:(h + 1) * 32],
                                    q_sb[:, (h // 2) * 512 + tb * 128:(h // 2) * 512 + tb * 128 + 128],
                                    keys_pad[:, h * S:(h + 1) * S],
                                    start=True, stop=True)
                            i.then_inc(ctx.inc("pe"), 1)
                        if STAGE < 4:
                            continue
                        tensor.wait_ge(ctx.h("vec"), vb + 13)
                        for g in range(16):
                            tp = tp1 if g % 2 == 0 else tp2
                            if g >= 2:
                                tensor.wait_ge(ctx.h("vec"), vb + 13 + g - 1)
                            tensor.transpose(tp[0:128, 0:128], a_bf[:, g * 128:(g + 1) * 128],
                                             ctx.ident[:]).then_inc(ctx.inc("pe"), 1)
                        if STAGE < 5:
                            continue
                        for tb in range(4):
                            tensor.wait_ge(ctx.h("vec"), vb + 13 + (tb + 1) * 4)
                            for half in range(2):
                                if tb >= 1:
                                    tensor.wait_ge(ctx.h("vec"),
                                                   vb + 29 + 2 * (tb - 1) + half + 1)
                                op = op1 if half == 0 else op2
                                for f in range(4 * half, 4 * half + 4):
                                    i = tensor.matmul(
                                        op[:, (f % 4) * 128:(f % 4) * 128 + 128],
                                        v2[:, f * 128:(f + 1) * 128],
                                        at_sb[:, (tb * 4 + f // 2) * 128:(tb * 4 + f // 2) * 128 + 128],
                                        start=True, stop=True)
                                i.then_inc(ctx.inc("pe"), 1)

                @block.vector
                def _(vector):
                    for sp in range(NSUP):
                        pb = p0 + sp * PEI
                        vb = v0 + sp * VEI
                        for of in range(NFT):
                            qp = qp1 if of % 2 == 0 else qp2
                            vector.wait_ge(ctx.h("pe"), pb + of + 1)
                            vector.tensor_scalar_add(
                                q_sb[:, of * 512:(of + 1) * 512], qp[:],
                                bcols[:, of:of + 1]).then_inc(ctx.inc("vec"), 1)
                        if STAGE < 2:
                            continue
                        for tb in range(4):
                            vector.wait_ge(ctx.h("pe"), pb + 8 + tb + 1)
                            vector.tensor_scalar_mul(
                                z_sb[:, tb * 512:(tb + 1) * 512], scp[:],
                                0.125).then_inc(ctx.inc("vec"), 1)
                        if STAGE < 3:
                            continue
                        zs, cum = _emit_sparsemax(ctx, vector, z_sb, zb1, zb2, zb3)
                        nsegs = 64
                        z3 = zs.rearrange("p (s c) -> p s c", c=32)
                        c3 = cum.rearrange("p (s c) -> p s c", c=32)
                        kr = apof(krow)
                        kb = bass.AP(kr.tensor, kr.offset,
                                     [list(kr.ap[0])] + [[0, nsegs], [1, 32]])
                        s3 = zb3.rearrange("p (s c) -> p s c", c=32)
                        vector.tensor_mul(s3[:], z3[:], kb)
                        vector.drain()
                        vector.tensor_sub(s3[:], s3[:], c3[:])
                        vector.drain()
                        vector.tensor_single_scalar(s3[:], s3[:], -1.0, ALU.is_gt)
                        vector.drain()
                        vector.reduce_sum(
                            small[:, 0:nsegs].rearrange("p (s c) -> p s c", c=1),
                            s3[:], axis=mybir.AxisListType.X)
                        vector.drain()
                        vector.tensor_mul(s3[:], s3[:], z3[:])
                        vector.drain()
                        vector.reduce_sum(
                            small[:, 64:64 + nsegs].rearrange("p (s c) -> p s c", c=1),
                            s3[:], axis=mybir.AxisListType.X)
                        vector.drain()
                        vector.reciprocal(small[:, 128:128 + nsegs], small[:, 0:nsegs])
                        vector.drain()
                        vector.tensor_scalar_add(small[:, 64:64 + nsegs],
                                                 small[:, 64:64 + nsegs], -1.0)
                        vector.drain()
                        vector.tensor_mul(small[:, 64:64 + nsegs],
                                          small[:, 64:64 + nsegs],
                                          small[:, 128:128 + nsegs])
                        vector.drain()
                        sm = apof(small)
                        taub = bass.AP(sm.tensor, sm.offset + 64,
                                       [list(sm.ap[0])] + [[1, nsegs], [0, 32]])
                        zr3 = z_sb.rearrange("p (s c) -> p s c", c=32)
                        vector.tensor_sub(zb3.rearrange("p (s c) -> p s c", c=32)[:],
                                          zr3[:], taub)
                        vector.drain()
                        i = vector.tensor_scalar_max(a_bf[:], zb3[:], 0.0)
                        i.then_inc(ctx.inc("vec"), 1)
                        if STAGE < 4:
                            continue
                        for g in range(16):
                            tp = tp1 if g % 2 == 0 else tp2
                            vector.wait_ge(ctx.h("pe"), pb + 12 + g + 1)
                            vector.tensor_copy(at_sb[:, g * 128:(g + 1) * 128],
                                               tp[0:128, 0:128]).then_inc(ctx.inc("vec"), 1)
                        if STAGE < 5:
                            continue
                        for tb in range(4):
                            tbg = sp * 4 + tb
                            bidx = tbg // 16
                            tbb = tbg % 16
                            for half in range(2):
                                vector.wait_ge(ctx.h("pe"),
                                               pb + 28 + 2 * tb + half + 1)
                                op = op1 if half == 0 else op2
                                opa = apof(op)
                                for f in range(4 * half, 4 * half + 4):
                                    src = bass.AP(opa.tensor, opa.offset + (f % 4) * 128,
                                                  [list(opa.ap[0])] + [[1, 16], [16, 8]])
                                    oro = apof(out_r)
                                    dst = bass.AP(oro.tensor,
                                                  oro.offset + f * NTOK + bidx * 2048 + tbb * 8,
                                                  [list(oro.ap[0])] + [[128, 16], [1, 8]])
                                    vector.tensor_copy(dst, src)
                                vector.engine_nop().then_inc(ctx.inc("vec"), 1)
            barrier_block(ctx)

        debug_dump(ctx, "ab_outr", out_r[:])
        debug_dump(ctx, "ab_kpad", keys_pad[:])
        debug_dump(ctx, "ab_v2", v2[:])
        debug_dump(ctx, "ab_z", ctx.scr_f32[:, 0:2048])
        debug_dump(ctx, "ab_zs", ctx.scr_f32[:, 2048:4096])
        debug_dump(ctx, "ab_cum", ctx.scr_f32[:, 4096:6144])
        debug_dump(ctx, "ab_small", small_dbg[:])
        debug_dump(ctx, "ab_abf", ctx.scr_xr[:, 0:2048])
        if parts < 3:
            es_r.close()
            return
        with nc.Block() as block:
            @block.sync
            def _(sync):
                for pt in range(NFT):
                    for bidx in range(B):
                        for hsub in range(2):
                            oro = apof(out_r)
                            pdim = list(oro.ap[0])
                            src = bass.AP(oro.tensor,
                                          oro.offset + (hsub * 64) * pdim[0]
                                          + pt * NTOK + bidx * 2048,
                                          [[pdim[0], 64]] + [[1, 2048]])
                            dst = bass.AP(scr2.ap().tensor,
                                          bidx * (16 * 16 * 64 * 128)
                                          + (2 * pt + hsub) * 131072,
                                          [[128, 64], [8192, 16], [1, 128]])
                            dma(ctx, sync, dst, src)
                wait_all(ctx, sync, "dma")
        barrier_block(ctx)
        es_r.close()

        if parts < 4:
            return
        with contextlib.ExitStack() as es3:
            wo_sb = es3.enter_context(nc.sbuf_tensor("ab_wo", [128, NFT * D], BF16))
            rhs_sb = es3.enter_context(nc.sbuf_tensor("ab_rhs", [128, NFT * 512], BF16))
            pw1 = es3.enter_context(nc.psum_tensor("ab_pw1", [128, 512], F32))
            pw2 = es3.enter_context(nc.psum_tensor("ab_pw2", [128, 512], F32))
            with nc.Block() as block:
                q0 = [ctx.val("dq0"), ctx.val("dq1")]
                s0 = ctx.val("scl")

                @block.sync
                def _(sync):
                    for it in range(NFT):
                        if it >= 2:
                            sync.wait_ge(ctx.h("scl"), s0 + it - 1)
                        dma(ctx, sync, ctx.scr_f32[:, (it % 2) * D:(it % 2) * D + D],
                            W["L0_attn_b_Wo"][it * 128:(it + 1) * 128, :],
                            sem=f"dq{it % 2}")

                @block.scalar
                def _(scalar):
                    for it in range(NFT):
                        scalar.wait_ge(ctx.h(f"dq{it % 2}"),
                                       q0[it % 2] + 16 * (it // 2 + 1))
                        scalar.activation(wo_sb[:, it * D:(it + 1) * D],
                                          ctx.scr_f32[:, (it % 2) * D:(it % 2) * D + D],
                                          AF.Copy).then_inc(ctx.inc("scl"), 1)
            barrier_block(ctx)
            for ch in range(NTOK // 512):
                bidx = ch // 4
                l2c = (ch % 4) * 512
                with nc.Block() as block:
                    d0 = ctx.val("dma")
                    b0 = ctx.val("dmb")
                    p0 = ctx.val("pe")
                    v0 = ctx.val("vec")

                    @block.sync
                    def _(sync, ch=ch, bidx=bidx, l2c=l2c):
                        for ct in range(NFT):
                            for psub in range(2):
                                src = bass.AP(scr2.ap().tensor,
                                              bidx * (16 * 16 * 64 * 128)
                                              + (2 * ct + psub) * 8192
                                              + (l2c // 128) * 131072,
                                              [[128, 64], [131072, 4], [1, 128]])
                                dma(ctx, sync,
                                    rhs_sb[psub * 64:psub * 64 + 64,
                                           ct * 512:(ct + 1) * 512], src)

                    @block.scalar
                    def _(scalar, ch=ch):
                        for of in range(NFT):
                            scalar.wait_ge(ctx.h("vec"), v0 + of - 1)
                            dma(ctx, scalar,
                                ctx.scr_f32[:, (of % 2) * 512:(of % 2) * 512 + 512],
                                xres.ap()[of, :, ch * 512:(ch + 1) * 512], sem="dmb")
                            scalar.wait_ge(ctx.h("vec"), v0 + of + 1)
                            dma(ctx, scalar, xres.ap()[of, :, ch * 512:(ch + 1) * 512],
                                ctx.scr_f32[:, NTOK + (of % 2) * 512:NTOK + (of % 2) * 512 + 512],
                                sem="dmb")

                    @block.tensor
                    def _(tensor, ch=ch):
                        tensor.wait_ge(ctx.h("dma"), d0 + 16 * 2 * NFT)
                        for of in range(NFT):
                            pw = pw1 if of % 2 == 0 else pw2
                            if of >= 2:
                                tensor.wait_ge(ctx.h("vec"), v0 + of - 1)
                            for ct in range(NFT):
                                i = tensor.matmul(
                                    pw[:], wo_sb[:, ct * D + of * 128:ct * D + (of + 1) * 128],
                                    rhs_sb[:, ct * 512:(ct + 1) * 512],
                                    start=(ct == 0), stop=(ct == NFT - 1))
                            i.then_inc(ctx.inc("pe"), 1)

                    @block.vector
                    def _(vector, ch=ch):
                        for of in range(NFT):
                            pw = pw1 if of % 2 == 0 else pw2
                            vector.wait_ge(ctx.h("pe"), p0 + of + 1)
                            vector.wait_ge(ctx.h("dmb"), b0 + 16 * (2 * of + 1))
                            vector.scalar_tensor_tensor(
                                ctx.scr_f32[:, NTOK + (of % 2) * 512:NTOK + (of % 2) * 512 + 512],
                                pw[:], bcols[:, NFT + of:NFT + of + 1],
                                ctx.scr_f32[:, (of % 2) * 512:(of % 2) * 512 + 512],
                                ALU.add, ALU.add).then_inc(ctx.inc("vec"), 1)
                        wait_all(ctx, vector, "dmb")
            barrier_block(ctx)
    barrier_block(ctx)


def build_attna(ctx, xres, W, qraw_d, kraw_d, vraw_d, ks_d):
    nc = ctx.nc
    H, DH = 16, 64
    with contextlib.ExitStack() as es:
        es_xn = contextlib.ExitStack()
        xn_all = es_xn.enter_context(
            nc.sbuf_tensor("aa_xn", [128, NFT * NTOK], BF16, side="right"))
        xn = [xn_all[:, i * NTOK:(i + 1) * NTOK] for i in range(NFT)]
        build_layernorm(ctx, xres, xn)

        kv_stage = ctx.scr_bf[:, 0:NTOK]
        for wname, dscr in (("L1_attn_a_Wq", qraw_d),
                            ("L1_attn_a_Wk", kraw_d), ("L1_attn_a_Wv", vraw_d)):
            def evac_kv(of, ch, ps, eng, dscr=dscr):
                return eng.tensor_copy(
                    kv_stage[:, ch * 512:(ch + 1) * 512], ps[:])
            for of in range(NFT):
                build_proj(ctx, xn, W[wname], 1, evac_kv, of0=of, evac_eng="vec")
                with nc.Block() as block:
                    @block.scalar
                    def _(scalar, of=of, dscr=dscr):
                        dma(ctx, scalar, dscr.ap()[of], kv_stage, sem="dmb")
                        scalar.wait_ge(ctx.h("dmb"), ctx.val("dmb"))
        es_xn.close()

        for t in range(NFT):
            with nc.Block() as block:
                d0 = ctx.val("dma")
                v0 = ctx.val("vec")
                s0 = ctx.val("scl")
                qt = ctx.scr_bf[:, 0:NTOK]

                @block.sync
                def _(sync, t=t):
                    dma(ctx, sync, qt, qraw_d.ap()[t])

                @block.vector
                def _(vector, t=t):
                    vector.wait_ge(ctx.h("dma"), d0 + 16)
                    vector.tensor_scalar_min(ctx.scr_xr[:, 0:NTOK], qt, 0.0).then_inc(
                        ctx.inc("vec"), 1)
                    vector.wait_ge(ctx.h("scl"), s0 + 1)
                    vector.tensor_scalar_max(qt, qt, 0.0)
                    vector.drain()
                    vector.tensor_add(qt, qt, ctx.scr_xr[:, NTOK:2 * NTOK]).then_inc(
                        ctx.inc("vec"), 1)

                @block.scalar
                def _(scalar, t=t):
                    scalar.wait_ge(ctx.h("vec"), v0 + 1)
                    scalar.activation(ctx.scr_xr[:, NTOK:2 * NTOK],
                                      ctx.scr_xr[:, 0:NTOK],
                                      AF.Exp).then_inc(ctx.inc("scl"), 1)
                    scalar.wait_ge(ctx.h("vec"), v0 + 2)
                    dma(ctx, scalar, qraw_d.ap()[t], qt, sem="dmb")
                    scalar.wait_ge(ctx.h("dmb"), ctx.val("dmb"))
        barrier_block(ctx)

        with contextlib.ExitStack() as e2:
            ktm = e2.enter_context(nc.sbuf_tensor("aa_ktm", [128, D], BF16))
            vtm = e2.enter_context(nc.sbuf_tensor("aa_vtm", [128, D], BF16))
            kvsb = e2.enter_context(nc.sbuf_tensor("aa_kv", [64, 2 * D], F32))
            kssb = e2.enter_context(nc.sbuf_tensor("aa_ks", [1, 2 * D], F32))
            blkin = e2.enter_context(nc.sbuf_tensor("aa_blk", [128, 2 * D], BF16))
            tp1 = e2.enter_context(nc.psum_tensor("aa_tp1", [128, 1024], BF16))
            tp2 = e2.enter_context(nc.psum_tensor("aa_tp2", [128, 1024], BF16))
            kvp = e2.enter_context(nc.psum_tensor("aa_kvp", [64, 1024], F32))
            ksp = e2.enter_context(nc.psum_tensor("aa_ksp", [1, 1024], F32))
            with nc.Block() as block:
                @block.vector
                def _(vector):
                    vector.memset(kvsb[:], 0.0)
                    vector.memset(kssb[:], 0.0).then_inc(ctx.inc("vec"), 1)
            barrier_block(ctx)
            NTB = NTOK // 128
            for tb in range(NTB):
                bidx = tb // 16
                with nc.Block() as block:
                    q0 = [ctx.val("dq0"), ctx.val("dq1")]
                    p0 = ctx.val("pe")
                    v0 = ctx.val("vec")
                    s0 = ctx.val("scl")

                    @block.sync
                    def _(sync, tb=tb):
                        for t in range(NFT):
                            src = bass.AP(kraw_d.ap().tensor,
                                          t * 128 * NTOK + tb * 128,
                                          [[NTOK, 128], [1, 128]])
                            dma(ctx, sync, blkin[:, t * 128:(t + 1) * 128], src,
                                sem="dq0")
                        for t in range(NFT):
                            src = bass.AP(vraw_d.ap().tensor,
                                          t * 128 * NTOK + tb * 128,
                                          [[NTOK, 128], [1, 128]])
                            dma(ctx, sync, blkin[:, D + t * 128:D + (t + 1) * 128],
                                src, sem="dq1")

                    @block.tensor
                    def _(tensor, tb=tb):
                        tensor.wait_ge(ctx.h("dq0"), q0[0] + 16 * NFT)
                        for t in range(NFT):
                            tensor.transpose(tp1[:, t * 128:(t + 1) * 128],
                                             blkin[:, t * 128:(t + 1) * 128],
                                             ctx.ident[:])
                        tensor.wait_ge(ctx.h("dq1"), q0[1] + 16 * NFT)
                        for t in range(NFT):
                            i = tensor.transpose(tp2[:, t * 128:(t + 1) * 128],
                                                 blkin[:, D + t * 128:D + (t + 1) * 128],
                                                 ctx.ident[:])
                        i.then_inc(ctx.inc("pe"), 1)
                        tensor.wait_ge(ctx.h("vec"), v0 + 3)
                        for h in range(H):
                            i = tensor.matmul(kvp[:, h * 64:(h + 1) * 64],
                                              ktm[:, h * 64:(h + 1) * 64],
                                              vtm[:, h * 64:(h + 1) * 64],
                                              start=True, stop=True)
                        i.then_inc(ctx.inc("pe"), 1)
                        for hc in range(2):
                            i = tensor.matmul(ksp[:, hc * 512:(hc + 1) * 512],
                                              ctx.ones_col[:],
                                              ktm[:, hc * 512:(hc + 1) * 512],
                                              start=True, stop=True)
                        i.then_inc(ctx.inc("pe"), 1)

                    @block.vector
                    def _(vector, tb=tb, bidx=bidx):
                        vector.wait_ge(ctx.h("pe"), p0 + 1)
                        vector.tensor_copy(vtm[:], tp2[:]).then_inc(ctx.inc("vec"), 1)
                        vector.tensor_scalar_min(ctx.scr_xr[:, 0:D], tp1[:], 0.0).then_inc(
                            ctx.inc("vec"), 1)
                        vector.wait_ge(ctx.h("scl"), s0 + 1)
                        vector.tensor_scalar_max(ktm[:], tp1[:], 0.0)
                        vector.drain()
                        vector.tensor_add(ktm[:], ktm[:],
                                          ctx.scr_xr[:, NTOK:NTOK + D]).then_inc(
                            ctx.inc("vec"), 1)
                        vector.wait_ge(ctx.h("pe"), p0 + 2)
                        vector.tensor_add(kvsb[:, bidx * D:(bidx + 1) * D],
                                          kvsb[:, bidx * D:(bidx + 1) * D], kvp[:])
                        vector.drain().then_inc(ctx.inc("vec"), 1)
                        vector.wait_ge(ctx.h("pe"), p0 + 3)
                        vector.tensor_add(kssb[:, bidx * D:(bidx + 1) * D],
                                          kssb[:, bidx * D:(bidx + 1) * D], ksp[:])
                        vector.drain().then_inc(ctx.inc("vec"), 1)

                    @block.scalar
                    def _(scalar, tb=tb):
                        scalar.wait_ge(ctx.h("vec"), v0 + 2)
                        scalar.activation(ctx.scr_xr[:, NTOK:NTOK + D],
                                          ctx.scr_xr[:, 0:D],
                                          AF.Exp).then_inc(ctx.inc("scl"), 1)

            with nc.Block() as block:
                @block.scalar
                def _(scalar):
                    dma(ctx, scalar, ks_d.ap()[:], kssb[:], sem="dmb")
                    scalar.wait_ge(ctx.h("dmb"), ctx.val("dmb"))
            barrier_block(ctx)
            kspad = e2.enter_context(nc.sbuf_tensor("aa_kspad", [128, 2 * 128], BF16))
            kvb = e2.enter_context(nc.sbuf_tensor("aa_kvb", [64, 2 * D], BF16))
            z2b = e2.enter_context(nc.sbuf_tensor("aa_z2b", [16, NTOK], BF16))
            kscol = e2.enter_context(nc.sbuf_tensor("aa_kscol", [128, 16], F32))
            kv2 = e2.enter_context(nc.sbuf_tensor("aa_kv2", [128, 2 * NFT * 128], BF16))
            z2 = e2.enter_context(nc.sbuf_tensor("aa_z2", [16, NTOK], F32))
            with nc.Block() as block:
                d0 = ctx.val("dma")

                @block.sync
                def _(sync):
                    for bidx in range(B):
                        for t in range(NFT):
                            src = bass.AP(ks_d.ap().tensor, bidx * D + t * 128,
                                          [[1, 128], [1, 1]])
                            dma(ctx, sync, kscol[:, bidx * NFT + t:bidx * NFT + t + 1],
                                src)

                @block.vector
                def _(vector):
                    wait_all(ctx, vector, "dma")
                    vector.tensor_copy(kvb[:], kvsb[:])
                    vector.memset(kspad[:], 0.0)
                    vector.drain()
                    for bidx in range(B):
                        for h in range(H):
                            r0 = (h % 2) * 64
                            vector.tensor_copy(
                                kspad[r0:r0 + 64,
                                      bidx * 128 + (h // 2) * 16 + h:bidx * 128 + (h // 2) * 16 + h + 1],
                                kscol[r0:r0 + 64,
                                      bidx * NFT + h // 2:bidx * NFT + h // 2 + 1])
                    vector.engine_nop().then_inc(ctx.inc("vec"), 1)
            barrier_block(ctx)

            qblk = e2.enter_context(nc.sbuf_tensor("aa_qblk", [128, 2 * NFT * 512], BF16))
            with (
                nc.psum_tensor("aa_zp", [16, 512], F32) as zp,
                nc.psum_tensor("aa_k2p", [128, 128], F32) as k2p,
                nc.Block() as block,
            ):
                p0 = ctx.val("pe")
                v0 = ctx.val("vec")
                qz0 = [ctx.val("dq0"), ctx.val("dq1")]

                @block.sync
                def _(sync):
                    n = 0
                    for bidx in range(B):
                        for chk in range(T // 512):
                            if n >= 2:
                                sync.wait_ge(ctx.h("pe"), p0 + n - 1)
                            for t in range(NFT):
                                src = bass.AP(qraw_d.ap().tensor,
                                              t * 128 * NTOK + bidx * T + chk * 512,
                                              [[NTOK, 128], [1, 512]])
                                dma(ctx, sync,
                                    qblk[:, (n % 2) * 4096 + t * 512:(n % 2) * 4096 + (t + 1) * 512],
                                    src, sem=f"dq{n % 2}")
                            n += 1

                @block.tensor
                def _(tensor):
                    n = 0
                    qq0 = [ctx.val("dq0") - 0, ctx.val("dq1")]
                    for bidx in range(B):
                        for chk in range(T // 512):
                            if n >= 1:
                                tensor.wait_ge(ctx.h("vec"), v0 + n)
                            tensor.wait_ge(ctx.h(f"dq{n % 2}"),
                                           qz0[n % 2] + 16 * NFT * (n // 2 + 1))
                            for t in range(NFT):
                                i = tensor.matmul(
                                    zp[:], kspad[:, bidx * 128 + t * 16:bidx * 128 + (t + 1) * 16],
                                    qblk[:, (n % 2) * 4096 + t * 512:(n % 2) * 4096 + (t + 1) * 512],
                                    start=(t == 0), stop=(t == NFT - 1))
                            i.then_inc(ctx.inc("pe"), 1)
                            n += 1
                    for bidx in range(B):
                        for f in range(NFT):
                            tensor.wait_ge(ctx.h("vec"), v0 + n)
                            i = tensor.matmul(k2p[:, 0:64],
                                              ctx.ident[0:64, :],
                                              kvb[:, bidx * D + (2 * f) * 64:bidx * D + (2 * f) * 64 + 64],
                                              start=True, stop=True)
                            i = tensor.matmul(k2p[:, 64:128],
                                              ctx.selhi[:],
                                              kvb[:, bidx * D + (2 * f + 1) * 64:bidx * D + (2 * f + 1) * 64 + 64],
                                              start=True, stop=True)
                            i.then_inc(ctx.inc("pe"), 1)
                            n += 1

                @block.vector
                def _(vector):
                    n = 0
                    for bidx in range(B):
                        for chk in range(T // 512):
                            vector.wait_ge(ctx.h("pe"), p0 + n + 1)
                            vector.tensor_copy(
                                z2[:, bidx * T + chk * 512:bidx * T + (chk + 1) * 512],
                                zp[:]).then_inc(ctx.inc("vec"), 1)
                            n += 1
                    for bidx in range(B):
                        for f in range(NFT):
                            vector.wait_ge(ctx.h("pe"), p0 + n + 1)
                            vector.tensor_copy(
                                kv2[:, (bidx * NFT + f) * 128:(bidx * NFT + f + 1) * 128],
                                k2p[:]).then_inc(ctx.inc("vec"), 1)
                            n += 1
                    vector.tensor_scalar_add(z2[:], z2[:], 1e-6)
                    vector.drain()
                    vector.reciprocal(z2[:], z2[:])
                    vector.drain()
                    vector.tensor_copy(z2b[:], z2[:])
                    vector.drain().then_inc(ctx.inc("vec"), 1)
            barrier_block(ctx)

            with (
                nc.psum_tensor("aa_op", [128, 512], F32) as opp,
                nc.psum_tensor("aa_zb", [128, 512], F32) as zbp,
                nc.sbuf_tensor("aa_zbs", [128, 512], F32) as zbs,
                nc.sbuf_tensor("aa_ao", [128, NTOK], BF16) as ao,
                nc.sbuf_tensor("aa_qtile", [128, NTOK], BF16) as qtile,
            ):
                for f in range(NFT):
                    with nc.Block() as block:
                        p0 = ctx.val("pe")
                        v0 = ctx.val("vec")
                        d0 = ctx.val("dma")

                        @block.sync
                        def _(sync, f=f):
                            dma(ctx, sync, qtile[:], qraw_d.ap()[f])

                        @block.tensor
                        def _(tensor, f=f, d0=d0):
                            tensor.wait_ge(ctx.h("dma"), d0 + 16)
                            n = 0
                            for bidx in range(B):
                                for chk in range(T // 512):
                                    if n >= 1:
                                        tensor.wait_ge(ctx.h("vec"), v0 + 2 * n)
                                    tensor.matmul(
                                        zbp[:], ctx.ef_const[:, f * 128:(f + 1) * 128],
                                        z2b[:, bidx * T + chk * 512:bidx * T + (chk + 1) * 512],
                                        start=True, stop=True).then_inc(ctx.inc("pe"), 1)
                                    tensor.matmul(
                                        opp[:], kv2[:, (bidx * NFT + f) * 128:(bidx * NFT + f + 1) * 128],
                                        qtile[:, bidx * T + chk * 512:
                                              bidx * T + (chk + 1) * 512],
                                        start=True, stop=True).then_inc(ctx.inc("pe"), 1)
                                    n += 1

                        @block.vector
                        def _(vector, f=f):
                            n = 0
                            for bidx in range(B):
                                for chk in range(T // 512):
                                    c = slice(bidx * T + chk * 512,
                                              bidx * T + (chk + 1) * 512)
                                    vector.wait_ge(ctx.h("pe"), p0 + 2 * n + 1)
                                    vector.tensor_copy(zbs[:], zbp[:]).then_inc(
                                        ctx.inc("vec"), 1)
                                    vector.wait_ge(ctx.h("pe"), p0 + 2 * n + 2)
                                    vector.drain()
                                    vector.tensor_mul(ao[:, c], opp[:], zbs[:]).then_inc(
                                        ctx.inc("vec"), 1)
                                    n += 1

                        v1 = ctx.val("vec")

                        @block.scalar
                        def _(scalar, f=f):
                            scalar.wait_ge(ctx.h("vec"), v1)
                            dma(ctx, scalar, kraw_d.ap()[f], ao[:], sem="dmb")
                            scalar.wait_ge(ctx.h("dmb"), ctx.val("dmb"))
            barrier_block(ctx)

        ao_all = es.enter_context(
            nc.sbuf_tensor("aa_aoall", [128, NFT * NTOK], BF16, side="right"))
        with nc.Block() as block:
            d0 = ctx.val("dma")

            @block.sync
            def _(sync):
                for t in range(NFT):
                    dma(ctx, sync, ao_all[:, t * NTOK:(t + 1) * NTOK],
                        kraw_d.ap()[t])
                wait_all(ctx, sync, "dma")
        build_proj_residual(ctx, ao_all[:], NFT, W["L1_attn_a_Wo"],
                            W["L1_attn_a_bo"], xres)
    barrier_block(ctx)


def build_ffn1_prep(ctx, W):
    nc = ctx.nc
    ctx.f1_a1 = ctx._stack_append(nc.sbuf_tensor("f1_a1", [128, 32], F32))
    ctx.f1_a2 = ctx._stack_append(nc.sbuf_tensor("f1_a2", [128, NFT], F32))
    ctx.f1_b1 = ctx._stack_append(nc.sbuf_tensor("f1_b1", [128, 32], F32))
    ctx.f1_b2 = ctx._stack_append(nc.sbuf_tensor("f1_b2", [128, NFT], F32))
    wtile = ctx.scr_f32[:, 0:2 * D]
    with nc.Block() as block:
        q0 = [ctx.val("dq0"), ctx.val("dq1")]
        v0 = ctx.val("vec")

        @block.sync
        def _(sync):
            for ht in range(32):
                if ht >= 2:
                    sync.wait_ge(ctx.h("vec"), v0 + ht - 1)
                dma(ctx, sync, wtile[:, (ht % 2) * D:(ht % 2) * D + D],
                    W["L1_ffn_w1"][ht * 128:(ht + 1) * 128, :], sem=f"dq{ht % 2}")
            for ft in range(NFT):
                dma(ctx, sync, ctx.f1_b1[:, 0:32],
                    W["L1_ffn_b1"][0:0, :]) if False else None
            for bi in range(32):
                dma(ctx, sync, ctx.f1_b1[:, bi:bi + 1],
                    W["L1_ffn_b1"][bi * 128:(bi + 1) * 128, :])
            for bi in range(NFT):
                dma(ctx, sync, ctx.f1_b2[:, bi:bi + 1],
                    W["L1_ffn_b2"][bi * 128:(bi + 1) * 128, :])

        @block.vector
        def _(vector):
            for ht in range(32):
                vector.wait_ge(ctx.h(f"dq{ht % 2}"), q0[ht % 2] + 16 * (ht // 2 + 1))
                vector.reduce_sum(ctx.f1_a1[:, ht:ht + 1],
                                  wtile[:, (ht % 2) * D:(ht % 2) * D + D],
                                  axis=mybir.AxisListType.X,
                                  apply_absolute_value=True)
                vector.drain()
                vector.tensor_scalar_mul(ctx.f1_a1[:, ht:ht + 1],
                                         ctx.f1_a1[:, ht:ht + 1], 1.0 / D)
                vector.drain().then_inc(ctx.inc("vec"), 1)
    barrier_block(ctx)
    w2tile = ctx.scr_f32[:, 0:NTOK]
    for dt in range(NFT):
        with nc.Block() as block:
            d0 = ctx.val("dma")

            @block.sync
            def _(sync, dt=dt):
                dma(ctx, sync, w2tile, W["L1_ffn_w2"][dt * 128:(dt + 1) * 128, :])

            @block.vector
            def _(vector, dt=dt, d0=d0):
                vector.wait_ge(ctx.h("dma"), d0 + 16)
                vector.reduce_sum(ctx.f1_a2[:, dt:dt + 1], w2tile,
                                  axis=mybir.AxisListType.X,
                                  apply_absolute_value=True)
                vector.drain()
                vector.tensor_scalar_mul(ctx.f1_a2[:, dt:dt + 1],
                                         ctx.f1_a2[:, dt:dt + 1],
                                         1.0 / (4 * D))
                vector.drain().then_inc(ctx.inc("vec"), 1)
    barrier_block(ctx)


def build_ffn1(ctx, xres, W, h_dram):
    nc = ctx.nc
    with contextlib.ExitStack() as es:
        es_xn = contextlib.ExitStack()
        xn_all = es_xn.enter_context(
            nc.sbuf_tensor("f1_xn", [128, NFT * NTOK], BF16, side="right"))
        xn = [xn_all[:, i * NTOK:(i + 1) * NTOK] for i in range(NFT)]
        build_layernorm(ctx, xres, xn)

        hst = ctx.scr_bf[:, 0:NTOK]
        for ht in range(32):
            def evac_h(of, ch, ps, eng, ht=ht):
                return eng.activation(hst[:, ch * 512:(ch + 1) * 512], ps[:],
                                      AF.Gelu,
                                      scale=ctx.f1_a1[:, ht:ht + 1],
                                      bias=ctx.f1_b1[:, ht:ht + 1])
            build_proj(ctx, xn, W["L1_ffn_w1T"], 1, evac_h, of0=ht,
                       evac_eng="scl", conv_func=AF.Sign)
            with nc.Block() as block:
                @block.scalar
                def _(scalar, ht=ht):
                    dma(ctx, scalar, h_dram.ap()[ht], hst, sem="dmb")
                    scalar.wait_ge(ctx.h("dmb"), ctx.val("dmb"))
        es_xn.close()

        with contextlib.ExitStack() as e2:
            hblk = e2.enter_context(
                nc.sbuf_tensor("f1_hblk", [128, 32 * 512], BF16, side="right"))
            sgw = e2.enter_context(
                nc.sbuf_tensor("f1_sgw", [128, NFT * 32 * 128], BF16))
            wfst = e2.enter_context(nc.sbuf_tensor("f1_wf", [128, 2 * 128], F32))
            p1 = e2.enter_context(nc.psum_tensor("f1_p1", [128, 512], F32))
            p2 = e2.enter_context(nc.psum_tensor("f1_p2", [128, 512], F32))
            xf = ctx.scr_f32[:, 0:NTOK]
            of_out = ctx.scr_f32[:, NTOK:2 * NTOK]
            with nc.Block() as block:
                q0 = [ctx.val("dq0"), ctx.val("dq1")]
                g0 = ctx.val("gps")

                @block.sync
                def _(sync):
                    n = 0
                    for of in range(NFT):
                        for it in range(32):
                            if n >= 2:
                                sync.wait_ge(ctx.h("gps"), g0 + n - 1)
                            dma(ctx, sync,
                                wfst[:, (n % 2) * 128:(n % 2) * 128 + 128],
                                W["L1_ffn_w2T"][it * 128:(it + 1) * 128,
                                                of * 128:(of + 1) * 128],
                                sem=f"dq{n % 2}")
                            n += 1

                @block.scalar
                def _(scalar):
                    n = 0
                    for of in range(NFT):
                        for it in range(32):
                            scalar.wait_ge(ctx.h(f"dq{n % 2}"),
                                           q0[n % 2] + 16 * (n // 2 + 1))
                            idx = of * 32 + it
                            scalar.activation(
                                sgw[:, idx * 128:(idx + 1) * 128],
                                wfst[:, (n % 2) * 128:(n % 2) * 128 + 128],
                                AF.Sign).then_inc(ctx.inc("gps"), 1)
                            n += 1
            barrier_block(ctx)
            for ch in range(NTOK // 512):
                with nc.Block() as block:
                    b0 = ctx.val("dmg")
                    p0 = ctx.val("pe")
                    v0 = ctx.val("vec")

                    d0 = ctx.val("dma")

                    @block.sync
                    def _(sync, ch=ch):
                        for ht in range(32):
                            src = bass.AP(h_dram.ap().tensor,
                                          ht * 128 * NTOK + ch * 512,
                                          [[NTOK, 128], [1, 512]])
                            dma(ctx, sync, hblk[:, ht * 512:(ht + 1) * 512], src)

                    @block.gpsimd
                    def _(gpsimd, ch=ch):
                        for of in range(NFT):
                            gpsimd.wait_ge(ctx.h("vec"), v0 + max(of - 1, 0))
                            dma(ctx, gpsimd,
                                xf[:, (of % 2) * 512:(of % 2) * 512 + 512],
                                xres.ap()[of, :, ch * 512:(ch + 1) * 512], sem="dmg")
                            gpsimd.wait_ge(ctx.h("vec"), v0 + of + 1)
                            dma(ctx, gpsimd,
                                xres.ap()[of, :, ch * 512:(ch + 1) * 512],
                                of_out[:, (of % 2) * 512:(of % 2) * 512 + 512],
                                sem="dmg")

                    @block.tensor
                    def _(tensor, ch=ch, d0=d0):
                        tensor.wait_ge(ctx.h("dma"), d0 + 16 * 32)
                        for of in range(NFT):
                            ps = p1 if of % 2 == 0 else p2
                            if of >= 2:
                                tensor.wait_ge(ctx.h("vec"), v0 + of - 1)
                            for it in range(32):
                                idx = of * 32 + it
                                i = tensor.matmul(
                                    ps[:], sgw[:, idx * 128:(idx + 1) * 128],
                                    hblk[:, it * 512:(it + 1) * 512],
                                    start=(it == 0), stop=(it == 31))
                            i.then_inc(ctx.inc("pe"), 1)

                    @block.vector
                    def _(vector, ch=ch, b0=b0):
                        for of in range(NFT):
                            ps = p1 if of % 2 == 0 else p2
                            vector.wait_ge(ctx.h("pe"), p0 + of + 1)
                            vector.wait_ge(ctx.h("dmg"), b0 + 16 * (2 * of + 1))
                            c = slice((of % 2) * 512, (of % 2) * 512 + 512)
                            vector.tensor_scalar(of_out[:, c], ps[:],
                                                 ctx.f1_a2[:, of:of + 1],
                                                 ctx.f1_b2[:, of:of + 1],
                                                 ALU.mult, ALU.add)
                            vector.drain()
                            vector.tensor_add(of_out[:, c], of_out[:, c],
                                              xf[:, c]).then_inc(ctx.inc("vec"), 1)
                        wait_all(ctx, vector, "dmg")
    barrier_block(ctx)


def build_rwkv(ctx, xres, W, h2_d):
    nc = ctx.nc
    NG = NFT
    with contextlib.ExitStack() as es:
        if getattr(ctx, "debug", False):
            kw = {"kind": "ExternalOutput"}
        else:
            kw = {}
        xb_d = nc.dram_tensor("rw_xb", [NFT, 128, NTOK], F32, **kw)
        kvr_d = nc.dram_tensor("rw_kvr", [3 * NFT, 128, NTOK], F32, **kw)
        x1_d = nc.dram_tensor("rw_x1", [NFT, 128, NTOK], F32, **kw)
        rr_d = nc.dram_tensor("rw_rr", [NFT, 128, NTOK], F32, **kw)

        cols = es.enter_context(nc.sbuf_tensor("rw_cols", [128, 12 * NG + 1], F32))
        lncols = es.enter_context(nc.sbuf_tensor("rw_lnc", [128, 4 * NG], F32))
        es_tau = contextlib.ExitStack()
        tau = es_tau.enter_context(nc.sbuf_tensor("rw_tau", [128, T], F32))
        CW, CU, CTK, COTK, CTV, COTV, CTR, COTR, CCK, COCK, CCR, COCR = \
            [i * NG for i in range(12)]
        zcol = cols[:, 12 * NG:12 * NG + 1]

        def col(base, i):
            return cols[:, base + i:base + i + 1]

        with nc.Block() as block:
            d0 = ctx.val("dma")

            @block.sync
            def _(sync):
                dma(ctx, sync, tau[:], W["const_tau"][:])
                for gi, pname in ((CW, "L1_attn_b_time_decay"),
                                  (CU, "L1_attn_b_time_first"),
                                  (CTK, "L1_attn_b_tmk"),
                                  (CTV, "L1_attn_b_tmv"),
                                  (CTR, "L1_attn_b_tmr"),
                                  (CCK, "L1_attn_b_cmk"),
                                  (CCR, "L1_attn_b_cmr")):
                    for ft in range(NFT):
                        dma(ctx, sync, cols[:, gi + ft:gi + ft + 1],
                            W[pname][ft * 128:(ft + 1) * 128, :])
                for gi, pname in ((0, "L1_attn_b_g1"), (NG, "L1_attn_b_b1"),
                                  (2 * NG, "L1_attn_b_g2"),
                                  (3 * NG, "L1_attn_b_b2")):
                    for ft in range(NFT):
                        dma(ctx, sync, lncols[:, gi + ft:gi + ft + 1],
                            W[pname][ft * 128:(ft + 1) * 128, :])

            @block.vector
            def _(vector):
                vector.wait_ge(ctx.h("dma"), d0 + 16 * (1 + 7 * NFT + 4 * NFT))
                vector.memset(zcol, 0.0)
                vector.drain()
                for src, dst in ((CTK, COTK), (CTV, COTV), (CTR, COTR),
                                 (CCK, COCK), (CCR, COCR)):
                    vector.tensor_scalar(cols[:, dst:dst + NG],
                                         cols[:, src:src + NG], -1.0, 1.0,
                                         ALU.mult, ALU.add)
                vector.drain().then_inc(ctx.inc("vec"), 1)
        barrier_block(ctx)

        g1c = [lncols[:, i:i + 1] for i in range(NG)]
        b1c = [lncols[:, NG + i:NG + i + 1] for i in range(NG)]
        g2c = [lncols[:, 2 * NG + i:2 * NG + i + 1] for i in range(NG)]
        b2c = [lncols[:, 3 * NG + i:3 * NG + i + 1] for i in range(NG)]

        es_xn = contextlib.ExitStack()
        xn_all = es_xn.enter_context(
            nc.sbuf_tensor("rw_xn", [128, NFT * NTOK], BF16, side="right"))
        xn = [xn_all[:, i * NTOK:(i + 1) * NTOK] for i in range(NFT)]
        build_layernorm(ctx, xres, xn, fp32_dram=xb_d)
        build_layernorm(ctx, xb_d, xn, gain=g1c, bias=b1c)

        st32 = ctx.scr_f32b[:, 0:NTOK]
        for of in range(3 * NFT):
            def evac_kvr(o, ch, ps, eng):
                return eng.tensor_copy(st32[:, ch * 512:(ch + 1) * 512], ps[:])
            build_proj(ctx, xn, W["L1_attn_b_W_tm"], 1, evac_kvr, of0=of,
                       evac_eng="vec")
            with nc.Block() as block:
                @block.scalar
                def _(scalar, of=of):
                    dma(ctx, scalar, kvr_d.ap()[of], st32, sem="dmb")
                    scalar.wait_ge(ctx.h("dmb"), ctx.val("dmb"))
        es_xn.close()

        with contextlib.ExitStack() as eb:
            ED = eb.enter_context(nc.sbuf_tensor("rw_ed", [128, T], F32))
            P1 = eb.enter_context(nc.sbuf_tensor("rw_p1", [128, T], F32))
            S3 = eb.enter_context(nc.sbuf_tensor("rw_s3", [128, T], F32))
            WT = eb.enter_context(nc.sbuf_tensor("rw_wt", [128, T], F32))
            ONE = eb.enter_context(nc.sbuf_tensor("rw_one", [128, T], F32))
            B1 = eb.enter_context(nc.sbuf_tensor("rw_b1", [128, T + 1], F32))
            A1 = eb.enter_context(nc.sbuf_tensor("rw_a1", [128, T + 1], F32))
            KB = ctx.scr_f32[:, 0:T]
            VB = ctx.scr_f32[:, T:2 * T]
            RB = ctx.scr_f32[:, 2 * T:3 * T]
            XB = ctx.scr_f32[:, 3 * T:4 * T]
            S1 = ctx.scr_f32b[:, 0:T]
            S2 = ctx.scr_f32b[:, T:2 * T]
            with nc.Block() as block:
                @block.vector
                def _(vector):
                    vector.memset(ONE[:], 1.0)
                    vector.memset(ED[:, 0:1], 0.0)
                    vector.memset(B1[:, 0:1], -1e38)
                    vector.memset(A1[:, 0:1], 0.0)
                    vector.drain().then_inc(ctx.inc("vec"), 1)
            barrier_block(ctx)

            for ct in range(NFT):
                wcol = col(CW, ct)
                ucol = col(CU, ct)
                with nc.Block() as block:
                    @block.vector
                    def _(vector, wcol=wcol):
                        vector.tensor_scalar(WT[:], tau[:], wcol, zcol,
                                             ALU.mult, ALU.add)
                        vector.drain().then_inc(ctx.inc("vec"), 1)
                for b in range(B):
                    sl = slice(b * T, (b + 1) * T)
                    prog = []

                    def V(fn):
                        prog.append(("V", fn))

                    def S(fn):
                        prog.append(("S", fn))

                    def ts_mul(out, in_, c):
                        V(lambda e, out=out, in_=in_, c=c: e.tensor_scalar(
                            out, in_, c, zcol, ALU.mult, ALU.add))

                    def ts_addc(out, in_, c):
                        V(lambda e, out=out, in_=in_, c=c: e.tensor_scalar(
                            out, in_, c, zcol, ALU.add, ALU.add))

                    def tt(out, a, bb, op):
                        V(lambda e, out=out, a=a, bb=bb, op=op:
                          e.tensor_tensor(out, a, bb, op))

                    def sexp(x):
                        S(lambda e, x=x: e.activation(x, x, AF.Exp))

                    for tcol, ocol, buf in ((CTK, COTK, KB), (CTV, COTV, VB),
                                            (CTR, COTR, RB)):
                        tc, oc = col(tcol, ct), col(ocol, ct)
                        ts_mul(S3[:, 1:T], XB[:, 0:T - 1], oc)
                        ts_mul(S3[:, 0:1], XB[:, 0:1], oc)
                        ts_mul(buf, buf, tc)
                        tt(buf, buf, S3[:], ALU.add)
                    S(lambda e: e.activation(RB, RB, AF.Sigmoid))

                    tt(S1, KB, WT[:], ALU.subtract)
                    V(lambda e: e.tensor_tensor_scan(
                        P1[:], ONE[:], S1, -1e38, ALU.mult, ALU.max))
                    tt(ED[:, 1:T], P1[:, 0:T - 1], P1[:, 1:T], ALU.subtract)
                    S(lambda e: e.activation(ED[:, 1:T], ED[:, 1:T], AF.Exp))
                    tt(S2, S1, P1[:], ALU.subtract)
                    sexp(S2)
                    tt(P1[:], P1[:], WT[:], ALU.add)
                    V(lambda e: e.tensor_tensor_scan(
                        S3[:], ED[:], S2, 0.0, ALU.mult, ALU.add))
                    S(lambda e: e.activation(S3[:], S3[:], AF.Ln))
                    tt(B1[:, 1:T + 1], S3[:], P1[:], ALU.add)
                    ts_addc(S3[:], B1[:, 0:T], wcol)
                    tt(S3[:], S3[:], KB, ALU.max)
                    tt(S3[:], S3[:], P1[:], ALU.subtract)
                    sexp(S3[:])
                    V(lambda e: e.tensor_scalar_mul(S3[:], S3[:], 1e-8))
                    tt(S2, S2, S3[:], ALU.add)
                    V(lambda e: e.tensor_tensor_scan(
                        S3[:], ED[:], S2, 0.0, ALU.mult, ALU.add))
                    S(lambda e: e.activation(S3[:], S3[:], AF.Ln))
                    tt(B1[:, 1:T + 1], S3[:], P1[:], ALU.add)
                    ts_addc(S1, B1[:, 0:T], wcol)
                    tt(S1, S1, KB, ALU.max)
                    ts_addc(S2, B1[:, 0:T], wcol)
                    tt(S2, S2, S1, ALU.subtract)
                    sexp(S2)
                    tt(S3[:], KB, S1, ALU.subtract)
                    sexp(S3[:])
                    tt(S3[:], S3[:], VB, ALU.mult)
                    V(lambda e: e.tensor_tensor_scan(
                        A1[:, 1:T + 1], S2, S3[:], 0.0, ALU.mult, ALU.add))
                    ts_addc(KB, KB, ucol)
                    tt(S1, B1[:, 0:T], KB, ALU.max)
                    tt(S2, B1[:, 0:T], S1, ALU.subtract)
                    sexp(S2)
                    tt(KB, KB, S1, ALU.subtract)
                    S(lambda e: e.activation(KB, KB, AF.Exp))
                    tt(S1, S2, KB, ALU.add)
                    V(lambda e: e.tensor_scalar_add(S1, S1, 1e-8))
                    tt(S3[:], KB, VB, ALU.mult)
                    tt(S2, S2, A1[:, 0:T], ALU.mult)
                    tt(S2, S2, S3[:], ALU.add)
                    V(lambda e: e.reciprocal(S1, S1))
                    tt(VB, S2, S1, ALU.mult)
                    tt(VB, VB, RB, ALU.mult)
                    tt(VB, VB, XB, ALU.add)

                    nv_tot = sum(1 for t_, _ in prog if t_ == "V")
                    with nc.Block() as block:
                        d0 = ctx.val("dma")
                        v0 = ctx.val("vec")
                        s0 = ctx.val("scl")

                        @block.sync
                        def _(sync, ct=ct, sl=sl):
                            dma(ctx, sync, KB, kvr_d.ap()[ct, :, sl])
                            dma(ctx, sync, VB, kvr_d.ap()[NFT + ct, :, sl])
                            dma(ctx, sync, RB, kvr_d.ap()[2 * NFT + ct, :, sl])
                            dma(ctx, sync, XB, xb_d.ap()[ct, :, sl])

                        @block.vector
                        def _(vector, prog=prog, d0=d0, s0=s0):
                            vector.wait_ge(ctx.h("dma"), d0 + 16 * 4)
                            ns = 0
                            for eng, fn in prog:
                                if eng == "V":
                                    vector.wait_ge(ctx.h("scl"), s0 + ns)
                                    vector.drain()
                                    fn(vector).then_inc(ctx.inc("vec"), 1)
                                else:
                                    ns += 1

                        @block.scalar
                        def _(scalar, prog=prog, v0=v0, ct=ct, sl=sl,
                              nv_tot=nv_tot):
                            nv = 0
                            for eng, fn in prog:
                                if eng == "V":
                                    nv += 1
                                else:
                                    scalar.wait_ge(ctx.h("vec"), v0 + nv)
                                    fn(scalar).then_inc(ctx.inc("scl"), 1)
                            scalar.wait_ge(ctx.h("vec"), v0 + nv_tot)
                            dma(ctx, scalar, x1_d.ap()[ct, :, sl], VB,
                                sem="dmb")
                            scalar.wait_ge(ctx.h("dmb"), ctx.val("dmb"))
            barrier_block(ctx)
        es_tau.close()

        es_xn = contextlib.ExitStack()
        xn_all = es_xn.enter_context(
            nc.sbuf_tensor("rw_xn2", [128, NFT * NTOK], BF16, side="right"))
        xn2 = [xn_all[:, i * NTOK:(i + 1) * NTOK] for i in range(NFT)]
        build_layernorm(ctx, x1_d, xn2, gain=g2c, bias=b2c)

        with contextlib.ExitStack() as ec:
            u32 = ec.enter_context(nc.sbuf_tensor("rw_u32", [128, 1024], F32))
            Pm = ctx.scr_f32[:, 0:NTOK]
            x1t = ctx.scr_f32[:, NTOK:2 * NTOK]
            rrb = ctx.scr_f32b[:, 0:NTOK]
            h2st = ctx.scr_bf[:, 0:NTOK]
            for ft in range(NFT):
                ckc, ock = col(CCK, ft), col(COCK, ft)
                crc, ocr = col(CCR, ft), col(COCR, ft)
                with nc.Block() as block:
                    d0 = ctx.val("dma")
                    v0 = ctx.val("vec")

                    @block.sync
                    def _(sync, ft=ft):
                        dma(ctx, sync, x1t, x1_d.ap()[ft])

                    @block.vector
                    def _(vector, ft=ft, ock=ock, ocr=ocr, crc=crc, d0=d0):
                        vector.wait_ge(ctx.h("dma"), d0 + 16)
                        for b in range(B):
                            o = b * T
                            vector.tensor_scalar(
                                Pm[:, o + 1:o + T], x1t[:, o:o + T - 1],
                                ock, zcol, ALU.mult, ALU.add)
                            vector.tensor_scalar(
                                Pm[:, o:o + 1], x1t[:, o + T - 1:o + T],
                                ock, zcol, ALU.mult, ALU.add)
                            vector.tensor_scalar(
                                rrb[:, o + 1:o + T], x1t[:, o:o + T - 1],
                                ocr, zcol, ALU.mult, ALU.add)
                            vector.tensor_scalar(
                                rrb[:, o:o + 1], x1t[:, o + T - 1:o + T],
                                ocr, zcol, ALU.mult, ALU.add)
                        vector.drain()
                        vector.scalar_tensor_tensor(
                            rrb[:], xn2[ft], crc, rrb[:], ALU.mult, ALU.add)
                        vector.drain().then_inc(ctx.inc("vec"), 1)

                    @block.scalar
                    def _(scalar, ft=ft, v0=v0):
                        scalar.wait_ge(ctx.h("vec"), v0 + 1)
                        scalar.activation(rrb[:], rrb[:],
                                          AF.Sigmoid).then_inc(ctx.inc("scl"), 1)
                        scalar.wait_ge(ctx.h("scl"), ctx.val("scl"))
                        dma(ctx, scalar, rr_d.ap()[ft], rrb[:], sem="dmb")
                        scalar.wait_ge(ctx.h("dmb"), ctx.val("dmb"))
                for j in range(4):
                    ht = ft + NFT * j

                    def evac_kk(o, ch, ps, eng, ckc=ckc):
                        c = slice(ch * 512, (ch + 1) * 512)
                        cu = slice((ch % 2) * 512, (ch % 2) * 512 + 512)
                        eng.scalar_tensor_tensor(u32[:, cu], ps[:], ckc,
                                                 Pm[:, c], ALU.mult, ALU.add)
                        eng.drain()
                        eng.tensor_scalar_max(u32[:, cu], u32[:, cu], 0.0)
                        eng.drain()
                        return eng.tensor_tensor(h2st[:, c], u32[:, cu],
                                                 u32[:, cu], ALU.mult)
                    build_proj(ctx, xn2, W["L1_attn_b_W_cm"], 1, evac_kk,
                               of0=ht, evac_eng="vec")
                    with nc.Block() as block:
                        @block.scalar
                        def _(scalar, ht=ht):
                            dma(ctx, scalar, h2_d.ap()[ht], h2st, sem="dmb")
                            scalar.wait_ge(ctx.h("dmb"), ctx.val("dmb"))
        es_xn.close()

        with contextlib.ExitStack() as e2:
            hblk = e2.enter_context(
                nc.sbuf_tensor("rw_hblk", [128, 32 * 512], BF16, side="right"))
            cw = e2.enter_context(
                nc.sbuf_tensor("rw_cw", [128, NFT * 32 * 128], BF16))
            wfst = e2.enter_context(nc.sbuf_tensor("rw_wf", [128, 2 * 128], F32))
            p1 = e2.enter_context(nc.psum_tensor("rw_p1", [128, 512], F32))
            p2 = e2.enter_context(nc.psum_tensor("rw_p2", [128, 512], F32))
            xf = ctx.scr_f32[:, 0:1024]
            x1f = ctx.scr_f32[:, 1024:2048]
            rrf = ctx.scr_f32[:, 2048:3072]
            of_out = ctx.scr_f32[:, 3072:4096]
            with nc.Block() as block:
                q0 = [ctx.val("dq0"), ctx.val("dq1")]
                g0 = ctx.val("gps")

                @block.sync
                def _(sync):
                    n = 0
                    for of in range(NFT):
                        for it in range(32):
                            if n >= 2:
                                sync.wait_ge(ctx.h("gps"), g0 + n - 1)
                            dma(ctx, sync,
                                wfst[:, (n % 2) * 128:(n % 2) * 128 + 128],
                                W["L1_attn_b_W_cp"][it * 128:(it + 1) * 128,
                                                    of * 128:(of + 1) * 128],
                                sem=f"dq{n % 2}")
                            n += 1

                @block.scalar
                def _(scalar):
                    n = 0
                    for of in range(NFT):
                        for it in range(32):
                            scalar.wait_ge(ctx.h(f"dq{n % 2}"),
                                           q0[n % 2] + 16 * (n // 2 + 1))
                            idx = of * 32 + it
                            scalar.activation(
                                cw[:, idx * 128:(idx + 1) * 128],
                                wfst[:, (n % 2) * 128:(n % 2) * 128 + 128],
                                AF.Copy).then_inc(ctx.inc("gps"), 1)
                            n += 1
            barrier_block(ctx)
            for ch in range(NTOK // 512):
                with nc.Block() as block:
                    b0 = ctx.val("dmg")
                    p0 = ctx.val("pe")
                    v0 = ctx.val("vec")
                    d0 = ctx.val("dma")

                    @block.sync
                    def _(sync, ch=ch):
                        for ht in range(32):
                            src = bass.AP(h2_d.ap().tensor,
                                          ht * 128 * NTOK + ch * 512,
                                          [[NTOK, 128], [1, 512]])
                            dma(ctx, sync, hblk[:, ht * 512:(ht + 1) * 512], src)

                    @block.gpsimd
                    def _(gpsimd, ch=ch):
                        for of in range(NFT):
                            gpsimd.wait_ge(ctx.h("vec"), v0 + max(of - 1, 0))
                            c2 = slice((of % 2) * 512, (of % 2) * 512 + 512)
                            dma(ctx, gpsimd, xf[:, c2],
                                xres.ap()[of, :, ch * 512:(ch + 1) * 512],
                                sem="dmg")
                            dma(ctx, gpsimd, x1f[:, c2],
                                x1_d.ap()[of, :, ch * 512:(ch + 1) * 512],
                                sem="dmg")
                            dma(ctx, gpsimd, rrf[:, c2],
                                rr_d.ap()[of, :, ch * 512:(ch + 1) * 512],
                                sem="dmg")
                            gpsimd.wait_ge(ctx.h("vec"), v0 + of + 1)
                            dma(ctx, gpsimd,
                                xres.ap()[of, :, ch * 512:(ch + 1) * 512],
                                of_out[:, c2], sem="dmg")

                    @block.tensor
                    def _(tensor, ch=ch, d0=d0):
                        tensor.wait_ge(ctx.h("dma"), d0 + 16 * 32)
                        for of in range(NFT):
                            ps = p1 if of % 2 == 0 else p2
                            if of >= 2:
                                tensor.wait_ge(ctx.h("vec"), v0 + of - 1)
                            for it in range(32):
                                idx = of * 32 + it
                                i = tensor.matmul(
                                    ps[:],
                                    cw[:, idx * 128:(idx + 1) * 128],
                                    hblk[:, it * 512:(it + 1) * 512],
                                    start=(it == 0), stop=(it == 31))
                            i.then_inc(ctx.inc("pe"), 1)

                    @block.vector
                    def _(vector, ch=ch, b0=b0):
                        for of in range(NFT):
                            ps = p1 if of % 2 == 0 else p2
                            vector.wait_ge(ctx.h("pe"), p0 + of + 1)
                            vector.wait_ge(ctx.h("dmg"), b0 + 16 * (4 * of + 3))
                            c2 = slice((of % 2) * 512, (of % 2) * 512 + 512)
                            vector.tensor_tensor(of_out[:, c2], ps[:],
                                                 rrf[:, c2], ALU.mult)
                            vector.drain()
                            vector.tensor_add(of_out[:, c2], of_out[:, c2],
                                              xf[:, c2])
                            vector.drain()
                            vector.tensor_add(of_out[:, c2], of_out[:, c2],
                                              x1f[:, c2]).then_inc(
                                ctx.inc("vec"), 1)
                        wait_all(ctx, vector, "dmg")
    barrier_block(ctx)


class _UniqueNC:

    def __init__(self, nc):
        self._nc = nc
        self._n = 0

    def __getattr__(self, a):
        return getattr(self._nc, a)

    def sbuf_tensor(self, name, *a, **kw):
        self._n += 1
        return self._nc.sbuf_tensor(f"{name}_{self._n}", *a, **kw)

    def psum_tensor(self, name, *a, **kw):
        self._n += 1
        return self._nc.psum_tensor(f"{name}_{self._n}", *a, **kw)


def build_program(upto=99, debug=False):
    nc0 = bass.Bass("TRN2", target_bir_lowering=False,
                    detect_race_conditions=False)
    nc = _UniqueNC(nc0)
    ctx = Ctx(nc)
    ctx.debug = debug
    for s in ("dma", "dmb", "dmg", "pe", "vec", "scl", "gps", "dq0", "dq1"):
        ctx.new_sem(s)

    xin = nc.dram_tensor("xin", [B, D, T], F32, kind="ExternalInput").ap()
    xout = nc.dram_tensor("xout", [B, D, T], F32, kind="ExternalOutput").ap()
    xres = nc.dram_tensor("xres", [NFT, 128, NTOK], F32)
    W = make_weights_decl(nc)

    def glob(name, shape, dtype):
        hh = nc.sbuf_tensor(name, shape, dtype)
        tt = hh.__enter__()
        ctx._stack.append(hh)
        return tt
    ctx.ones_col = glob("c_ones_col", [128, 1], BF16)
    ctx.ones_row = glob("c_ones_row", [1, 128], F32)
    ctx.ident = glob("c_ident", [128, 128], BF16)
    ctx.utmask = glob("c_utmask", [64, 64], BF16)
    ctx.rep32 = glob("c_rep32", [32, 128], BF16)
    ctx.selhi = glob("c_selhi", [64, 128], BF16)
    ctx.ef_const = glob("c_ef", [16, NFT * 128], BF16)
    ctx.scr_f32 = glob("c_scrf32", [128, 2 * NTOK], F32)
    ctx.scr_f32b = glob("c_scrf32b", [128, NTOK], F32)
    ctx.scr_bf = glob("c_scrbf", [128, 2 * NTOK], BF16)
    ctx.scr_xr = glob("c_scrxr", [128, 2 * NTOK], BF16)
    ctx.stat = glob("c_stat", [1, 2 * T], F32)
    ctx.epscol = glob("c_epscol", [1, 1], F32)

    with nc.Block() as block:
        @block.sync
        def _(sync):
            dma(ctx, sync, ctx.ident[:], W["const_ident"][:])
            dma(ctx, sync, ctx.utmask[:], W["const_utmask"][:])
            dma(ctx, sync, ctx.rep32[:], W["const_rep32"][:])
            dma(ctx, sync, ctx.selhi[:], W["const_selhi"][:])
            dma(ctx, sync, ctx.ef_const[:], W["const_ef"][:])

        @block.vector
        def _(vector):
            vector.memset(ctx.ones_col[:], 1.0)
            vector.memset(ctx.ones_row[:], 1.0)
            vector.memset(ctx.epscol[:], LN_EPS)

    with nc.Block() as block:
        @block.sync
        def _(sync):
            for b in range(B):
                for ft in range(NFT):
                    dma(ctx, sync, xres.ap()[ft, :, b * T:(b + 1) * T],
                        xin[b, ft * 128:(ft + 1) * 128, :])
            wait_all(ctx, sync, "dma")

    if 1 <= upto < 20:
        with nc0.named_scope("gla"):
            build_gla(ctx, xres, W)
    if 1.5 <= upto < 20:
        build_ffn0_prep(ctx, W)
    if 1.8 <= upto < 20:
        with nc0.named_scope("ffn0a"):
            build_ffn0(ctx, xres, W, parts=(2 if upto >= 2 else 1))
    if upto == 30:
        xn_dram = nc.dram_tensor("xn_scr", [NFT, 128, NTOK], BF16)
        scr2 = nc.dram_tensor("scr2", [B, 16, 16, 64, 128], BF16)
        build_attnb(ctx, xres, W, xn_dram, scr2)
    if 2.2 <= upto < 30:
        xn_dram = nc.dram_tensor("xn_scr", [NFT, 128, NTOK], BF16)
        scr2 = nc.dram_tensor("scr2", [B, 16, 16, 64, 128], BF16)
        with nc0.named_scope("attnb"):
            build_attnb(ctx, xres, W, xn_dram, scr2,
                        parts=(4 if upto >= 3
                               else int((upto - 2) * 10 + 0.5) // 2))
    if 4 <= upto < 20:
        with nc0.named_scope("ffn0b"):
            build_ffn0(ctx, xres, W)
    if (5 <= upto < 20) or upto == 31:
        qraw_d = nc.dram_tensor("aa_qraw", [NFT, 128, NTOK], BF16)
        kraw_d = nc.dram_tensor("aa_kraw", [NFT, 128, NTOK], BF16)
        vraw_d = nc.dram_tensor("aa_vraw", [NFT, 128, NTOK], BF16)
        ks_d = nc.dram_tensor("aa_ksd", [1, 2 * D], F32)
        with nc0.named_scope("attna"):
            build_attna(ctx, xres, W, qraw_d, kraw_d, vraw_d, ks_d)
    if (5.5 <= upto < 20) or upto == 32:
        ctx.f1h_dram = nc.dram_tensor("f1_h", [32, 128, NTOK], BF16)
        build_ffn1_prep(ctx, W)
    if (6 <= upto < 20) or upto == 32:
        with nc0.named_scope("ffn1a"):
            build_ffn1(ctx, xres, W, ctx.f1h_dram)
    if (7 <= upto < 20) or upto == 33:
        if upto == 33:
            ctx.f1h_dram = nc.dram_tensor("f1_h", [32, 128, NTOK], BF16)
        with nc0.named_scope("rwkv"):
            build_rwkv(ctx, xres, W, ctx.f1h_dram)
    if 8 <= upto < 20:
        with nc0.named_scope("ffn1b"):
            build_ffn1(ctx, xres, W, ctx.f1h_dram)

    with nc.Block() as block:
        @block.sync
        def _(sync):
            for b in range(B):
                for ft in range(NFT):
                    dma(ctx, sync, xout[b, ft * 128:(ft + 1) * 128, :],
                        xres.ap()[ft, :, b * T:(b + 1) * T])
            wait_all(ctx, sync, "dma")
    return nc


def make_in_maps(inputs, upto=99):
    x = np.asarray(inputs["L0_attn_a_hidden_state"], dtype=np.float32)
    shared = {}
    for n in ("L0_attn_a_Wq", "L0_attn_a_bq", "L0_attn_a_Wk", "L0_attn_a_bk",
              "L0_attn_a_Wg", "L0_attn_a_bg", "L0_attn_a_Wv", "L0_attn_a_bv",
              "L0_attn_a_Wo", "L0_attn_a_bo"):
        a = np.asarray(inputs[n], dtype=np.float32)
        shared[n] = a.reshape(-1, 1) if a.ndim == 1 else np.ascontiguousarray(a)
    shared["L0_ffn_kernel"] = np.ascontiguousarray(np.asarray(inputs["L0_ffn_kernel"]))
    shared["L0_ffn_input_scale"] = np.asarray(inputs["L0_ffn_input_scale"], dtype=np.float32).reshape(1, 1)
    shared["L0_ffn_kernel_scale"] = np.asarray(inputs["L0_ffn_kernel_scale"], dtype=np.float32).reshape(1, 1)
    shared["L0_ffn_bias"] = np.ascontiguousarray(np.asarray(inputs["L0_ffn_bias"]).reshape(-1, 1))
    for n in ("L0_attn_b_Wq", "L0_attn_b_bq", "L0_attn_b_Wk", "L0_attn_b_bk",
              "L0_attn_b_Wv", "L0_attn_b_bv", "L0_attn_b_Wo", "L0_attn_b_bo"):
        a = np.asarray(inputs[n], dtype=np.float32)
        shared[n] = a.reshape(-1, 1) if a.ndim == 1 else np.ascontiguousarray(a)
    shared["L0_attn_b_key_param"] = np.ascontiguousarray(
        np.asarray(inputs["L0_attn_b_key_param"], dtype=np.float32)[0].T)
    shared["const_krow"] = np.ascontiguousarray(
        np.tile(np.arange(1, 33, dtype=np.float32)[None, :], (128, 1)))
    shared["const_rep32"] = np.ascontiguousarray(
        np.tile(np.eye(32, dtype=ml_dtypes.bfloat16), (1, 4)))
    selhi = np.zeros((64, 128), dtype=ml_dtypes.bfloat16)
    for dd in range(64):
        selhi[dd, 64 + dd] = 1
    shared["const_selhi"] = selhi
    ef = np.zeros((16, 8 * 128), dtype=ml_dtypes.bfloat16)
    for f in range(8):
        for p in range(128):
            ef[2 * f + p // 64, f * 128 + p] = 1
    shared["const_ef"] = ef
    for n in ("L1_attn_a_Wq", "L1_attn_a_Wk", "L1_attn_a_Wv", "L1_attn_a_Wo",
              "L1_attn_a_bo", "L1_ffn_w1", "L1_ffn_b1", "L1_ffn_w2", "L1_ffn_b2"):
        a = np.asarray(inputs[n], dtype=np.float32)
        shared[n] = a.reshape(-1, 1) if a.ndim == 1 else np.ascontiguousarray(a)
    shared["L1_ffn_w1T"] = np.ascontiguousarray(
        np.asarray(inputs["L1_ffn_w1"], dtype=np.float32).T)
    shared["L1_ffn_w2T"] = np.ascontiguousarray(
        np.asarray(inputs["L1_ffn_w2"], dtype=np.float32).T)
    shared["const_ident"] = np.eye(128, dtype=ml_dtypes.bfloat16)
    shared["const_utmask"] = np.ascontiguousarray(
        np.triu(np.ones((64, 64), dtype=ml_dtypes.bfloat16)))
    for n in ("L1_attn_b_time_decay", "L1_attn_b_time_first",
              "L1_attn_b_g1", "L1_attn_b_b1", "L1_attn_b_g2", "L1_attn_b_b2",
              "L1_attn_b_tmk", "L1_attn_b_tmv", "L1_attn_b_tmr",
              "L1_attn_b_cmk", "L1_attn_b_cmr"):
        shared[n] = np.ascontiguousarray(
            np.asarray(inputs[n], dtype=np.float32).reshape(-1, 1))
    for n in ("L1_attn_b_W_tm", "L1_attn_b_W_cm", "L1_attn_b_W_cp"):
        shared[n] = np.ascontiguousarray(np.asarray(inputs[n], dtype=np.float32))
    shared["const_tau"] = np.ascontiguousarray(
        np.tile(np.arange(T, dtype=np.float32)[None, :], (128, 1)))
    maps = []
    for c in range(8):
        m = dict(shared)
        m["xin"] = np.ascontiguousarray(x[B * c:B * (c + 1)].transpose(0, 2, 1))
        maps.append(m)
    return maps


def kernel(**inputs):
    nc = build_program(upto=10)
    maps = make_in_maps(inputs)
    res = run_bass_kernel_spmd(nc, maps, core_ids=list(range(8)))
    out = np.empty((16, T, D), dtype=np.float32)
    for c in range(8):
        out[B * c:B * (c + 1)] = res.results[c]["xout"].transpose(0, 2, 1)
    return out


# version 03 (final)
# speedup vs baseline: 1.3610x; 1.0101x over previous
import contextlib
import numpy as np
import ml_dtypes
import concourse.bass as bass
import concourse.mybir as mybir
from concourse.bass_utils import run_bass_kernel_spmd

F32 = mybir.dt.float32
BF16 = mybir.dt.bfloat16
FP8 = mybir.dt.float8e4
AF = mybir.ActivationFunctionType
ALU = mybir.AluOpType

B = 2
T = 2048
NTOK = B * T
D = 1024
NFT = D // 128
CHUNK = 64
NCH = T // CHUNK
LN_EPS = 1e-5
S_PAT = 32


class Ctx:
    def __init__(self, nc):
        self.nc = nc
        self.sem = {}
        self._stack = []

    def new_sem(self, name):
        h = self.nc.semaphore(name)
        handle = h.__enter__()
        self._stack.append(h)
        self.sem[name] = [handle, 0]

    def inc(self, name, by=1):
        s = self.sem[name]
        s[1] += by
        return s[0]

    def val(self, name):
        return self.sem[name][1]

    def h(self, name):
        return self.sem[name][0]

    def _stack_append(self, h):
        t = h.__enter__()
        self._stack.append(h)
        return t


def dma(ctx, q, out, in_, sem="dma"):
    q.dma_start(out=out, in_=in_).then_inc(ctx.inc(sem, 16), 16)
    return ctx.val(sem)


def wait_all(ctx, eng, *names):
    for n in names:
        eng.wait_ge(ctx.h(n), ctx.val(n))


def debug_dump(ctx, name, ap):
    if not getattr(ctx, "debug", False):
        return
    nc = ctx.nc
    shp = [p[1] for p in ap.ap]
    t = nc._nc.dram_tensor(f"dbg_{name}", shp, ap.tensor.dtype if hasattr(ap.tensor, 'dtype') else ap.dtype,
                           kind="ExternalOutput").ap()
    with nc.Block() as block:
        @block.sync
        def _(sync):
            wait_all(ctx, sync, "dma", "pe", "vec", "scl", "gps")
            dma(ctx, sync, t[:], ap)
            sync.wait_ge(ctx.h("dma"), ctx.val("dma"))


def barrier_block(ctx):
    nc = ctx.nc
    with nc.Block() as block:
        @block.sync
        def _(sync):
            wait_all(ctx, sync, "dma", "dmb", "dmg", "pe", "vec", "scl", "gps",
                     "dq0", "dq1")


def build_layernorm(ctx, src_dram, xn_tiles, gain=None, bias=None,
                    keep=None, xn_dram=None, fp32_dram=None):
    nc = ctx.nc
    xf = ctx.scr_f32[:, 0:NTOK]
    r_bc = ctx.scr_f32[:, NTOK:2 * NTOK]
    mr_bc = ctx.scr_f32b[:, 0:NTOK]
    stat = ctx.stat
    es_u = contextlib.ExitStack()
    if gain is not None:
        u_bc = es_u.enter_context(nc.sbuf_tensor("ln_u", [128, NTOK], F32))
    if fp32_dram is not None:
        x32buf = es_u.enter_context(nc.sbuf_tensor("ln_x32", [128, NTOK], F32))

    for half in range(2):
        hs = slice(half * T, (half + 1) * T)
        with (
            nc.psum_tensor("ln_ps1", [1, T], F32) as ps1,
            nc.psum_tensor("ln_ps2", [1, T], F32) as ps2,
            nc.Block() as block,
        ):
            q0 = [ctx.val("dq0"), ctx.val("dq1")]
            g0 = ctx.val("gps")
            s0 = ctx.val("scl")
            p0 = ctx.val("pe")
            xbf = ctx.scr_bf[:, 0:2 * T]
            sqbf = ctx.scr_bf[:, 2 * T:4 * T]

            @block.sync
            def _(sync):
                for ft in range(NFT):
                    if ft >= 2:
                        sync.wait_ge(ctx.h("gps"), g0 + ft - 1)
                        sync.wait_ge(ctx.h("scl"), s0 + ft - 1)
                    dma(ctx, sync, xf[:, (ft % 2) * T:(ft % 2) * T + T],
                        src_dram.ap()[ft, :, hs], sem=f"dq{ft % 2}")

            @block.vector
            def _(vector):
                for ft in range(NFT):
                    vector.wait_ge(ctx.h(f"dq{ft % 2}"), q0[ft % 2] + 16 * (ft // 2 + 1))
                    if ft >= 2:
                        vector.wait_ge(ctx.h("pe"), p0 + ft - 1)
                    vector.tensor_copy(xbf[:, (ft % 2) * T:(ft % 2) * T + T],
                                       xf[:, (ft % 2) * T:(ft % 2) * T + T]).then_inc(
                        ctx.inc("gps"), 1)

            @block.scalar
            def _(scalar):
                for ft in range(NFT):
                    scalar.wait_ge(ctx.h(f"dq{ft % 2}"), q0[ft % 2] + 16 * (ft // 2 + 1))
                    if ft >= 2:
                        scalar.wait_ge(ctx.h("pe"), p0 + ft - 1)
                    scalar.activation(sqbf[:, (ft % 2) * T:(ft % 2) * T + T],
                                      xf[:, (ft % 2) * T:(ft % 2) * T + T],
                                      AF.Square).then_inc(ctx.inc("scl"), 1)

            @block.tensor
            def _(tensor):
                for ft in range(NFT):
                    tensor.wait_ge(ctx.h("gps"), g0 + ft + 1)
                    tensor.wait_ge(ctx.h("scl"), s0 + ft + 1)
                    o = (ft % 2) * T
                    for ch in range(T // 512):
                        c = slice(ch * 512, (ch + 1) * 512)
                        co = slice(o + ch * 512, o + (ch + 1) * 512)
                        tensor.matmul(ps1[:, c], ctx.ones_col[:], xbf[:, co],
                                      start=(ft == 0), stop=(ft == NFT - 1))
                        i = tensor.matmul(ps2[:, c], ctx.ones_col[:], sqbf[:, co],
                                          start=(ft == 0), stop=(ft == NFT - 1))
                    i.then_inc(ctx.inc("pe"), 1)

            @block.vector
            def _(vector):
                vector.wait_ge(ctx.h("pe"), ctx.val("pe"))
                m = stat[:, 0:T]
                q = stat[:, T:2 * T]
                vector.tensor_scalar_mul(m, ps1[:], 1.0 / D)
                vector.tensor_scalar_mul(q, ps2[:], 1.0 / D)
                vector.drain()
                vector.tensor_mul(ps1[:], m, m)
                vector.drain()
                vector.tensor_sub(q, q, ps1[:]).then_inc(ctx.inc("vec"), 1)

            @block.scalar
            def _(scalar):
                scalar.wait_ge(ctx.h("vec"), ctx.val("vec"))
                scalar.activation(stat[:, T:2 * T], stat[:, T:2 * T],
                                  AF.Sqrt, bias=ctx.epscol[:]).then_inc(ctx.inc("scl"), 1)

            @block.vector
            def _(vector):
                vector.wait_ge(ctx.h("scl"), ctx.val("scl"))
                vector.reciprocal(stat[:, T:2 * T], stat[:, T:2 * T])
                vector.drain()
                vector.tensor_mul(stat[:, 0:T], stat[:, 0:T],
                                  stat[:, T:2 * T]).then_inc(ctx.inc("vec"), 1)

        with (
            nc.psum_tensor("ln_pb1", [128, T], F32) as pb1,
            nc.psum_tensor("ln_pb2", [128, T], F32) as pb2,
            nc.Block() as block,
        ):
            pp0 = ctx.val("pe")

            @block.tensor
            def _(tensor):
                tensor.wait_ge(ctx.h("vec"), ctx.val("vec"))
                for ch in range(T // 512):
                    c = slice(ch * 512, (ch + 1) * 512)
                    tensor.matmul(pb1[:, c], ctx.ones_row[:],
                                  stat[:, T + ch * 512:T + (ch + 1) * 512],
                                  start=True, stop=True)
                    i = tensor.matmul(pb2[:, c], ctx.ones_row[:],
                                      stat[:, ch * 512:(ch + 1) * 512],
                                      start=True, stop=True)
                i.then_inc(ctx.inc("pe"), 1)

            @block.vector
            def _(vector):
                vector.wait_ge(ctx.h("pe"), pp0 + 1)
                vector.tensor_copy(r_bc[:, hs], pb1[:])
                vector.tensor_copy(mr_bc[:, hs], pb2[:]).then_inc(ctx.inc("vec"), 1)

    with nc.Block() as block:
        d0 = ctx.val("dma")
        b0 = ctx.val("dmb")
        v0 = ctx.val("vec")

        @block.sync
        def _(sync):
            for ft in range(NFT):
                if ft >= 1:
                    sync.wait_ge(ctx.h("vec"), v0 + 2 * ft - 1)
                dma(ctx, sync, xf, src_dram.ap()[ft])

        v0 = ctx.val("vec")

        @block.vector
        def _(vector):
            for ft in range(NFT):
                vector.wait_ge(ctx.h("dma"), d0 + 16 * (ft + 1))
                nst = (1 if xn_dram is not None else 0) + \
                      (1 if fp32_dram is not None else 0)
                if nst and ft >= 1:
                    vector.wait_ge(ctx.h("dmb"), b0 + 16 * nst * ft)
                vector.tensor_mul(ctx.scr_xr[:, 0:NTOK], xf, r_bc).then_inc(
                    ctx.inc("vec"), 1)
                if gain is None:
                    if fp32_dram is not None:
                        vector.tensor_sub(x32buf[:], ctx.scr_xr[:, 0:NTOK], mr_bc)
                        vector.drain()
                        vector.tensor_copy(xn_tiles[ft][:], x32buf[:]).then_inc(
                            ctx.inc("vec"), 1)
                    else:
                        vector.tensor_sub(xn_tiles[ft][:], ctx.scr_xr[:, 0:NTOK],
                                          mr_bc).then_inc(ctx.inc("vec"), 1)
                else:
                    vector.tensor_scalar(u_bc[:], mr_bc, gain[ft], bias[ft],
                                         ALU.mult, ALU.subtract)
                    vector.scalar_tensor_tensor(
                        xn_tiles[ft][:], ctx.scr_xr[:, 0:NTOK], gain[ft], u_bc[:],
                        ALU.mult, ALU.subtract).then_inc(ctx.inc("vec"), 1)

        if xn_dram is not None or fp32_dram is not None:
            @block.scalar
            def _(scalar):
                for ft in range(NFT):
                    scalar.wait_ge(ctx.h("vec"), v0 + 2 * (ft + 1))
                    if xn_dram is not None:
                        dma(ctx, scalar, xn_dram.ap()[ft], xn_tiles[ft][:],
                            sem="dmb")
                    if fp32_dram is not None:
                        dma(ctx, scalar, fp32_dram.ap()[ft], x32buf[:],
                            sem="dmb")

    es_u.close()
    barrier_block(ctx)


def build_proj(ctx, xn_tiles, w_dram, out_nf, evac, nin_tiles=NFT,
               of0=0, evac_eng="vec", ntok=NTOK, conv_func=None):
    nc = ctx.nc
    nch = ntok // 512
    with (
        nc.sbuf_tensor("pj_wf", [128, 2 * 128], F32) as wf,
        nc.sbuf_tensor("pj_w", [128, 2 * nin_tiles * 128], BF16) as wb,
        nc.psum_tensor("pj_p1", [128, 512], F32) as p1,
        nc.psum_tensor("pj_p2", [128, 512], F32) as p2,
        nc.Block() as block,
    ):
        q0 = [ctx.val("dq0"), ctx.val("dq1")]
        g0 = ctx.val("gps")
        p0 = ctx.val("pe")
        e0 = ctx.val(evac_eng)

        @block.sync
        def _(sync):
            n = 0
            for of in range(out_nf):
                for it in range(nin_tiles):
                    if n >= 2:
                        sync.wait_ge(ctx.h("gps"), g0 + n - 1)
                    dma(ctx, sync, wf[:, (n % 2) * 128:(n % 2) * 128 + 128],
                        w_dram[it * 128:(it + 1) * 128,
                               (of0 + of) * 128:(of0 + of + 1) * 128],
                        sem=f"dq{n % 2}")
                    n += 1

        @block.scalar
        def _(scalar):
            n = 0
            for of in range(out_nf):
                for it in range(nin_tiles):
                    scalar.wait_ge(ctx.h(f"dq{n % 2}"), q0[n % 2] + 16 * (n // 2 + 1))
                    if of >= 2 and it == 0:
                        scalar.wait_ge(ctx.h("pe"), p0 + (of - 1) * nch)
                    half = (of % 2) * nin_tiles
                    scalar.activation(
                        wb[:, (half + it) * 128:(half + it + 1) * 128],
                        wf[:, (n % 2) * 128:(n % 2) * 128 + 128],
                        conv_func if conv_func is not None else AF.Copy).then_inc(
                        ctx.inc("gps"), 1)
                    n += 1

        @block.tensor
        def _(tensor):
            n = 0
            for of in range(out_nf):
                tensor.wait_ge(ctx.h("gps"), g0 + (of + 1) * nin_tiles)
                half = (of % 2) * nin_tiles
                for ch in range(nch):
                    ps = p1 if n % 2 == 0 else p2
                    if n >= 2:
                        tensor.wait_ge(ctx.h(evac_eng), e0 + n - 1)
                    for it in range(nin_tiles):
                        i = tensor.matmul(
                            ps[:], wb[:, (half + it) * 128:(half + it + 1) * 128],
                            xn_tiles[it][:, ch * 512:(ch + 1) * 512],
                            start=(it == 0), stop=(it == nin_tiles - 1))
                    i.then_inc(ctx.inc("pe"), 1)
                    n += 1

        def emit_evac(eng):
            n = 0
            for of in range(out_nf):
                for ch in range(nch):
                    ps = p1 if n % 2 == 0 else p2
                    eng.wait_ge(ctx.h("pe"), p0 + n + 1)
                    i = evac(of, ch, ps, eng)
                    i.then_inc(ctx.inc(evac_eng), 1)
                    n += 1

        if evac_eng == "vec":
            @block.vector
            def _(vector):
                emit_evac(vector)
        else:
            @block.scalar
            def _(scalar):
                emit_evac(scalar)

    barrier_block(ctx)



def make_weights_decl(nc):
    W = {}
    SP = {
        "L0_attn_a_Wq": ([D, 256], F32), "L0_attn_a_bq": ([256, 1], F32),
        "L0_attn_a_Wk": ([D, 256], F32), "L0_attn_a_bk": ([256, 1], F32),
        "L0_attn_a_Wg": ([D, 256], F32), "L0_attn_a_bg": ([256, 1], F32),
        "L0_attn_a_Wv": ([D, 256], F32), "L0_attn_a_bv": ([256, 1], F32),
        "L0_attn_a_Wo": ([256, D], F32), "L0_attn_a_bo": ([D, 1], F32),
        "const_ident": ([128, 128], BF16),
        "const_utmask": ([64, 64], BF16),
        "L0_attn_b_key_param": ([D, S_PAT], F32),
        "L0_attn_b_Wq": ([D, D], F32), "L0_attn_b_bq": ([D, 1], F32),
        "L0_attn_b_Wk": ([D, D], F32), "L0_attn_b_bk": ([D, 1], F32),
        "L0_attn_b_Wv": ([D, D], F32), "L0_attn_b_bv": ([D, 1], F32),
        "L0_attn_b_Wo": ([D, D], F32), "L0_attn_b_bo": ([D, 1], F32),
        "const_krow": ([128, 32], F32),
        "const_selhi": ([64, 128], BF16),
        "const_ef": ([16, NFT * 128], BF16),
        "L1_attn_a_Wq": ([D, D], F32), "L1_attn_a_Wk": ([D, D], F32),
        "L1_attn_a_Wv": ([D, D], F32), "L1_attn_a_Wo": ([D, D], F32),
        "L1_attn_a_bo": ([D, 1], F32),
        "L1_ffn_w1": ([4 * D, D], F32), "L1_ffn_w1T": ([D, 4 * D], F32),
        "L1_ffn_b1": ([4 * D, 1], F32),
        "L1_ffn_w2": ([D, 4 * D], F32), "L1_ffn_w2T": ([4 * D, D], F32),
        "L1_ffn_b2": ([D, 1], F32),
        "const_rep32": ([32, 128], BF16),
        "const_tau": ([128, T], F32),
        "L1_attn_b_time_decay": ([D, 1], F32),
        "L1_attn_b_time_first": ([D, 1], F32),
        "L1_attn_b_W_tm": ([D, 3 * D], F32),
        "L1_attn_b_g1": ([D, 1], F32), "L1_attn_b_b1": ([D, 1], F32),
        "L1_attn_b_tmk": ([D, 1], F32), "L1_attn_b_tmv": ([D, 1], F32),
        "L1_attn_b_tmr": ([D, 1], F32),
        "L1_attn_b_W_cm": ([D, 4 * D], F32),
        "L1_attn_b_W_cp": ([4 * D, D], F32),
        "L1_attn_b_g2": ([D, 1], F32), "L1_attn_b_b2": ([D, 1], F32),
        "L1_attn_b_cmk": ([D, 1], F32), "L1_attn_b_cmr": ([D, 1], F32),
        "L0_ffn_kernel": ([D, D], BF16),
        "L0_ffn_input_scale": ([1, 1], F32),
        "L0_ffn_kernel_scale": ([1, 1], F32),
        "L0_ffn_bias": ([D, 1], BF16),
    }
    for n, (sh, dt) in SP.items():
        W[n] = nc.dram_tensor(n, sh, dt, kind="ExternalInput").ap()
    return W


def build_gla(ctx, xres, W):
    nc = ctx.nc
    KD = 256
    NKT = 2

    with contextlib.ExitStack() as es:
        es_xn = contextlib.ExitStack()
        xn_all = es_xn.enter_context(
            nc.sbuf_tensor("g_xn", [128, NFT * NTOK], BF16, side="right"))
        xn = [xn_all[:, i * NTOK:(i + 1) * NTOK] for i in range(NFT)]
        build_layernorm(ctx, xres, xn)
        debug_dump(ctx, "xn0", xn[0])
        q_t = es.enter_context(nc.sbuf_tensor("g_q", [128, NKT * NTOK], BF16))
        k_t = es.enter_context(nc.sbuf_tensor("g_k", [128, NKT * NTOK], BF16))
        v_t = ctx.scr_xr[:, 0:NKT * NTOK]
        lg_t = ctx.scr_f32b[:, 0:NTOK]
        gbias = es.enter_context(nc.sbuf_tensor("g_bias", [128, 4 * NKT], F32))

        with nc.Block() as block:
            @block.sync
            def _(sync):
                for wi, bn in enumerate(["L0_attn_a_bq", "L0_attn_a_bk",
                                         "L0_attn_a_bg", "L0_attn_a_bv"]):
                    for kt in range(NKT):
                        dma(ctx, sync, gbias[:, wi * NKT + kt:wi * NKT + kt + 1],
                            W[bn][kt * 128:(kt + 1) * 128, :])
                wait_all(ctx, sync, "dma")


        def evac_q(of, ch, ps, eng):
            sl = (slice(None), slice(of * NTOK + ch * 512, of * NTOK + (ch + 1) * 512))
            return eng.tensor_scalar_add(q_t[sl], ps[:],
                                         gbias[:, 0 * NKT + of:0 * NKT + of + 1])
        build_proj(ctx, xn, W["L0_attn_a_Wq"], NKT, evac_q, evac_eng="vec")

        def evac_k(of, ch, ps, eng):
            sl = (slice(None), slice(of * NTOK + ch * 512, of * NTOK + (ch + 1) * 512))
            return eng.activation(k_t[sl], ps[:], AF.Sigmoid,
                                  bias=gbias[:, 1 * NKT + of:1 * NKT + of + 1])
        build_proj(ctx, xn, W["L0_attn_a_Wk"], NKT, evac_k, evac_eng="scl")

        def evac_v(of, ch, ps, eng):
            sl = (slice(None), slice(of * NTOK + ch * 512, of * NTOK + (ch + 1) * 512))
            return eng.tensor_scalar_add(v_t[sl], ps[:],
                                         gbias[:, 3 * NKT + of:3 * NKT + of + 1])
        build_proj(ctx, xn, W["L0_attn_a_Wv"], NKT, evac_v, evac_eng="vec")
        debug_dump(ctx, "v_early", v_t)
        debug_dump(ctx, "xn3", xn[3])
        debug_dump(ctx, "xn7", xn[7])

        debug_dump(ctx, "q", q_t[:])
        debug_dump(ctx, "k", k_t[:])
        debug_dump(ctx, "v", v_t)

        qt_t = q_t
        kt_t = k_t
        kh_t = ctx.scr_bf[:, 0:NKT * NTOK]
        cs_sl = ctx.scr_f32[:, 0:NTOK]
        e_sl = ctx.scr_f32[:, NTOK:2 * NTOK]
        z64 = es.enter_context(nc.sbuf_tensor("g_z64", [128, CHUNK], F32))
        ec_t = es.enter_context(nc.sbuf_tensor("g_ec", [128, NKT * B * NCH], F32))
        with nc.Block() as block:
            @block.vector
            def _(vector):
                vector.memset(z64[:], 0.0).then_inc(ctx.inc("vec"), 1)
        barrier_block(ctx)
        for kt in range(NKT):
            def evac_g(of, ch, ps, eng, kt=kt):
                sl = (slice(None), slice(ch * 512, (ch + 1) * 512))
                eng.activation(lg_t[sl], ps[:], AF.Sigmoid,
                               bias=gbias[:, 2 * NKT + kt:2 * NKT + kt + 1])
                return eng.activation(lg_t[sl], lg_t[sl], AF.Ln)
            build_proj(ctx, xn, W["L0_attn_a_Wg"], 1, evac_g, of0=kt,
                       evac_eng="scl")
            with nc.Block() as block:
                @block.vector
                def _(vector, kt=kt):
                    for bc in range(B * NCH):
                        sl = (slice(None), slice(bc * CHUNK, (bc + 1) * CHUNK))
                        vector.tensor_tensor_scan(cs_sl[sl], lg_t[sl], z64[:],
                                                  0.0, ALU.add, ALU.add)
                    vector.tensor_scalar_max(cs_sl, cs_sl, -85.0).then_inc(
                        ctx.inc("vec"), 1)

                @block.scalar
                def _(scalar, kt=kt):
                    scalar.wait_ge(ctx.h("vec"), ctx.val("vec"))
                    scalar.activation(e_sl, cs_sl, AF.Exp).then_inc(ctx.inc("scl"), 1)

                @block.vector
                def _(vector, kt=kt):
                    vector.wait_ge(ctx.h("scl"), ctx.val("scl"))
                    sl = (slice(None), slice(kt * NTOK, (kt + 1) * NTOK))
                    vector.tensor_mul(qt_t[sl], q_t[sl], e_sl)
                    cs3 = cs_sl.rearrange("p (o c) -> p o c", c=CHUNK)
                    last = cs_sl[:, CHUNK - 1:NTOK:CHUNK]
                    lastb = bass.AP(last.tensor, last.offset,
                                    [list(pp) for pp in last.ap] + [[0, CHUNK]])
                    e3 = e_sl.rearrange("p (o c) -> p o c", c=CHUNK)
                    vector.tensor_sub(e3[:], lastb, cs3[:]).then_inc(
                        ctx.inc("vec"), 1)

                @block.scalar
                def _(scalar, kt=kt):
                    scalar.wait_ge(ctx.h("vec"), ctx.val("vec"))
                    last = cs_sl[:, CHUNK - 1:NTOK:CHUNK]
                    scalar.activation(ec_t[:, kt * B * NCH:(kt + 1) * B * NCH],
                                      last, AF.Exp)
                    scalar.activation(e_sl, e_sl, AF.Exp).then_inc(
                        ctx.inc("scl"), 1)

                @block.vector
                def _(vector, kt=kt):
                    vector.wait_ge(ctx.h("scl"), ctx.val("scl"))
                    sl = (slice(None), slice(kt * NTOK, (kt + 1) * NTOK))
                    vector.tensor_mul(kh_t[sl], k_t[sl], e_sl).then_inc(
                        ctx.inc("vec"), 1)

                @block.scalar
                def _(scalar, kt=kt):
                    scalar.wait_ge(ctx.h("vec"), ctx.val("vec"))
                    scalar.activation(e_sl, cs_sl, AF.Exp,
                                      scale=-1.0).then_inc(ctx.inc("scl"), 1)

                @block.vector
                def _(vector, kt=kt):
                    vector.wait_ge(ctx.h("scl"), ctx.val("scl"))
                    sl = (slice(None), slice(kt * NTOK, (kt + 1) * NTOK))
                    vector.tensor_mul(kt_t[sl], k_t[sl], e_sl).then_inc(
                        ctx.inc("vec"), 1)
            barrier_block(ctx)
        debug_dump(ctx, "lg", lg_t)
        debug_dump(ctx, "cs", cs_sl)
        debug_dump(ctx, "esl", e_sl)
        debug_dump(ctx, "qt", qt_t[:])
        debug_dump(ctx, "kt", kt_t[:])
        debug_dump(ctx, "kh", kh_t)
        debug_dump(ctx, "ec", ec_t[:])
        es_xn.close()
        if True:
            v_tm = es.enter_context(nc.sbuf_tensor("g_vtm", [64, (NTOK // CHUNK) * KD], BF16))
            kh_tm = es.enter_context(nc.sbuf_tensor("g_khtm", [64, (NTOK // CHUNK) * KD], BF16))
            state = es.enter_context(nc.sbuf_tensor("g_state", [128, NKT * B * KD], F32))
            state_bf = es.enter_context(nc.sbuf_tensor("g_statebf", [128, NKT * B * KD], BF16))
            a_sm = es.enter_context(nc.sbuf_tensor("g_asm", [64, CHUNK], BF16))
            gout = ctx.scr_bf[:, 0:NKT * NTOK]
            if True:
                with (
                    nc.psum_tensor("g_pt1", [64, 128], BF16) as pt1,
                    nc.psum_tensor("g_pt2", [64, 128], BF16) as pt2,
                    nc.Block() as block,
                ):
                    p0 = ctx.val("pe")
                    v0 = ctx.val("vec")
                    njob = 2 * NKT * (NTOK // CHUNK)

                    @block.tensor
                    def _(tensor):
                        n = 0
                        for src in (v_t, kh_t):
                            for kt in range(NKT):
                                for c in range(NTOK // CHUNK):
                                    ps = pt1 if n % 2 == 0 else pt2
                                    if n >= 2:
                                        tensor.wait_ge(ctx.h("vec"), v0 + n - 1)
                                    tensor.transpose(
                                        ps[:, :],
                                        src[:, kt * NTOK + c * CHUNK:kt * NTOK + (c + 1) * CHUNK],
                                        ctx.ident[:]).then_inc(ctx.inc("pe"), 1)
                                    n += 1

                    @block.vector
                    def _(vector):
                        n = 0
                        for dst in (v_tm, kh_tm):
                            for kt in range(NKT):
                                for c in range(NTOK // CHUNK):
                                    ps = pt1 if n % 2 == 0 else pt2
                                    vector.wait_ge(ctx.h("pe"), p0 + n + 1)
                                    vector.tensor_copy(
                                        dst[:, c * KD + kt * 128:c * KD + (kt + 1) * 128],
                                        ps[0:64, :]).then_inc(ctx.inc("vec"), 1)
                                    n += 1
                barrier_block(ctx)

                with (
                    nc.psum_tensor("g_psc", [64, CHUNK], F32) as ps_sc,
                    nc.psum_tensor("g_po1", [128, CHUNK], F32) as ps_o1,
                    nc.psum_tensor("g_po2", [128, CHUNK], F32) as ps_o2,
                    nc.psum_tensor("g_ps1", [128, KD], F32) as ps_s1,
                    nc.psum_tensor("g_ps2", [128, KD], F32) as ps_s2,
                    nc.Block() as block,
                ):
                    p0 = ctx.val("pe")
                    v0 = ctx.val("vec")

                    @block.vector
                    def _(vector):
                        vector.memset(state[:], 0.0)
                        vector.memset(state_bf[:], 0.0).then_inc(ctx.inc("gps"), 1)

                    @block.tensor
                    def _(tensor):
                        tensor.wait_ge(ctx.h("gps"), ctx.val("gps"))
                        n = 0
                        for b in range(B):
                            for c in range(NCH):
                                t0 = b * T + c * CHUNK
                                cc = t0 // CHUNK
                                for kt in range(NKT):
                                    i = tensor.matmul(
                                        ps_sc[:],
                                        kt_t[:, kt * NTOK + t0:kt * NTOK + t0 + CHUNK],
                                        qt_t[:, kt * NTOK + t0:kt * NTOK + t0 + CHUNK],
                                        start=(kt == 0), stop=(kt == NKT - 1))
                                i.then_inc(ctx.inc("pe"), 1)
                                for vt in range(NKT):
                                    ps_o = ps_o1 if vt == 0 else ps_o2
                                    for kt in range(NKT):
                                        tensor.matmul(
                                            ps_o[:],
                                            state_bf[:, (kt * B + b) * KD + vt * 128:
                                                     (kt * B + b) * KD + (vt + 1) * 128],
                                            qt_t[:, kt * NTOK + t0:kt * NTOK + t0 + CHUNK],
                                            start=(kt == 0), stop=False)
                                tensor.wait_ge(ctx.h("vec"), v0 + 4 * n + 1)
                                for vt in range(NKT):
                                    ps_o = ps_o1 if vt == 0 else ps_o2
                                    i = tensor.matmul(
                                        ps_o[:],
                                        v_tm[:, cc * KD + vt * 128:cc * KD + (vt + 1) * 128],
                                        a_sm[:],
                                        start=False, stop=True)
                                i.then_inc(ctx.inc("pe"), 1)
                                for kt in range(NKT):
                                    ps_s = ps_s1 if kt == 0 else ps_s2
                                    i = tensor.matmul(
                                        ps_s[:],
                                        kh_tm[:, cc * KD + kt * 128:cc * KD + (kt + 1) * 128],
                                        v_tm[:, cc * KD:(cc + 1) * KD],
                                        start=True, stop=True)
                                i.then_inc(ctx.inc("pe"), 1)
                                tensor.wait_ge(ctx.h("vec"), v0 + 4 * n + 4)
                                n += 1

                    @block.vector
                    def _(vector):
                        n = 0
                        for b in range(B):
                            for c in range(NCH):
                                t0 = b * T + c * CHUNK
                                vector.wait_ge(ctx.h("pe"), p0 + 3 * n + 1)
                                vector.tensor_mul(a_sm[:], ps_sc[:],
                                                  ctx.utmask[:]).then_inc(
                                    ctx.inc("vec"), 1)
                                vector.wait_ge(ctx.h("pe"), p0 + 3 * n + 2)
                                vector.tensor_copy(gout[:, 0 * NTOK + t0:0 * NTOK + t0 + CHUNK],
                                                   ps_o1[:])
                                vector.tensor_copy(gout[:, 1 * NTOK + t0:1 * NTOK + t0 + CHUNK],
                                                   ps_o2[:]).then_inc(ctx.inc("vec"), 1)
                                vector.wait_ge(ctx.h("pe"), p0 + 3 * n + 3)
                                for kt in range(NKT):
                                    ps_s = ps_s1 if kt == 0 else ps_s2
                                    idx = kt * B + b
                                    eccol = ec_t[:, kt * B * NCH + b * NCH + c:
                                                  kt * B * NCH + b * NCH + c + 1]
                                    vector.scalar_tensor_tensor(
                                        state[:, idx * KD:(idx + 1) * KD],
                                        state[:, idx * KD:(idx + 1) * KD],
                                        eccol, ps_s[:], ALU.mult, ALU.add)
                                vector.engine_nop().then_inc(ctx.inc("vec"), 1)
                                for kt in range(NKT):
                                    idx = kt * B + b
                                    vector.tensor_copy(
                                        state_bf[:, idx * KD:(idx + 1) * KD],
                                        state[:, idx * KD:(idx + 1) * KD])
                                vector.engine_nop().then_inc(ctx.inc("vec"), 1)
                                n += 1
                barrier_block(ctx)

                debug_dump(ctx, "gout", gout)
                debug_dump(ctx, "state", state[:])
                def evac_wo(of, ch, ps, vector):
                    bo = ctx.gla_bo[:, of:of + 1]
                    vector.scalar_tensor_tensor(
                        ctx.scr_f32b[:, ch * 512:(ch + 1) * 512], ps[:], bo,
                        ctx.scr_f32[:, ch * 512:(ch + 1) * 512],
                        ALU.add, ALU.add)

                build_proj_residual(ctx, gout, NKT, W["L0_attn_a_Wo"],
                                    W["L0_attn_a_bo"], xres)

    barrier_block(ctx)


def build_proj_residual(ctx, src_fm, n_in_tiles, w_dram, b_dram, xres,
                        extra_add_dram=None):
    nc = ctx.nc
    with (
        nc.sbuf_tensor("pr_wf", [128, n_in_tiles * 128], F32) as wf,
        nc.sbuf_tensor("pr_w", [128, n_in_tiles * 128], BF16) as wb,
        nc.sbuf_tensor("pr_b", [128, NFT], F32) as bcols,
        nc.psum_tensor("pr_p1", [128, 512], F32) as p1,
        nc.psum_tensor("pr_p2", [128, 512], F32) as p2,
    ):
        xf = ctx.scr_f32[:, 0:NTOK]
        of_out = ctx.scr_f32[:, NTOK:2 * NTOK]
        with nc.Block() as block:
            @block.sync
            def _(sync):
                for ft in range(NFT):
                    dma(ctx, sync, bcols[:, ft:ft + 1],
                        b_dram[ft * 128:(ft + 1) * 128, :])
                wait_all(ctx, sync, "dma")
        for of in range(NFT):
            with nc.Block() as block:
                d0 = ctx.val("dma")
                g0 = ctx.val("gps")
                p0 = ctx.val("pe")
                v0 = ctx.val("vec")
                nch = NTOK // 512

                @block.sync
                def _(sync):
                    dma(ctx, sync, xf[:], xres.ap()[of])
                    for it in range(n_in_tiles):
                        dma(ctx, sync, wf[:, it * 128:(it + 1) * 128],
                            w_dram[it * 128:(it + 1) * 128, of * 128:(of + 1) * 128])

                @block.scalar
                def _(scalar):
                    scalar.wait_ge(ctx.h("dma"), d0 + 16 * (1 + n_in_tiles))
                    scalar.activation(wb[:], wf[:], AF.Copy).then_inc(
                        ctx.inc("gps"), 1)

                @block.tensor
                def _(tensor):
                    tensor.wait_ge(ctx.h("gps"), g0 + 1)
                    for ch in range(nch):
                        ps = p1 if ch % 2 == 0 else p2
                        if ch >= 2:
                            tensor.wait_ge(ctx.h("vec"), v0 + ch - 1)
                        for it in range(n_in_tiles):
                            i = tensor.matmul(
                                ps[:], wb[:, it * 128:(it + 1) * 128],
                                src_fm[:, it * NTOK + ch * 512:it * NTOK + (ch + 1) * 512],
                                start=(it == 0), stop=(it == n_in_tiles - 1))
                        i.then_inc(ctx.inc("pe"), 1)

                v1 = ctx.val("vec")

                @block.vector
                def _(vector):
                    for ch in range(nch):
                        ps = p1 if ch % 2 == 0 else p2
                        vector.wait_ge(ctx.h("pe"), p0 + ch + 1)
                        vector.scalar_tensor_tensor(
                            of_out[:, ch * 512:(ch + 1) * 512], ps[:],
                            bcols[:, of:of + 1], xf[:, ch * 512:(ch + 1) * 512],
                            ALU.add, ALU.add).then_inc(ctx.inc("vec"), 1)

                @block.scalar
                def _(scalar):
                    scalar.wait_ge(ctx.h("vec"), v1 + nch)
                    dma(ctx, scalar, xres.ap()[of], of_out[:], sem="dmb")
                    scalar.wait_ge(ctx.h("dmb"), ctx.val("dmb"))
    barrier_block(ctx)



def build_ffn0_prep(ctx, W):
    nc = ctx.nc
    ctx.ffn_kq8 = ctx._stack_append(nc.sbuf_tensor("f0_kq8", [128, NFT * D], FP8))
    ctx.ffn_bias = ctx._stack_append(nc.sbuf_tensor("f0_bias", [128, NFT], F32))
    ctx.ffn_scol = ctx._stack_append(nc.sbuf_tensor("f0_scol", [128, 5], F32))
    with contextlib.ExitStack() as es:
        kbf = es.enter_context(nc.sbuf_tensor("f0_kbf", [128, D], BF16))
        srow = es.enter_context(nc.sbuf_tensor("f0_srow", [1, 2], F32))
        with (
            nc.psum_tensor("f0_ps", [128, 2], F32) as ps,
            nc.Block() as block,
        ):
            d0 = ctx.val("dma")

            @block.sync
            def _(sync):
                dma(ctx, sync, srow[:, 0:1], W["L0_ffn_input_scale"][:])
                dma(ctx, sync, srow[:, 1:2], W["L0_ffn_kernel_scale"][:])
                for ft in range(NFT):
                    dma(ctx, sync, kbf[:, ft:ft + 1],
                        W["L0_ffn_bias"][ft * 128:(ft + 1) * 128, :])

            @block.tensor
            def _(tensor):
                tensor.wait_ge(ctx.h("dma"), ctx.val("dma"))
                tensor.matmul(ps[:], ctx.ones_row[:], srow[:],
                              start=True, stop=True).then_inc(ctx.inc("pe"), 1)

            @block.vector
            def _(vector):
                vector.wait_ge(ctx.h("pe"), ctx.val("pe"))
                wait_all(ctx, vector, "dma")
                vector.tensor_copy(ctx.ffn_bias[:], kbf[:, 0:NFT])
                vector.tensor_copy(ctx.ffn_scol[:, 0:2], ps[:])
                vector.drain()
                vector.tensor_mul(ctx.ffn_scol[:, 2:3], ctx.ffn_scol[:, 0:1],
                                  ctx.ffn_scol[:, 1:2])
                vector.drain()
                vector.reciprocal(ctx.ffn_scol[:, 3:5],
                                  ctx.ffn_scol[:, 0:2]).then_inc(ctx.inc("vec"), 1)

        for it in range(NFT):
            with nc.Block() as block:
                d0 = ctx.val("dma")

                @block.sync
                def _(sync, it=it):
                    dma(ctx, sync, kbf[:], W["L0_ffn_kernel"][it * 128:(it + 1) * 128, :])

                @block.vector
                def _(vector, it=it, d0=d0):
                    vector.wait_ge(ctx.h("dma"), d0 + 16)
                    vector.tensor_scalar_mul(ctx.ffn_kq8[:, it * D:(it + 1) * D],
                                              kbf[:],
                                              ctx.ffn_scol[:, 4:5]).then_inc(
                        ctx.inc("vec"), 1)
    barrier_block(ctx)


def build_ffn0(ctx, xres, W, parts=2):
    nc = ctx.nc
    with contextlib.ExitStack() as es:
        es_r = contextlib.ExitStack()
        xn_all = es_r.enter_context(
            nc.sbuf_tensor("f0_xn", [128, NFT * NTOK], BF16, side="right"))
        xn = [xn_all[:, i * NTOK:(i + 1) * NTOK] for i in range(NFT)]
        build_layernorm(ctx, xres, xn)
        xq8 = es_r.enter_context(
            nc.sbuf_tensor("f0_xq8", [128, NFT * NTOK], FP8, side="right"))

        with nc.Block() as block:
            @block.vector
            def _(vector):
                for it in range(NFT):
                    vector.tensor_scalar_mul(
                        xq8[:, it * NTOK:(it + 1) * NTOK], xn[it],
                        ctx.ffn_scol[:, 3:4]).then_inc(ctx.inc("vec"), 1)
        barrier_block(ctx)
        debug_dump(ctx, "f0xq8", xq8[:, 0:NTOK])
        debug_dump(ctx, "f0kq8", ctx.ffn_kq8[:])
        debug_dump(ctx, "f0scol", ctx.ffn_scol[:])
        debug_dump(ctx, "f0bias", ctx.ffn_bias[:])
        debug_dump(ctx, "f0xn0", xn[0])
        if parts < 2:
            es_r.close()
            return

        with contextlib.ExitStack() as es2:
            ybf = ctx.scr_bf[:, 0:NTOK]
            p1 = es2.enter_context(nc.psum_tensor("f0_p1", [128, 512], F32))
            p2 = es2.enter_context(nc.psum_tensor("f0_p2", [128, 512], F32))
            xf = ctx.scr_f32[:, 0:NTOK]
            of_out = ctx.scr_f32[:, NTOK:2 * NTOK]
            for of in range(NFT):
                with nc.Block() as block:
                    d0 = ctx.val("dma")
                    p0 = ctx.val("pe")
                    v0 = ctx.val("vec")
                    nch = NTOK // 512

                    @block.sync
                    def _(sync, of=of):
                        dma(ctx, sync, xf, xres.ap()[of])

                    @block.tensor
                    def _(tensor, of=of, v0=v0):
                        for ch in range(nch):
                            ps = p1 if ch % 2 == 0 else p2
                            if ch >= 2:
                                tensor.wait_ge(ctx.h("vec"), v0 + 2 * (ch - 1))
                            for it in range(NFT):
                                i = tensor.matmul(
                                    ps[:],
                                    ctx.ffn_kq8[:, it * D + of * 128:it * D + (of + 1) * 128],
                                    xq8[:, it * NTOK + ch * 512:it * NTOK + (ch + 1) * 512],
                                    start=(it == 0), stop=(it == NFT - 1))
                            i.then_inc(ctx.inc("pe"), 1)

                    @block.vector
                    def _(vector, of=of, d0=d0, p0=p0):
                        vector.wait_ge(ctx.h("dma"), d0 + 16)
                        for ch in range(nch):
                            ps = p1 if ch % 2 == 0 else p2
                            vector.wait_ge(ctx.h("pe"), p0 + ch + 1)
                            c = slice(ch * 512, (ch + 1) * 512)
                            vector.tensor_scalar(ybf[:, c], ps[:],
                                                 ctx.ffn_scol[:, 0:1],
                                                 ctx.ffn_scol[:, 1:2],
                                                 ALU.mult, ALU.mult)
                            vector.scalar_tensor_tensor(
                                of_out[:, c], ybf[:, c],
                                ctx.ffn_bias[:, of:of + 1], xf[:, c],
                                ALU.add, ALU.add).then_inc(ctx.inc("vec", 2), 2)

                    v1 = ctx.val("vec")

                    @block.scalar
                    def _(scalar, of=of, v1=v1):
                        scalar.wait_ge(ctx.h("vec"), v1)
                        dma(ctx, scalar, xres.ap()[of], of_out, sem="dmb")
                        scalar.wait_ge(ctx.h("dmb"), ctx.val("dmb"))
        es_r.close()
    barrier_block(ctx)


def apof(t):
    return t if isinstance(t, bass.AP) else t[:]


def _bitonic_rounds(n=32):
    out = []
    k = 2
    while k <= n:
        j = k // 2
        while j >= 1:
            out.append((k, j, n // (2 * j), max(k // (2 * j), 1)))
            j //= 2
        k *= 2
    return out


def _emit_sparsemax(ctx, vector, z, zb1, zb2, zb3):
    nseg = z.ap[-1][1] // 32
    cur, nxt = z, zb1
    for (k, j, nb, m) in _bitonic_rounds(32):
        for par in (0, 1):
            if par == 1 and nb < 2 * m:
                continue
            base = par * m * 2 * j
            if nb >= 2 * m:
                dims = [[4 * m * j, nseg * (nb // (2 * m))], [2 * j, m], [1, j]]
            else:
                dims = [[2 * j, nseg * nb], [1, j]]

            def mk(t, extra, dims=dims, base=base):
                return bass.AP(t.tensor, t.offset + base + extra,
                               [list(t.ap[0])] + [list(dd) for dd in dims])
            lo_i, hi_i = mk(cur, 0), mk(cur, j)
            lo_o, hi_o = mk(nxt, 0), mk(nxt, j)
            if par == 0:
                vector.tensor_max(lo_o, lo_i, hi_i)
                vector.tensor_tensor(hi_o, lo_i, hi_i, ALU.min)
            else:
                vector.tensor_tensor(lo_o, lo_i, hi_i, ALU.min)
                vector.tensor_max(hi_o, lo_i, hi_i)
        vector.drain()
        if cur is z:
            cur, nxt = zb1, zb3
        else:
            cur, nxt = nxt, cur
    zs = cur
    if zs is not zb1:
        vector.tensor_copy(zb1[:], zs[:])
        zs = zb1
    buf_a, buf_b = zs, zb3
    d = 1
    while d < 32:
        a3 = buf_a.rearrange("p (s c) -> p s c", c=32)
        b3 = buf_b.rearrange("p (s c) -> p s c", c=32)
        vector.tensor_add(b3[:, :, d:32], a3[:, :, d:32], a3[:, :, 0:32 - d])
        vector.tensor_copy(b3[:, :, 0:d], a3[:, :, 0:d])
        vector.drain()
        buf_a = buf_b
        buf_b = zb2 if buf_a is zb3 else zb3
        d *= 2
    cum = buf_a
    if cum is not zb2:
        vector.tensor_copy(zb2[:], cum[:])
        cum = zb2
    return zs, cum


def build_attnb(ctx, xres, W, xn_dram, scr2, parts=4):
    nc = ctx.nc
    H, E, S = 16, 64, 32
    with contextlib.ExitStack() as es:
        es_a = contextlib.ExitStack()
        xn_buf = es_a.enter_context(
            nc.sbuf_tensor("ab_xn", [128, NTOK], BF16, side="right"))
        build_layernorm(ctx, xres, [xn_buf] * NFT, xn_dram=xn_dram)
        es_a.close()

        wq_sb = es.enter_context(nc.sbuf_tensor("ab_wq", [128, NFT * D], BF16))
        keys_fm = es.enter_context(nc.sbuf_tensor("ab_kfm", [128, NFT * S], BF16))
        keys_pad = es.enter_context(nc.sbuf_tensor("ab_kpad", [128, 16 * S], BF16))
        v2 = es.enter_context(nc.sbuf_tensor("ab_v2", [128, NFT * 128], BF16))
        bcols = es.enter_context(nc.sbuf_tensor("ab_b", [128, 2 * NFT], F32))
        krow = es.enter_context(nc.sbuf_tensor("ab_krow", [128, S], F32))
        with contextlib.ExitStack() as es1:
            kp_fm = es1.enter_context(nc.sbuf_tensor("ab_kp", [128, NFT * S], BF16))
            kp_f32 = es1.enter_context(nc.sbuf_tensor("ab_kpf", [128, S], F32))
            brow = es1.enter_context(nc.sbuf_tensor("ab_br", [1, 2 * D], BF16))
            brf = es1.enter_context(nc.sbuf_tensor("ab_brf", [1, 2 * D], F32))
            ones32 = es1.enter_context(nc.sbuf_tensor("ab_o32", [1, 32], BF16))
            keys_sm = es1.enter_context(nc.sbuf_tensor("ab_ksm", [32, D], BF16))
            vals_rep = es1.enter_context(nc.sbuf_tensor("ab_vrep", [128, D], BF16))

            with nc.Block() as block:
                d0 = ctx.val("dma")
                dq00 = ctx.val("dq0")
                v0b1 = ctx.val("vec")

                @block.sync
                def _(sync):
                    for it in range(NFT):
                        if it >= 1:
                            sync.wait_ge(ctx.h("vec"), v0b1 + it)
                        dma(ctx, sync, kp_f32[:],
                            W["L0_attn_b_key_param"][it * 128:(it + 1) * 128, :],
                            sem="dq0")
                    dma(ctx, sync, brf[:, 0:D],
                        bass.AP(W["L0_attn_b_bk"].tensor, 0, [[2 * D, 1], [1, D]]))
                    dma(ctx, sync, brf[:, D:2 * D],
                        bass.AP(W["L0_attn_b_bv"].tensor, 0, [[2 * D, 1], [1, D]]))
                    for ft in range(NFT):
                        dma(ctx, sync, bcols[:, ft:ft + 1],
                            W["L0_attn_b_bq"][ft * 128:(ft + 1) * 128, :])
                        dma(ctx, sync, bcols[:, NFT + ft:NFT + ft + 1],
                            W["L0_attn_b_bo"][ft * 128:(ft + 1) * 128, :])
                    dma(ctx, sync, krow[:], W["const_krow"][:])

                @block.vector
                def _(vector):
                    for it in range(NFT):
                        vector.wait_ge(ctx.h("dq0"), dq00 + 16 * (it + 1))
                        vector.tensor_copy(kp_fm[:, it * S:(it + 1) * S],
                                           kp_f32[:]).then_inc(ctx.inc("vec"), 1)
                    vector.wait_ge(ctx.h("dma"), ctx.val("dma"))
                    vector.tensor_copy(brow[:], brf[:])
                    vector.memset(ones32[:], 1.0)
                    vector.memset(v2[:], 0.0).then_inc(ctx.inc("vec"), 1)
            barrier_block(ctx)

            for step in range(2):
                wname = "L0_attn_b_Wk" if step == 0 else "L0_attn_b_Wv"
                boff = 0 if step == 0 else D
                with nc.Block() as block:
                    q0 = [ctx.val("dq0"), ctx.val("dq1")]
                    s0 = ctx.val("scl")

                    @block.sync
                    def _(sync, wname=wname):
                        for it in range(NFT):
                            if it >= 2:
                                sync.wait_ge(ctx.h("scl"), s0 + it - 1)
                            dma(ctx, sync, ctx.scr_f32[:, (it % 2) * D:(it % 2) * D + D],
                                W[wname][it * 128:(it + 1) * 128, :],
                                sem=f"dq{it % 2}")

                    @block.scalar
                    def _(scalar):
                        for it in range(NFT):
                            scalar.wait_ge(ctx.h(f"dq{it % 2}"),
                                           q0[it % 2] + 16 * (it // 2 + 1))
                            scalar.activation(ctx.scr_bf[:, it * D:(it + 1) * D],
                                              ctx.scr_f32[:, (it % 2) * D:(it % 2) * D + D],
                                              AF.Copy).then_inc(ctx.inc("scl"), 1)
                with (
                    nc.psum_tensor("ab_ps", [32, 512], F32) as ps,
                    nc.Block() as block,
                ):
                    p0 = ctx.val("pe")
                    v0 = ctx.val("vec")

                    @block.tensor
                    def _(tensor, step=step, boff=boff):
                        for hc in range(2):
                            if hc == 1:
                                tensor.wait_ge(ctx.h("vec"), v0 + 1)
                            for it in range(NFT):
                                lhs = (kp_fm if step == 0 else keys_fm)
                                tensor.matmul(ps[:], lhs[:, it * S:(it + 1) * S],
                                              ctx.scr_bf[:, it * D + hc * 512:it * D + hc * 512 + 512],
                                              start=(it == 0), stop=False)
                            i = tensor.matmul(ps[:], ones32[:],
                                              brow[:, boff + hc * 512:boff + hc * 512 + 512],
                                              start=False, stop=True)
                            i.then_inc(ctx.inc("pe"), 1)

                    @block.vector
                    def _(vector, step=step):
                        for hc in range(2):
                            vector.wait_ge(ctx.h("pe"), p0 + hc + 1)
                            dst = keys_sm if step == 0 else vals_rep[0:32, :]
                            vector.tensor_copy(dst[0:32, hc * 512:hc * 512 + 512],
                                               ps[:]).then_inc(ctx.inc("vec"), 1)
                barrier_block(ctx)
                if step == 0:
                    with (
                        nc.psum_tensor("ab_pt", [128, 32], BF16) as pt,
                        nc.Block() as block,
                    ):
                        p0 = ctx.val("pe")
                        v0 = ctx.val("vec")

                        @block.tensor
                        def _(tensor):
                            for it in range(NFT):
                                if it >= 1:
                                    tensor.wait_ge(ctx.h("vec"), v0 + it)
                                tensor.transpose(pt[:], keys_sm[0:32, it * 128:(it + 1) * 128],
                                                 ctx.ident[0:32, 0:32]).then_inc(ctx.inc("pe"), 1)

                        @block.vector
                        def _(vector):
                            for it in range(NFT):
                                vector.wait_ge(ctx.h("pe"), p0 + it + 1)
                                vector.tensor_copy(keys_fm[:, it * S:(it + 1) * S],
                                                   pt[:]).then_inc(ctx.inc("vec"), 1)
                    barrier_block(ctx)

            with (
                nc.psum_tensor("ab_pr", [128, 512], F32) as pr,
                nc.Block() as block,
            ):
                p0 = ctx.val("pe")
                v0 = ctx.val("vec")

                @block.tensor
                def _(tensor):
                    for hc in range(2):
                        if hc == 1:
                            tensor.wait_ge(ctx.h("vec"), v0 + 1)
                        tensor.matmul(pr[:], ctx.rep32[:],
                                      vals_rep[0:32, hc * 512:hc * 512 + 512],
                                      start=True, stop=True).then_inc(ctx.inc("pe"), 1)

                @block.vector
                def _(vector):
                    for hc in range(2):
                        vector.wait_ge(ctx.h("pe"), p0 + hc + 1)
                        vector.tensor_copy(vals_rep[:, hc * 512:hc * 512 + 512],
                                           pr[:]).then_inc(ctx.inc("vec"), 1)
            barrier_block(ctx)

            with nc.Block() as block:
                @block.vector
                def _(vector):
                    vector.memset(keys_pad[:], 0.0)
                    for h in range(H):
                        r0 = (h % 2) * 64
                        vector.tensor_copy(
                            keys_pad[r0:r0 + 64, h * S:(h + 1) * S],
                            keys_fm[r0:r0 + 64, (h // 2) * S:(h // 2) * S + S])
                    for f in range(NFT):
                        b0 = (f % 2) * 64
                        vector.tensor_copy(
                            v2[b0:b0 + 32, f * 128:f * 128 + 64],
                            vals_rep[b0:b0 + 32, (2 * f) * 64:(2 * f) * 64 + 64])
                        vector.tensor_copy(
                            v2[b0 + 32:b0 + 64, f * 128 + 64:f * 128 + 128],
                            vals_rep[b0 + 32:b0 + 64, (2 * f + 1) * 64:(2 * f + 1) * 64 + 64])
                    vector.engine_nop().then_inc(ctx.inc("vec"), 1)
            barrier_block(ctx)

            with nc.Block() as block:
                q0 = [ctx.val("dq0"), ctx.val("dq1")]
                s0 = ctx.val("scl")

                @block.sync
                def _(sync):
                    for it in range(NFT):
                        if it >= 2:
                            sync.wait_ge(ctx.h("scl"), s0 + it - 1)
                        dma(ctx, sync, ctx.scr_f32[:, (it % 2) * D:(it % 2) * D + D],
                            W["L0_attn_b_Wq"][it * 128:(it + 1) * 128, :],
                            sem=f"dq{it % 2}")

                @block.scalar
                def _(scalar):
                    for it in range(NFT):
                        scalar.wait_ge(ctx.h(f"dq{it % 2}"),
                                       q0[it % 2] + 16 * (it // 2 + 1))
                        scalar.activation(wq_sb[:, it * D:(it + 1) * D],
                                          ctx.scr_f32[:, (it % 2) * D:(it % 2) * D + D],
                                          AF.Copy).then_inc(ctx.inc("scl"), 1)
            barrier_block(ctx)

        if parts < 2:
            return
        es_r = contextlib.ExitStack()
        out_r = es_r.enter_context(
            nc.sbuf_tensor("ab_outr", [128, NFT * NTOK], BF16, side="right"))
        small_dbg = es.enter_context(nc.sbuf_tensor("ab_small", [128, 4 * 64], F32))
        with contextlib.ExitStack() as es2:
            xn_sb = ctx.scr_bf[:, 0:NFT * 512]
            q_sb = ctx.scr_bf[:, NFT * 512:2 * NFT * 512]
            z_sb = ctx.scr_f32[:, 0:2048]
            zb1 = ctx.scr_f32[:, 2048:4096]
            zb2 = ctx.scr_f32[:, 4096:6144]
            zb3 = ctx.scr_f32[:, 6144:8192]
            small = small_dbg
            a_bf = ctx.scr_xr[:, 0:2048]
            at_sb = ctx.scr_xr[:, 2048:2048 + 16 * 128]
            qp1 = es2.enter_context(nc.psum_tensor("ab_qp1", [128, 512], F32))
            qp2 = es2.enter_context(nc.psum_tensor("ab_qp2", [128, 512], F32))
            scp = es2.enter_context(nc.psum_tensor("ab_scp", [128, 512], F32))
            op1 = es2.enter_context(nc.psum_tensor("ab_op1", [128, 512], F32))
            op2 = es2.enter_context(nc.psum_tensor("ab_op2", [128, 512], F32))
            tp1 = es2.enter_context(nc.psum_tensor("ab_tp1", [128, 1024], BF16))
            tp2 = es2.enter_context(nc.psum_tensor("ab_tp2", [128, 1024], BF16))

            NSUP = NTOK // 512
            STAGE = 5
            with nc.Block() as block:
                d0 = ctx.val("dma")
                p0 = ctx.val("pe")
                v0 = ctx.val("vec")
                PEI = [0, 8, 12, 12, 28, 36][STAGE]
                VEI = [0, 8, 12, 13, 29, 37][STAGE]

                @block.sync
                def _(sync):
                    for sp in range(NSUP):
                        if sp >= 1:
                            sync.wait_ge(ctx.h("pe"), p0 + sp * PEI)
                        for it in range(NFT):
                            dma(ctx, sync, xn_sb[:, it * 512:(it + 1) * 512],
                                bass.AP(xn_dram.ap().tensor,
                                        it * 128 * NTOK + sp * 512,
                                        [[NTOK, 128], [1, 512]]))

                @block.tensor
                def _(tensor):
                    for sp in range(NSUP):
                        pb = p0 + sp * PEI
                        vb = v0 + sp * VEI
                        tensor.wait_ge(ctx.h("dma"), d0 + 16 * NFT * (sp + 1))
                        if sp >= 1:
                            tensor.wait_ge(ctx.h("vec"), vb)
                        nq = 0
                        for of in range(NFT):
                            qp = qp1 if of % 2 == 0 else qp2
                            if of >= 2:
                                tensor.wait_ge(ctx.h("vec"), vb + of - 1)
                            for it in range(NFT):
                                i = tensor.matmul(
                                    qp[:], wq_sb[:, it * D + of * 128:it * D + (of + 1) * 128],
                                    xn_sb[:, it * 512:(it + 1) * 512],
                                    start=(it == 0), stop=(it == NFT - 1))
                            i.then_inc(ctx.inc("pe"), 1)
                        if STAGE < 2:
                            continue
                        for tb in range(4):
                            tensor.wait_ge(ctx.h("vec"), vb + 8 + tb)
                            for h in range(H):
                                i = tensor.matmul(
                                    scp[:, h * 32:(h + 1) * 32],
                                    q_sb[:, (h // 2) * 512 + tb * 128:(h // 2) * 512 + tb * 128 + 128],
                                    keys_pad[:, h * S:(h + 1) * S],
                                    start=True, stop=True)
                            i.then_inc(ctx.inc("pe"), 1)
                        if STAGE < 4:
                            continue
                        tensor.wait_ge(ctx.h("vec"), vb + 13)
                        for g in range(16):
                            tp = tp1 if g % 2 == 0 else tp2
                            if g >= 2:
                                tensor.wait_ge(ctx.h("vec"), vb + 13 + g - 1)
                            tensor.transpose(tp[0:128, 0:128], a_bf[:, g * 128:(g + 1) * 128],
                                             ctx.ident[:]).then_inc(ctx.inc("pe"), 1)
                        if STAGE < 5:
                            continue
                        for tb in range(4):
                            tensor.wait_ge(ctx.h("vec"), vb + 13 + (tb + 1) * 4)
                            for half in range(2):
                                if tb >= 1:
                                    tensor.wait_ge(ctx.h("vec"),
                                                   vb + 29 + 2 * (tb - 1) + half + 1)
                                op = op1 if half == 0 else op2
                                for f in range(4 * half, 4 * half + 4):
                                    i = tensor.matmul(
                                        op[:, (f % 4) * 128:(f % 4) * 128 + 128],
                                        v2[:, f * 128:(f + 1) * 128],
                                        at_sb[:, (tb * 4 + f // 2) * 128:(tb * 4 + f // 2) * 128 + 128],
                                        start=True, stop=True)
                                i.then_inc(ctx.inc("pe"), 1)

                @block.vector
                def _(vector):
                    for sp in range(NSUP):
                        pb = p0 + sp * PEI
                        vb = v0 + sp * VEI
                        for of in range(NFT):
                            qp = qp1 if of % 2 == 0 else qp2
                            vector.wait_ge(ctx.h("pe"), pb + of + 1)
                            vector.tensor_scalar_add(
                                q_sb[:, of * 512:(of + 1) * 512], qp[:],
                                bcols[:, of:of + 1]).then_inc(ctx.inc("vec"), 1)
                        if STAGE < 2:
                            continue
                        for tb in range(4):
                            vector.wait_ge(ctx.h("pe"), pb + 8 + tb + 1)
                            vector.tensor_scalar_mul(
                                z_sb[:, tb * 512:(tb + 1) * 512], scp[:],
                                0.125).then_inc(ctx.inc("vec"), 1)
                        if STAGE < 3:
                            continue
                        zs, cum = _emit_sparsemax(ctx, vector, z_sb, zb1, zb2, zb3)
                        nsegs = 64
                        z3 = zs.rearrange("p (s c) -> p s c", c=32)
                        c3 = cum.rearrange("p (s c) -> p s c", c=32)
                        kr = apof(krow)
                        kb = bass.AP(kr.tensor, kr.offset,
                                     [list(kr.ap[0])] + [[0, nsegs], [1, 32]])
                        s3 = zb3.rearrange("p (s c) -> p s c", c=32)
                        vector.tensor_mul(s3[:], z3[:], kb)
                        vector.drain()
                        vector.tensor_sub(s3[:], s3[:], c3[:])
                        vector.drain()
                        vector.tensor_single_scalar(s3[:], s3[:], -1.0, ALU.is_gt)
                        vector.drain()
                        vector.reduce_sum(
                            small[:, 0:nsegs].rearrange("p (s c) -> p s c", c=1),
                            s3[:], axis=mybir.AxisListType.X)
                        vector.drain()
                        vector.tensor_mul(s3[:], s3[:], z3[:])
                        vector.drain()
                        vector.reduce_sum(
                            small[:, 64:64 + nsegs].rearrange("p (s c) -> p s c", c=1),
                            s3[:], axis=mybir.AxisListType.X)
                        vector.drain()
                        vector.reciprocal(small[:, 128:128 + nsegs], small[:, 0:nsegs])
                        vector.drain()
                        vector.tensor_scalar_add(small[:, 64:64 + nsegs],
                                                 small[:, 64:64 + nsegs], -1.0)
                        vector.drain()
                        vector.tensor_mul(small[:, 64:64 + nsegs],
                                          small[:, 64:64 + nsegs],
                                          small[:, 128:128 + nsegs])
                        vector.drain()
                        sm = apof(small)
                        taub = bass.AP(sm.tensor, sm.offset + 64,
                                       [list(sm.ap[0])] + [[1, nsegs], [0, 32]])
                        zr3 = z_sb.rearrange("p (s c) -> p s c", c=32)
                        vector.tensor_sub(zb3.rearrange("p (s c) -> p s c", c=32)[:],
                                          zr3[:], taub)
                        vector.drain()
                        i = vector.tensor_scalar_max(a_bf[:], zb3[:], 0.0)
                        i.then_inc(ctx.inc("vec"), 1)
                        if STAGE < 4:
                            continue
                        for g in range(16):
                            tp = tp1 if g % 2 == 0 else tp2
                            vector.wait_ge(ctx.h("pe"), pb + 12 + g + 1)
                            vector.tensor_copy(at_sb[:, g * 128:(g + 1) * 128],
                                               tp[0:128, 0:128]).then_inc(ctx.inc("vec"), 1)
                        if STAGE < 5:
                            continue
                        for tb in range(4):
                            tbg = sp * 4 + tb
                            bidx = tbg // 16
                            tbb = tbg % 16
                            for half in range(2):
                                vector.wait_ge(ctx.h("pe"),
                                               pb + 28 + 2 * tb + half + 1)
                                op = op1 if half == 0 else op2
                                opa = apof(op)
                                for f in range(4 * half, 4 * half + 4):
                                    src = bass.AP(opa.tensor, opa.offset + (f % 4) * 128,
                                                  [list(opa.ap[0])] + [[1, 16], [16, 8]])
                                    oro = apof(out_r)
                                    dst = bass.AP(oro.tensor,
                                                  oro.offset + f * NTOK + bidx * 2048 + tbb * 8,
                                                  [list(oro.ap[0])] + [[128, 16], [1, 8]])
                                    vector.tensor_copy(dst, src)
                                vector.engine_nop().then_inc(ctx.inc("vec"), 1)
            barrier_block(ctx)

        debug_dump(ctx, "ab_outr", out_r[:])
        debug_dump(ctx, "ab_kpad", keys_pad[:])
        debug_dump(ctx, "ab_v2", v2[:])
        debug_dump(ctx, "ab_z", ctx.scr_f32[:, 0:2048])
        debug_dump(ctx, "ab_zs", ctx.scr_f32[:, 2048:4096])
        debug_dump(ctx, "ab_cum", ctx.scr_f32[:, 4096:6144])
        debug_dump(ctx, "ab_small", small_dbg[:])
        debug_dump(ctx, "ab_abf", ctx.scr_xr[:, 0:2048])
        if parts < 3:
            es_r.close()
            return
        with nc.Block() as block:
            @block.sync
            def _(sync):
                for pt in range(NFT):
                    for bidx in range(B):
                        for hsub in range(2):
                            oro = apof(out_r)
                            pdim = list(oro.ap[0])
                            src = bass.AP(oro.tensor,
                                          oro.offset + (hsub * 64) * pdim[0]
                                          + pt * NTOK + bidx * 2048,
                                          [[pdim[0], 64]] + [[1, 2048]])
                            dst = bass.AP(scr2.ap().tensor,
                                          bidx * (16 * 16 * 64 * 128)
                                          + (2 * pt + hsub) * 131072,
                                          [[128, 64], [8192, 16], [1, 128]])
                            dma(ctx, sync, dst, src)
                wait_all(ctx, sync, "dma")
        barrier_block(ctx)
        es_r.close()

        if parts < 4:
            return
        with contextlib.ExitStack() as es3:
            wo_sb = es3.enter_context(nc.sbuf_tensor("ab_wo", [128, NFT * D], BF16))
            rhs_sb = es3.enter_context(nc.sbuf_tensor("ab_rhs", [128, NFT * 512], BF16))
            pw1 = es3.enter_context(nc.psum_tensor("ab_pw1", [128, 512], F32))
            pw2 = es3.enter_context(nc.psum_tensor("ab_pw2", [128, 512], F32))
            with nc.Block() as block:
                q0 = [ctx.val("dq0"), ctx.val("dq1")]
                s0 = ctx.val("scl")

                @block.sync
                def _(sync):
                    for it in range(NFT):
                        if it >= 2:
                            sync.wait_ge(ctx.h("scl"), s0 + it - 1)
                        dma(ctx, sync, ctx.scr_f32[:, (it % 2) * D:(it % 2) * D + D],
                            W["L0_attn_b_Wo"][it * 128:(it + 1) * 128, :],
                            sem=f"dq{it % 2}")

                @block.scalar
                def _(scalar):
                    for it in range(NFT):
                        scalar.wait_ge(ctx.h(f"dq{it % 2}"),
                                       q0[it % 2] + 16 * (it // 2 + 1))
                        scalar.activation(wo_sb[:, it * D:(it + 1) * D],
                                          ctx.scr_f32[:, (it % 2) * D:(it % 2) * D + D],
                                          AF.Copy).then_inc(ctx.inc("scl"), 1)
            barrier_block(ctx)
            for ch in range(NTOK // 512):
                bidx = ch // 4
                l2c = (ch % 4) * 512
                with nc.Block() as block:
                    d0 = ctx.val("dma")
                    b0 = ctx.val("dmb")
                    p0 = ctx.val("pe")
                    v0 = ctx.val("vec")

                    @block.sync
                    def _(sync, ch=ch, bidx=bidx, l2c=l2c):
                        for ct in range(NFT):
                            for psub in range(2):
                                src = bass.AP(scr2.ap().tensor,
                                              bidx * (16 * 16 * 64 * 128)
                                              + (2 * ct + psub) * 8192
                                              + (l2c // 128) * 131072,
                                              [[128, 64], [131072, 4], [1, 128]])
                                dma(ctx, sync,
                                    rhs_sb[psub * 64:psub * 64 + 64,
                                           ct * 512:(ct + 1) * 512], src)

                    @block.scalar
                    def _(scalar, ch=ch):
                        for of in range(NFT):
                            scalar.wait_ge(ctx.h("vec"), v0 + of - 1)
                            dma(ctx, scalar,
                                ctx.scr_f32[:, (of % 2) * 512:(of % 2) * 512 + 512],
                                xres.ap()[of, :, ch * 512:(ch + 1) * 512], sem="dmb")
                            scalar.wait_ge(ctx.h("vec"), v0 + of + 1)
                            dma(ctx, scalar, xres.ap()[of, :, ch * 512:(ch + 1) * 512],
                                ctx.scr_f32[:, NTOK + (of % 2) * 512:NTOK + (of % 2) * 512 + 512],
                                sem="dmb")

                    @block.tensor
                    def _(tensor, ch=ch):
                        tensor.wait_ge(ctx.h("dma"), d0 + 16 * 2 * NFT)
                        for of in range(NFT):
                            pw = pw1 if of % 2 == 0 else pw2
                            if of >= 2:
                                tensor.wait_ge(ctx.h("vec"), v0 + of - 1)
                            for ct in range(NFT):
                                i = tensor.matmul(
                                    pw[:], wo_sb[:, ct * D + of * 128:ct * D + (of + 1) * 128],
                                    rhs_sb[:, ct * 512:(ct + 1) * 512],
                                    start=(ct == 0), stop=(ct == NFT - 1))
                            i.then_inc(ctx.inc("pe"), 1)

                    @block.vector
                    def _(vector, ch=ch):
                        for of in range(NFT):
                            pw = pw1 if of % 2 == 0 else pw2
                            vector.wait_ge(ctx.h("pe"), p0 + of + 1)
                            vector.wait_ge(ctx.h("dmb"), b0 + 16 * (2 * of + 1))
                            vector.scalar_tensor_tensor(
                                ctx.scr_f32[:, NTOK + (of % 2) * 512:NTOK + (of % 2) * 512 + 512],
                                pw[:], bcols[:, NFT + of:NFT + of + 1],
                                ctx.scr_f32[:, (of % 2) * 512:(of % 2) * 512 + 512],
                                ALU.add, ALU.add).then_inc(ctx.inc("vec"), 1)
                        wait_all(ctx, vector, "dmb")
            barrier_block(ctx)
    barrier_block(ctx)


def build_attna(ctx, xres, W, qraw_d, kraw_d, vraw_d, ks_d):
    nc = ctx.nc
    H, DH = 16, 64
    with contextlib.ExitStack() as es:
        es_xn = contextlib.ExitStack()
        xn_all = es_xn.enter_context(
            nc.sbuf_tensor("aa_xn", [128, NFT * NTOK], BF16, side="right"))
        xn = [xn_all[:, i * NTOK:(i + 1) * NTOK] for i in range(NFT)]
        build_layernorm(ctx, xres, xn)

        kv_stage = ctx.scr_bf[:, 0:NTOK]
        for wname, dscr in (("L1_attn_a_Wq", qraw_d),
                            ("L1_attn_a_Wk", kraw_d), ("L1_attn_a_Wv", vraw_d)):
            def evac_kv(of, ch, ps, eng, dscr=dscr):
                return eng.tensor_copy(
                    kv_stage[:, ch * 512:(ch + 1) * 512], ps[:])
            for of in range(NFT):
                build_proj(ctx, xn, W[wname], 1, evac_kv, of0=of, evac_eng="vec")
                with nc.Block() as block:
                    @block.scalar
                    def _(scalar, of=of, dscr=dscr):
                        dma(ctx, scalar, dscr.ap()[of], kv_stage, sem="dmb")
                        scalar.wait_ge(ctx.h("dmb"), ctx.val("dmb"))
        es_xn.close()

        for t in range(NFT):
            with nc.Block() as block:
                d0 = ctx.val("dma")
                v0 = ctx.val("vec")
                s0 = ctx.val("scl")
                qt = ctx.scr_bf[:, 0:NTOK]

                @block.sync
                def _(sync, t=t):
                    dma(ctx, sync, qt, qraw_d.ap()[t])

                @block.vector
                def _(vector, t=t):
                    vector.wait_ge(ctx.h("dma"), d0 + 16)
                    vector.tensor_scalar_min(ctx.scr_xr[:, 0:NTOK], qt, 0.0).then_inc(
                        ctx.inc("vec"), 1)
                    vector.wait_ge(ctx.h("scl"), s0 + 1)
                    vector.tensor_scalar_max(qt, qt, 0.0)
                    vector.drain()
                    vector.tensor_add(qt, qt, ctx.scr_xr[:, NTOK:2 * NTOK]).then_inc(
                        ctx.inc("vec"), 1)

                @block.scalar
                def _(scalar, t=t):
                    scalar.wait_ge(ctx.h("vec"), v0 + 1)
                    scalar.activation(ctx.scr_xr[:, NTOK:2 * NTOK],
                                      ctx.scr_xr[:, 0:NTOK],
                                      AF.Exp).then_inc(ctx.inc("scl"), 1)
                    scalar.wait_ge(ctx.h("vec"), v0 + 2)
                    dma(ctx, scalar, qraw_d.ap()[t], qt, sem="dmb")
                    scalar.wait_ge(ctx.h("dmb"), ctx.val("dmb"))
        barrier_block(ctx)

        with contextlib.ExitStack() as e2:
            ktm = e2.enter_context(nc.sbuf_tensor("aa_ktm", [128, D], BF16))
            vtm = e2.enter_context(nc.sbuf_tensor("aa_vtm", [128, D], BF16))
            kvsb = e2.enter_context(nc.sbuf_tensor("aa_kv", [64, 2 * D], F32))
            kssb = e2.enter_context(nc.sbuf_tensor("aa_ks", [1, 2 * D], F32))
            blkin = e2.enter_context(nc.sbuf_tensor("aa_blk", [128, 2 * D], BF16))
            tp1 = e2.enter_context(nc.psum_tensor("aa_tp1", [128, 1024], BF16))
            tp2 = e2.enter_context(nc.psum_tensor("aa_tp2", [128, 1024], BF16))
            kvp = e2.enter_context(nc.psum_tensor("aa_kvp", [64, 1024], F32))
            ksp = e2.enter_context(nc.psum_tensor("aa_ksp", [1, 1024], F32))
            with nc.Block() as block:
                @block.vector
                def _(vector):
                    vector.memset(kvsb[:], 0.0)
                    vector.memset(kssb[:], 0.0).then_inc(ctx.inc("vec"), 1)
            barrier_block(ctx)
            NTB = NTOK // 128
            for tb in range(NTB):
                bidx = tb // 16
                with nc.Block() as block:
                    q0 = [ctx.val("dq0"), ctx.val("dq1")]
                    p0 = ctx.val("pe")
                    v0 = ctx.val("vec")
                    s0 = ctx.val("scl")

                    @block.sync
                    def _(sync, tb=tb):
                        for t in range(NFT):
                            src = bass.AP(kraw_d.ap().tensor,
                                          t * 128 * NTOK + tb * 128,
                                          [[NTOK, 128], [1, 128]])
                            dma(ctx, sync, blkin[:, t * 128:(t + 1) * 128], src,
                                sem="dq0")
                        for t in range(NFT):
                            src = bass.AP(vraw_d.ap().tensor,
                                          t * 128 * NTOK + tb * 128,
                                          [[NTOK, 128], [1, 128]])
                            dma(ctx, sync, blkin[:, D + t * 128:D + (t + 1) * 128],
                                src, sem="dq1")

                    @block.tensor
                    def _(tensor, tb=tb):
                        tensor.wait_ge(ctx.h("dq0"), q0[0] + 16 * NFT)
                        for t in range(NFT):
                            tensor.transpose(tp1[:, t * 128:(t + 1) * 128],
                                             blkin[:, t * 128:(t + 1) * 128],
                                             ctx.ident[:])
                        tensor.wait_ge(ctx.h("dq1"), q0[1] + 16 * NFT)
                        for t in range(NFT):
                            i = tensor.transpose(tp2[:, t * 128:(t + 1) * 128],
                                                 blkin[:, D + t * 128:D + (t + 1) * 128],
                                                 ctx.ident[:])
                        i.then_inc(ctx.inc("pe"), 1)
                        tensor.wait_ge(ctx.h("vec"), v0 + 3)
                        for h in range(H):
                            i = tensor.matmul(kvp[:, h * 64:(h + 1) * 64],
                                              ktm[:, h * 64:(h + 1) * 64],
                                              vtm[:, h * 64:(h + 1) * 64],
                                              start=True, stop=True)
                        i.then_inc(ctx.inc("pe"), 1)
                        for hc in range(2):
                            i = tensor.matmul(ksp[:, hc * 512:(hc + 1) * 512],
                                              ctx.ones_col[:],
                                              ktm[:, hc * 512:(hc + 1) * 512],
                                              start=True, stop=True)
                        i.then_inc(ctx.inc("pe"), 1)

                    @block.vector
                    def _(vector, tb=tb, bidx=bidx):
                        vector.wait_ge(ctx.h("pe"), p0 + 1)
                        vector.tensor_copy(vtm[:], tp2[:]).then_inc(ctx.inc("vec"), 1)
                        vector.tensor_scalar_min(ctx.scr_xr[:, 0:D], tp1[:], 0.0).then_inc(
                            ctx.inc("vec"), 1)
                        vector.wait_ge(ctx.h("scl"), s0 + 1)
                        vector.tensor_scalar_max(ktm[:], tp1[:], 0.0)
                        vector.drain()
                        vector.tensor_add(ktm[:], ktm[:],
                                          ctx.scr_xr[:, NTOK:NTOK + D]).then_inc(
                            ctx.inc("vec"), 1)
                        vector.wait_ge(ctx.h("pe"), p0 + 2)
                        vector.tensor_add(kvsb[:, bidx * D:(bidx + 1) * D],
                                          kvsb[:, bidx * D:(bidx + 1) * D], kvp[:])
                        vector.drain().then_inc(ctx.inc("vec"), 1)
                        vector.wait_ge(ctx.h("pe"), p0 + 3)
                        vector.tensor_add(kssb[:, bidx * D:(bidx + 1) * D],
                                          kssb[:, bidx * D:(bidx + 1) * D], ksp[:])
                        vector.drain().then_inc(ctx.inc("vec"), 1)

                    @block.scalar
                    def _(scalar, tb=tb):
                        scalar.wait_ge(ctx.h("vec"), v0 + 2)
                        scalar.activation(ctx.scr_xr[:, NTOK:NTOK + D],
                                          ctx.scr_xr[:, 0:D],
                                          AF.Exp).then_inc(ctx.inc("scl"), 1)

            with nc.Block() as block:
                @block.scalar
                def _(scalar):
                    dma(ctx, scalar, ks_d.ap()[:], kssb[:], sem="dmb")
                    scalar.wait_ge(ctx.h("dmb"), ctx.val("dmb"))
            barrier_block(ctx)
            kspad = e2.enter_context(nc.sbuf_tensor("aa_kspad", [128, 2 * 128], BF16))
            kvb = e2.enter_context(nc.sbuf_tensor("aa_kvb", [64, 2 * D], BF16))
            z2b = e2.enter_context(nc.sbuf_tensor("aa_z2b", [16, NTOK], BF16))
            kscol = e2.enter_context(nc.sbuf_tensor("aa_kscol", [128, 16], F32))
            kv2 = e2.enter_context(nc.sbuf_tensor("aa_kv2", [128, 2 * NFT * 128], BF16))
            z2 = e2.enter_context(nc.sbuf_tensor("aa_z2", [16, NTOK], F32))
            with nc.Block() as block:
                d0 = ctx.val("dma")

                @block.sync
                def _(sync):
                    for bidx in range(B):
                        for t in range(NFT):
                            src = bass.AP(ks_d.ap().tensor, bidx * D + t * 128,
                                          [[1, 128], [1, 1]])
                            dma(ctx, sync, kscol[:, bidx * NFT + t:bidx * NFT + t + 1],
                                src)

                @block.vector
                def _(vector):
                    wait_all(ctx, vector, "dma")
                    vector.tensor_copy(kvb[:], kvsb[:])
                    vector.memset(kspad[:], 0.0)
                    vector.drain()
                    for bidx in range(B):
                        for h in range(H):
                            r0 = (h % 2) * 64
                            vector.tensor_copy(
                                kspad[r0:r0 + 64,
                                      bidx * 128 + (h // 2) * 16 + h:bidx * 128 + (h // 2) * 16 + h + 1],
                                kscol[r0:r0 + 64,
                                      bidx * NFT + h // 2:bidx * NFT + h // 2 + 1])
                    vector.engine_nop().then_inc(ctx.inc("vec"), 1)
            barrier_block(ctx)

            qblk = e2.enter_context(nc.sbuf_tensor("aa_qblk", [128, 2 * NFT * 512], BF16))
            with (
                nc.psum_tensor("aa_zp", [16, 512], F32) as zp,
                nc.psum_tensor("aa_k2p", [128, 128], F32) as k2p,
                nc.Block() as block,
            ):
                p0 = ctx.val("pe")
                v0 = ctx.val("vec")
                qz0 = [ctx.val("dq0"), ctx.val("dq1")]

                @block.sync
                def _(sync):
                    n = 0
                    for bidx in range(B):
                        for chk in range(T // 512):
                            if n >= 2:
                                sync.wait_ge(ctx.h("pe"), p0 + n - 1)
                            for t in range(NFT):
                                src = bass.AP(qraw_d.ap().tensor,
                                              t * 128 * NTOK + bidx * T + chk * 512,
                                              [[NTOK, 128], [1, 512]])
                                dma(ctx, sync,
                                    qblk[:, (n % 2) * 4096 + t * 512:(n % 2) * 4096 + (t + 1) * 512],
                                    src, sem=f"dq{n % 2}")
                            n += 1

                @block.tensor
                def _(tensor):
                    n = 0
                    qq0 = [ctx.val("dq0") - 0, ctx.val("dq1")]
                    for bidx in range(B):
                        for chk in range(T // 512):
                            if n >= 1:
                                tensor.wait_ge(ctx.h("vec"), v0 + n)
                            tensor.wait_ge(ctx.h(f"dq{n % 2}"),
                                           qz0[n % 2] + 16 * NFT * (n // 2 + 1))
                            for t in range(NFT):
                                i = tensor.matmul(
                                    zp[:], kspad[:, bidx * 128 + t * 16:bidx * 128 + (t + 1) * 16],
                                    qblk[:, (n % 2) * 4096 + t * 512:(n % 2) * 4096 + (t + 1) * 512],
                                    start=(t == 0), stop=(t == NFT - 1))
                            i.then_inc(ctx.inc("pe"), 1)
                            n += 1
                    for bidx in range(B):
                        for f in range(NFT):
                            tensor.wait_ge(ctx.h("vec"), v0 + n)
                            i = tensor.matmul(k2p[:, 0:64],
                                              ctx.ident[0:64, :],
                                              kvb[:, bidx * D + (2 * f) * 64:bidx * D + (2 * f) * 64 + 64],
                                              start=True, stop=True)
                            i = tensor.matmul(k2p[:, 64:128],
                                              ctx.selhi[:],
                                              kvb[:, bidx * D + (2 * f + 1) * 64:bidx * D + (2 * f + 1) * 64 + 64],
                                              start=True, stop=True)
                            i.then_inc(ctx.inc("pe"), 1)
                            n += 1

                @block.vector
                def _(vector):
                    n = 0
                    for bidx in range(B):
                        for chk in range(T // 512):
                            vector.wait_ge(ctx.h("pe"), p0 + n + 1)
                            vector.tensor_copy(
                                z2[:, bidx * T + chk * 512:bidx * T + (chk + 1) * 512],
                                zp[:]).then_inc(ctx.inc("vec"), 1)
                            n += 1
                    for bidx in range(B):
                        for f in range(NFT):
                            vector.wait_ge(ctx.h("pe"), p0 + n + 1)
                            vector.tensor_copy(
                                kv2[:, (bidx * NFT + f) * 128:(bidx * NFT + f + 1) * 128],
                                k2p[:]).then_inc(ctx.inc("vec"), 1)
                            n += 1
                    vector.tensor_scalar_add(z2[:], z2[:], 1e-6)
                    vector.drain()
                    vector.reciprocal(z2[:], z2[:])
                    vector.drain()
                    vector.tensor_copy(z2b[:], z2[:])
                    vector.drain().then_inc(ctx.inc("vec"), 1)
            barrier_block(ctx)

            with (
                nc.psum_tensor("aa_op", [128, 512], F32) as opp,
                nc.psum_tensor("aa_zb", [128, 512], F32) as zbp,
                nc.sbuf_tensor("aa_zbs", [128, 512], F32) as zbs,
                nc.sbuf_tensor("aa_ao", [128, NTOK], BF16) as ao,
                nc.sbuf_tensor("aa_qtile", [128, NTOK], BF16) as qtile,
            ):
                for f in range(NFT):
                    with nc.Block() as block:
                        p0 = ctx.val("pe")
                        v0 = ctx.val("vec")
                        d0 = ctx.val("dma")

                        @block.sync
                        def _(sync, f=f):
                            dma(ctx, sync, qtile[:], qraw_d.ap()[f])

                        @block.tensor
                        def _(tensor, f=f, d0=d0):
                            tensor.wait_ge(ctx.h("dma"), d0 + 16)
                            n = 0
                            for bidx in range(B):
                                for chk in range(T // 512):
                                    if n >= 1:
                                        tensor.wait_ge(ctx.h("vec"), v0 + 2 * n)
                                    tensor.matmul(
                                        zbp[:], ctx.ef_const[:, f * 128:(f + 1) * 128],
                                        z2b[:, bidx * T + chk * 512:bidx * T + (chk + 1) * 512],
                                        start=True, stop=True).then_inc(ctx.inc("pe"), 1)
                                    tensor.matmul(
                                        opp[:], kv2[:, (bidx * NFT + f) * 128:(bidx * NFT + f + 1) * 128],
                                        qtile[:, bidx * T + chk * 512:
                                              bidx * T + (chk + 1) * 512],
                                        start=True, stop=True).then_inc(ctx.inc("pe"), 1)
                                    n += 1

                        @block.vector
                        def _(vector, f=f):
                            n = 0
                            for bidx in range(B):
                                for chk in range(T // 512):
                                    c = slice(bidx * T + chk * 512,
                                              bidx * T + (chk + 1) * 512)
                                    vector.wait_ge(ctx.h("pe"), p0 + 2 * n + 1)
                                    vector.tensor_copy(zbs[:], zbp[:]).then_inc(
                                        ctx.inc("vec"), 1)
                                    vector.wait_ge(ctx.h("pe"), p0 + 2 * n + 2)
                                    vector.drain()
                                    vector.tensor_mul(ao[:, c], opp[:], zbs[:]).then_inc(
                                        ctx.inc("vec"), 1)
                                    n += 1

                        v1 = ctx.val("vec")

                        @block.scalar
                        def _(scalar, f=f):
                            scalar.wait_ge(ctx.h("vec"), v1)
                            dma(ctx, scalar, kraw_d.ap()[f], ao[:], sem="dmb")
                            scalar.wait_ge(ctx.h("dmb"), ctx.val("dmb"))
            barrier_block(ctx)

        ao_all = es.enter_context(
            nc.sbuf_tensor("aa_aoall", [128, NFT * NTOK], BF16, side="right"))
        with nc.Block() as block:
            d0 = ctx.val("dma")

            @block.sync
            def _(sync):
                for t in range(NFT):
                    dma(ctx, sync, ao_all[:, t * NTOK:(t + 1) * NTOK],
                        kraw_d.ap()[t])
                wait_all(ctx, sync, "dma")
        build_proj_residual(ctx, ao_all[:], NFT, W["L1_attn_a_Wo"],
                            W["L1_attn_a_bo"], xres)
    barrier_block(ctx)


def build_ffn1_prep(ctx, W):
    nc = ctx.nc
    ctx.f1_a1 = ctx._stack_append(nc.sbuf_tensor("f1_a1", [128, 32], F32))
    ctx.f1_a2 = ctx._stack_append(nc.sbuf_tensor("f1_a2", [128, NFT], F32))
    ctx.f1_b1 = ctx._stack_append(nc.sbuf_tensor("f1_b1", [128, 32], F32))
    ctx.f1_b2 = ctx._stack_append(nc.sbuf_tensor("f1_b2", [128, NFT], F32))
    wtile = ctx.scr_f32[:, 0:2 * D]
    with nc.Block() as block:
        q0 = [ctx.val("dq0"), ctx.val("dq1")]
        v0 = ctx.val("vec")

        @block.sync
        def _(sync):
            for ht in range(32):
                if ht >= 2:
                    sync.wait_ge(ctx.h("vec"), v0 + ht - 1)
                dma(ctx, sync, wtile[:, (ht % 2) * D:(ht % 2) * D + D],
                    W["L1_ffn_w1"][ht * 128:(ht + 1) * 128, :], sem=f"dq{ht % 2}")
            for ft in range(NFT):
                dma(ctx, sync, ctx.f1_b1[:, 0:32],
                    W["L1_ffn_b1"][0:0, :]) if False else None
            for bi in range(32):
                dma(ctx, sync, ctx.f1_b1[:, bi:bi + 1],
                    W["L1_ffn_b1"][bi * 128:(bi + 1) * 128, :])
            for bi in range(NFT):
                dma(ctx, sync, ctx.f1_b2[:, bi:bi + 1],
                    W["L1_ffn_b2"][bi * 128:(bi + 1) * 128, :])

        @block.vector
        def _(vector):
            for ht in range(32):
                vector.wait_ge(ctx.h(f"dq{ht % 2}"), q0[ht % 2] + 16 * (ht // 2 + 1))
                vector.reduce_sum(ctx.f1_a1[:, ht:ht + 1],
                                  wtile[:, (ht % 2) * D:(ht % 2) * D + D],
                                  axis=mybir.AxisListType.X,
                                  apply_absolute_value=True)
                vector.drain()
                vector.tensor_scalar_mul(ctx.f1_a1[:, ht:ht + 1],
                                         ctx.f1_a1[:, ht:ht + 1], 1.0 / D)
                vector.drain().then_inc(ctx.inc("vec"), 1)
    barrier_block(ctx)
    w2tile = ctx.scr_f32[:, 0:NTOK]
    for dt in range(NFT):
        with nc.Block() as block:
            d0 = ctx.val("dma")

            @block.sync
            def _(sync, dt=dt):
                dma(ctx, sync, w2tile, W["L1_ffn_w2"][dt * 128:(dt + 1) * 128, :])

            @block.vector
            def _(vector, dt=dt, d0=d0):
                vector.wait_ge(ctx.h("dma"), d0 + 16)
                vector.reduce_sum(ctx.f1_a2[:, dt:dt + 1], w2tile,
                                  axis=mybir.AxisListType.X,
                                  apply_absolute_value=True)
                vector.drain()
                vector.tensor_scalar_mul(ctx.f1_a2[:, dt:dt + 1],
                                         ctx.f1_a2[:, dt:dt + 1],
                                         1.0 / (4 * D))
                vector.drain().then_inc(ctx.inc("vec"), 1)
    barrier_block(ctx)


def build_ffn1(ctx, xres, W, h_dram):
    nc = ctx.nc
    with contextlib.ExitStack() as es:
        es_xn = contextlib.ExitStack()
        xn_all = es_xn.enter_context(
            nc.sbuf_tensor("f1_xn", [128, NFT * NTOK], BF16, side="right"))
        xn = [xn_all[:, i * NTOK:(i + 1) * NTOK] for i in range(NFT)]
        build_layernorm(ctx, xres, xn)

        for ht in range(32):
            hst = ctx.scr_bf[:, (ht % 2) * NTOK:(ht % 2) * NTOK + NTOK]

            def evac_h(of, ch, ps, eng, ht=ht, hst=hst):
                return eng.activation(hst[:, ch * 512:(ch + 1) * 512], ps[:],
                                      AF.Gelu,
                                      scale=ctx.f1_a1[:, ht:ht + 1],
                                      bias=ctx.f1_b1[:, ht:ht + 1])
            build_proj(ctx, xn, W["L1_ffn_w1T"], 1, evac_h, of0=ht,
                       evac_eng="scl", conv_func=AF.Sign)
            with nc.Block() as block:
                @block.scalar
                def _(scalar, ht=ht, hst=hst):
                    dma(ctx, scalar, h_dram.ap()[ht], hst, sem="dmb")
                    scalar.wait_ge(ctx.h("dmb"), ctx.val("dmb") - 16)
        with nc.Block() as block:
            @block.scalar
            def _(scalar):
                scalar.wait_ge(ctx.h("dmb"), ctx.val("dmb"))
        es_xn.close()

        with contextlib.ExitStack() as e2:
            hblk = e2.enter_context(
                nc.sbuf_tensor("f1_hblk", [128, 32 * 512], BF16, side="right"))
            sgw = e2.enter_context(
                nc.sbuf_tensor("f1_sgw", [128, NFT * 32 * 128], BF16))
            wfst = e2.enter_context(nc.sbuf_tensor("f1_wf", [128, 2 * 128], F32))
            p1 = e2.enter_context(nc.psum_tensor("f1_p1", [128, 512], F32))
            p2 = e2.enter_context(nc.psum_tensor("f1_p2", [128, 512], F32))
            xf = ctx.scr_f32[:, 0:NTOK]
            of_out = ctx.scr_f32[:, NTOK:2 * NTOK]
            with nc.Block() as block:
                q0 = [ctx.val("dq0"), ctx.val("dq1")]
                g0 = ctx.val("gps")

                @block.sync
                def _(sync):
                    n = 0
                    for of in range(NFT):
                        for it in range(32):
                            if n >= 2:
                                sync.wait_ge(ctx.h("gps"), g0 + n - 1)
                            dma(ctx, sync,
                                wfst[:, (n % 2) * 128:(n % 2) * 128 + 128],
                                W["L1_ffn_w2T"][it * 128:(it + 1) * 128,
                                                of * 128:(of + 1) * 128],
                                sem=f"dq{n % 2}")
                            n += 1

                @block.scalar
                def _(scalar):
                    n = 0
                    for of in range(NFT):
                        for it in range(32):
                            scalar.wait_ge(ctx.h(f"dq{n % 2}"),
                                           q0[n % 2] + 16 * (n // 2 + 1))
                            idx = of * 32 + it
                            scalar.activation(
                                sgw[:, idx * 128:(idx + 1) * 128],
                                wfst[:, (n % 2) * 128:(n % 2) * 128 + 128],
                                AF.Sign).then_inc(ctx.inc("gps"), 1)
                            n += 1
            barrier_block(ctx)
            for ch in range(NTOK // 512):
                with nc.Block() as block:
                    b0 = ctx.val("dmg")
                    p0 = ctx.val("pe")
                    v0 = ctx.val("vec")

                    d0 = ctx.val("dma")

                    @block.sync
                    def _(sync, ch=ch):
                        for ht in range(32):
                            src = bass.AP(h_dram.ap().tensor,
                                          ht * 128 * NTOK + ch * 512,
                                          [[NTOK, 128], [1, 512]])
                            dma(ctx, sync, hblk[:, ht * 512:(ht + 1) * 512], src)

                    @block.gpsimd
                    def _(gpsimd, ch=ch):
                        for of in range(NFT):
                            gpsimd.wait_ge(ctx.h("vec"), v0 + max(of - 1, 0))
                            dma(ctx, gpsimd,
                                xf[:, (of % 2) * 512:(of % 2) * 512 + 512],
                                xres.ap()[of, :, ch * 512:(ch + 1) * 512], sem="dmg")
                            gpsimd.wait_ge(ctx.h("vec"), v0 + of + 1)
                            dma(ctx, gpsimd,
                                xres.ap()[of, :, ch * 512:(ch + 1) * 512],
                                of_out[:, (of % 2) * 512:(of % 2) * 512 + 512],
                                sem="dmg")

                    @block.tensor
                    def _(tensor, ch=ch, d0=d0):
                        tensor.wait_ge(ctx.h("dma"), d0 + 16 * 32)
                        for of in range(NFT):
                            ps = p1 if of % 2 == 0 else p2
                            if of >= 2:
                                tensor.wait_ge(ctx.h("vec"), v0 + of - 1)
                            for it in range(32):
                                idx = of * 32 + it
                                i = tensor.matmul(
                                    ps[:], sgw[:, idx * 128:(idx + 1) * 128],
                                    hblk[:, it * 512:(it + 1) * 512],
                                    start=(it == 0), stop=(it == 31))
                            i.then_inc(ctx.inc("pe"), 1)

                    @block.vector
                    def _(vector, ch=ch, b0=b0):
                        for of in range(NFT):
                            ps = p1 if of % 2 == 0 else p2
                            vector.wait_ge(ctx.h("pe"), p0 + of + 1)
                            vector.wait_ge(ctx.h("dmg"), b0 + 16 * (2 * of + 1))
                            c = slice((of % 2) * 512, (of % 2) * 512 + 512)
                            vector.tensor_scalar(of_out[:, c], ps[:],
                                                 ctx.f1_a2[:, of:of + 1],
                                                 ctx.f1_b2[:, of:of + 1],
                                                 ALU.mult, ALU.add)
                            vector.drain()
                            vector.tensor_add(of_out[:, c], of_out[:, c],
                                              xf[:, c]).then_inc(ctx.inc("vec"), 1)
                        wait_all(ctx, vector, "dmg")
    barrier_block(ctx)


def build_rwkv(ctx, xres, W, h2_d):
    nc = ctx.nc
    NG = NFT
    with contextlib.ExitStack() as es:
        if getattr(ctx, "debug", False):
            kw = {"kind": "ExternalOutput"}
        else:
            kw = {}
        xb_d = nc.dram_tensor("rw_xb", [NFT, 128, NTOK], F32, **kw)
        kvr_d = nc.dram_tensor("rw_kvr", [3 * NFT, 128, NTOK], F32, **kw)
        x1_d = nc.dram_tensor("rw_x1", [NFT, 128, NTOK], F32, **kw)
        rr_d = nc.dram_tensor("rw_rr", [NFT, 128, NTOK], F32, **kw)

        cols = es.enter_context(nc.sbuf_tensor("rw_cols", [128, 12 * NG + 1], F32))
        lncols = es.enter_context(nc.sbuf_tensor("rw_lnc", [128, 4 * NG], F32))
        es_tau = contextlib.ExitStack()
        tau = es_tau.enter_context(nc.sbuf_tensor("rw_tau", [128, T], F32))
        CW, CU, CTK, COTK, CTV, COTV, CTR, COTR, CCK, COCK, CCR, COCR = \
            [i * NG for i in range(12)]
        zcol = cols[:, 12 * NG:12 * NG + 1]

        def col(base, i):
            return cols[:, base + i:base + i + 1]

        with nc.Block() as block:
            d0 = ctx.val("dma")

            @block.sync
            def _(sync):
                dma(ctx, sync, tau[:], W["const_tau"][:])
                for gi, pname in ((CW, "L1_attn_b_time_decay"),
                                  (CU, "L1_attn_b_time_first"),
                                  (CTK, "L1_attn_b_tmk"),
                                  (CTV, "L1_attn_b_tmv"),
                                  (CTR, "L1_attn_b_tmr"),
                                  (CCK, "L1_attn_b_cmk"),
                                  (CCR, "L1_attn_b_cmr")):
                    for ft in range(NFT):
                        dma(ctx, sync, cols[:, gi + ft:gi + ft + 1],
                            W[pname][ft * 128:(ft + 1) * 128, :])
                for gi, pname in ((0, "L1_attn_b_g1"), (NG, "L1_attn_b_b1"),
                                  (2 * NG, "L1_attn_b_g2"),
                                  (3 * NG, "L1_attn_b_b2")):
                    for ft in range(NFT):
                        dma(ctx, sync, lncols[:, gi + ft:gi + ft + 1],
                            W[pname][ft * 128:(ft + 1) * 128, :])

            @block.vector
            def _(vector):
                vector.wait_ge(ctx.h("dma"), d0 + 16 * (1 + 7 * NFT + 4 * NFT))
                vector.memset(zcol, 0.0)
                vector.drain()
                for src, dst in ((CTK, COTK), (CTV, COTV), (CTR, COTR),
                                 (CCK, COCK), (CCR, COCR)):
                    vector.tensor_scalar(cols[:, dst:dst + NG],
                                         cols[:, src:src + NG], -1.0, 1.0,
                                         ALU.mult, ALU.add)
                vector.drain().then_inc(ctx.inc("vec"), 1)
        barrier_block(ctx)

        g1c = [lncols[:, i:i + 1] for i in range(NG)]
        b1c = [lncols[:, NG + i:NG + i + 1] for i in range(NG)]
        g2c = [lncols[:, 2 * NG + i:2 * NG + i + 1] for i in range(NG)]
        b2c = [lncols[:, 3 * NG + i:3 * NG + i + 1] for i in range(NG)]

        es_xn = contextlib.ExitStack()
        xn_all = es_xn.enter_context(
            nc.sbuf_tensor("rw_xn", [128, NFT * NTOK], BF16, side="right"))
        xn = [xn_all[:, i * NTOK:(i + 1) * NTOK] for i in range(NFT)]
        build_layernorm(ctx, xres, xn, fp32_dram=xb_d)
        build_layernorm(ctx, xb_d, xn, gain=g1c, bias=b1c)

        for of in range(3 * NFT):
            st32 = ctx.scr_f32[:, (of % 2) * NTOK:(of % 2) * NTOK + NTOK]

            def evac_kvr(o, ch, ps, eng, st32=st32):
                return eng.tensor_copy(st32[:, ch * 512:(ch + 1) * 512], ps[:])
            build_proj(ctx, xn, W["L1_attn_b_W_tm"], 1, evac_kvr, of0=of,
                       evac_eng="vec")
            with nc.Block() as block:
                @block.scalar
                def _(scalar, of=of, st32=st32):
                    dma(ctx, scalar, kvr_d.ap()[of], st32, sem="dmb")
                    scalar.wait_ge(ctx.h("dmb"), ctx.val("dmb") - 16)
        with nc.Block() as block:
            @block.scalar
            def _(scalar):
                scalar.wait_ge(ctx.h("dmb"), ctx.val("dmb"))
        es_xn.close()

        with contextlib.ExitStack() as eb:
            ED = eb.enter_context(nc.sbuf_tensor("rw_ed", [128, T], F32))
            P1 = eb.enter_context(nc.sbuf_tensor("rw_p1", [128, T], F32))
            S3 = eb.enter_context(nc.sbuf_tensor("rw_s3", [128, T], F32))
            WT = eb.enter_context(nc.sbuf_tensor("rw_wt", [128, T], F32))
            ONE = eb.enter_context(nc.sbuf_tensor("rw_one", [128, T], F32))
            B1 = eb.enter_context(nc.sbuf_tensor("rw_b1", [128, T + 1], F32))
            A1 = eb.enter_context(nc.sbuf_tensor("rw_a1", [128, T + 1], F32))
            KB = ctx.scr_f32[:, 0:T]
            VB = ctx.scr_f32[:, T:2 * T]
            RB = ctx.scr_f32[:, 2 * T:3 * T]
            XB = ctx.scr_f32[:, 3 * T:4 * T]
            S1 = ctx.scr_f32b[:, 0:T]
            S2 = ctx.scr_f32b[:, T:2 * T]
            with nc.Block() as block:
                @block.vector
                def _(vector):
                    vector.memset(ONE[:], 1.0)
                    vector.memset(ED[:, 0:1], 0.0)
                    vector.memset(B1[:, 0:1], -1e38)
                    vector.memset(A1[:, 0:1], 0.0)
                    vector.drain().then_inc(ctx.inc("vec"), 1)
            barrier_block(ctx)

            for ct in range(NFT):
                wcol = col(CW, ct)
                ucol = col(CU, ct)
                with nc.Block() as block:
                    @block.vector
                    def _(vector, wcol=wcol):
                        vector.tensor_scalar(WT[:], tau[:], wcol, zcol,
                                             ALU.mult, ALU.add)
                        vector.drain().then_inc(ctx.inc("vec"), 1)
                for b in range(B):
                    sl = slice(b * T, (b + 1) * T)
                    prog = []

                    def V(fn):
                        prog.append(("V", fn))

                    def S(fn):
                        prog.append(("S", fn))

                    def ts_mul(out, in_, c):
                        V(lambda e, out=out, in_=in_, c=c: e.tensor_scalar(
                            out, in_, c, zcol, ALU.mult, ALU.add))

                    def ts_addc(out, in_, c):
                        V(lambda e, out=out, in_=in_, c=c: e.tensor_scalar(
                            out, in_, c, zcol, ALU.add, ALU.add))

                    def tt(out, a, bb, op):
                        V(lambda e, out=out, a=a, bb=bb, op=op:
                          e.tensor_tensor(out, a, bb, op))

                    def sexp(x):
                        S(lambda e, x=x: e.activation(x, x, AF.Exp))

                    for tcol, ocol, buf in ((CTK, COTK, KB), (CTV, COTV, VB),
                                            (CTR, COTR, RB)):
                        tc, oc = col(tcol, ct), col(ocol, ct)
                        ts_mul(S3[:, 1:T], XB[:, 0:T - 1], oc)
                        ts_mul(S3[:, 0:1], XB[:, 0:1], oc)
                        ts_mul(buf, buf, tc)
                        tt(buf, buf, S3[:], ALU.add)
                    S(lambda e: e.activation(RB, RB, AF.Sigmoid))

                    tt(S1, KB, WT[:], ALU.subtract)
                    V(lambda e: e.tensor_tensor_scan(
                        P1[:], ONE[:], S1, -1e38, ALU.mult, ALU.max))
                    tt(ED[:, 1:T], P1[:, 0:T - 1], P1[:, 1:T], ALU.subtract)
                    S(lambda e: e.activation(ED[:, 1:T], ED[:, 1:T], AF.Exp))
                    tt(S2, S1, P1[:], ALU.subtract)
                    sexp(S2)
                    tt(P1[:], P1[:], WT[:], ALU.add)
                    V(lambda e: e.tensor_tensor_scan(
                        S3[:], ED[:], S2, 0.0, ALU.mult, ALU.add))
                    S(lambda e: e.activation(S3[:], S3[:], AF.Ln))
                    tt(B1[:, 1:T + 1], S3[:], P1[:], ALU.add)
                    ts_addc(S3[:], B1[:, 0:T], wcol)
                    tt(S3[:], S3[:], KB, ALU.max)
                    tt(S3[:], S3[:], P1[:], ALU.subtract)
                    sexp(S3[:])
                    V(lambda e: e.tensor_scalar_mul(S3[:], S3[:], 1e-8))
                    tt(S2, S2, S3[:], ALU.add)
                    V(lambda e: e.tensor_tensor_scan(
                        S3[:], ED[:], S2, 0.0, ALU.mult, ALU.add))
                    S(lambda e: e.activation(S3[:], S3[:], AF.Ln))
                    tt(B1[:, 1:T + 1], S3[:], P1[:], ALU.add)
                    ts_addc(S1, B1[:, 0:T], wcol)
                    tt(S1, S1, KB, ALU.max)
                    ts_addc(S2, B1[:, 0:T], wcol)
                    tt(S2, S2, S1, ALU.subtract)
                    sexp(S2)
                    tt(S3[:], KB, S1, ALU.subtract)
                    sexp(S3[:])
                    tt(S3[:], S3[:], VB, ALU.mult)
                    V(lambda e: e.tensor_tensor_scan(
                        A1[:, 1:T + 1], S2, S3[:], 0.0, ALU.mult, ALU.add))
                    ts_addc(KB, KB, ucol)
                    tt(S1, B1[:, 0:T], KB, ALU.max)
                    tt(S2, B1[:, 0:T], S1, ALU.subtract)
                    sexp(S2)
                    tt(KB, KB, S1, ALU.subtract)
                    S(lambda e: e.activation(KB, KB, AF.Exp))
                    tt(S1, S2, KB, ALU.add)
                    V(lambda e: e.tensor_scalar_add(S1, S1, 1e-8))
                    tt(S3[:], KB, VB, ALU.mult)
                    tt(S2, S2, A1[:, 0:T], ALU.mult)
                    tt(S2, S2, S3[:], ALU.add)
                    V(lambda e: e.reciprocal(S1, S1))
                    tt(VB, S2, S1, ALU.mult)
                    tt(VB, VB, RB, ALU.mult)
                    tt(VB, VB, XB, ALU.add)

                    nv_tot = sum(1 for t_, _ in prog if t_ == "V")
                    with nc.Block() as block:
                        d0 = ctx.val("dma")
                        v0 = ctx.val("vec")
                        s0 = ctx.val("scl")

                        @block.sync
                        def _(sync, ct=ct, sl=sl):
                            dma(ctx, sync, KB, kvr_d.ap()[ct, :, sl])
                            dma(ctx, sync, VB, kvr_d.ap()[NFT + ct, :, sl])
                            dma(ctx, sync, RB, kvr_d.ap()[2 * NFT + ct, :, sl])
                            dma(ctx, sync, XB, xb_d.ap()[ct, :, sl])

                        @block.vector
                        def _(vector, prog=prog, d0=d0, s0=s0):
                            vector.wait_ge(ctx.h("dma"), d0 + 16 * 4)
                            ns = 0
                            for eng, fn in prog:
                                if eng == "V":
                                    vector.wait_ge(ctx.h("scl"), s0 + ns)
                                    vector.drain()
                                    fn(vector).then_inc(ctx.inc("vec"), 1)
                                else:
                                    ns += 1

                        @block.scalar
                        def _(scalar, prog=prog, v0=v0, ct=ct, sl=sl,
                              nv_tot=nv_tot):
                            nv = 0
                            for eng, fn in prog:
                                if eng == "V":
                                    nv += 1
                                else:
                                    scalar.wait_ge(ctx.h("vec"), v0 + nv)
                                    fn(scalar).then_inc(ctx.inc("scl"), 1)
                            scalar.wait_ge(ctx.h("vec"), v0 + nv_tot)
                            dma(ctx, scalar, x1_d.ap()[ct, :, sl], VB,
                                sem="dmb")
                            scalar.wait_ge(ctx.h("dmb"), ctx.val("dmb"))
            barrier_block(ctx)
        es_tau.close()

        es_xn = contextlib.ExitStack()
        xn_all = es_xn.enter_context(
            nc.sbuf_tensor("rw_xn2", [128, NFT * NTOK], BF16, side="right"))
        xn2 = [xn_all[:, i * NTOK:(i + 1) * NTOK] for i in range(NFT)]
        build_layernorm(ctx, x1_d, xn2, gain=g2c, bias=b2c)

        with contextlib.ExitStack() as ec:
            u32 = ec.enter_context(nc.sbuf_tensor("rw_u32", [128, 1024], F32))
            Pm = ctx.scr_f32[:, 0:NTOK]
            x1t = ctx.scr_f32[:, NTOK:2 * NTOK]
            rrb = ctx.scr_f32b[:, 0:NTOK]
            h2st = ctx.scr_bf[:, 0:NTOK]
            for ft in range(NFT):
                ckc, ock = col(CCK, ft), col(COCK, ft)
                crc, ocr = col(CCR, ft), col(COCR, ft)
                with nc.Block() as block:
                    d0 = ctx.val("dma")
                    v0 = ctx.val("vec")

                    @block.sync
                    def _(sync, ft=ft):
                        dma(ctx, sync, x1t, x1_d.ap()[ft])

                    @block.vector
                    def _(vector, ft=ft, ock=ock, ocr=ocr, crc=crc, d0=d0):
                        vector.wait_ge(ctx.h("dma"), d0 + 16)
                        for b in range(B):
                            o = b * T
                            vector.tensor_scalar(
                                Pm[:, o + 1:o + T], x1t[:, o:o + T - 1],
                                ock, zcol, ALU.mult, ALU.add)
                            vector.tensor_scalar(
                                Pm[:, o:o + 1], x1t[:, o + T - 1:o + T],
                                ock, zcol, ALU.mult, ALU.add)
                            vector.tensor_scalar(
                                rrb[:, o + 1:o + T], x1t[:, o:o + T - 1],
                                ocr, zcol, ALU.mult, ALU.add)
                            vector.tensor_scalar(
                                rrb[:, o:o + 1], x1t[:, o + T - 1:o + T],
                                ocr, zcol, ALU.mult, ALU.add)
                        vector.drain()
                        vector.scalar_tensor_tensor(
                            rrb[:], xn2[ft], crc, rrb[:], ALU.mult, ALU.add)
                        vector.drain().then_inc(ctx.inc("vec"), 1)

                    @block.scalar
                    def _(scalar, ft=ft, v0=v0):
                        scalar.wait_ge(ctx.h("vec"), v0 + 1)
                        scalar.activation(rrb[:], rrb[:],
                                          AF.Sigmoid).then_inc(ctx.inc("scl"), 1)
                        scalar.wait_ge(ctx.h("scl"), ctx.val("scl"))
                        dma(ctx, scalar, rr_d.ap()[ft], rrb[:], sem="dmb")
                        scalar.wait_ge(ctx.h("dmb"), ctx.val("dmb"))
                for j in range(4):
                    ht = ft + NFT * j
                    h2st = ctx.scr_bf[:, (j % 2) * NTOK:(j % 2) * NTOK + NTOK]

                    def evac_kk(o, ch, ps, eng, ckc=ckc, h2st=h2st):
                        c = slice(ch * 512, (ch + 1) * 512)
                        cu = slice((ch % 2) * 512, (ch % 2) * 512 + 512)
                        eng.scalar_tensor_tensor(u32[:, cu], ps[:], ckc,
                                                 Pm[:, c], ALU.mult, ALU.add)
                        eng.drain()
                        eng.tensor_scalar_max(u32[:, cu], u32[:, cu], 0.0)
                        eng.drain()
                        return eng.tensor_tensor(h2st[:, c], u32[:, cu],
                                                 u32[:, cu], ALU.mult)
                    build_proj(ctx, xn2, W["L1_attn_b_W_cm"], 1, evac_kk,
                               of0=ht, evac_eng="vec")
                    with nc.Block() as block:
                        @block.scalar
                        def _(scalar, ht=ht, h2st=h2st):
                            dma(ctx, scalar, h2_d.ap()[ht], h2st, sem="dmb")
                            scalar.wait_ge(ctx.h("dmb"), ctx.val("dmb") - 16)
        with nc.Block() as block:
            @block.scalar
            def _(scalar):
                scalar.wait_ge(ctx.h("dmb"), ctx.val("dmb"))
        es_xn.close()

        with contextlib.ExitStack() as e2:
            hblk = e2.enter_context(
                nc.sbuf_tensor("rw_hblk", [128, 32 * 512], BF16, side="right"))
            cw = e2.enter_context(
                nc.sbuf_tensor("rw_cw", [128, NFT * 32 * 128], BF16))
            wfst = e2.enter_context(nc.sbuf_tensor("rw_wf", [128, 2 * 128], F32))
            p1 = e2.enter_context(nc.psum_tensor("rw_p1", [128, 512], F32))
            p2 = e2.enter_context(nc.psum_tensor("rw_p2", [128, 512], F32))
            xf = ctx.scr_f32[:, 0:1024]
            x1f = ctx.scr_f32[:, 1024:2048]
            rrf = ctx.scr_f32[:, 2048:3072]
            of_out = ctx.scr_f32[:, 3072:4096]
            with nc.Block() as block:
                q0 = [ctx.val("dq0"), ctx.val("dq1")]
                g0 = ctx.val("gps")

                @block.sync
                def _(sync):
                    n = 0
                    for of in range(NFT):
                        for it in range(32):
                            if n >= 2:
                                sync.wait_ge(ctx.h("gps"), g0 + n - 1)
                            dma(ctx, sync,
                                wfst[:, (n % 2) * 128:(n % 2) * 128 + 128],
                                W["L1_attn_b_W_cp"][it * 128:(it + 1) * 128,
                                                    of * 128:(of + 1) * 128],
                                sem=f"dq{n % 2}")
                            n += 1

                @block.scalar
                def _(scalar):
                    n = 0
                    for of in range(NFT):
                        for it in range(32):
                            scalar.wait_ge(ctx.h(f"dq{n % 2}"),
                                           q0[n % 2] + 16 * (n // 2 + 1))
                            idx = of * 32 + it
                            scalar.activation(
                                cw[:, idx * 128:(idx + 1) * 128],
                                wfst[:, (n % 2) * 128:(n % 2) * 128 + 128],
                                AF.Copy).then_inc(ctx.inc("gps"), 1)
                            n += 1
            barrier_block(ctx)
            for ch in range(NTOK // 512):
                with nc.Block() as block:
                    b0 = ctx.val("dmg")
                    p0 = ctx.val("pe")
                    v0 = ctx.val("vec")
                    d0 = ctx.val("dma")

                    @block.sync
                    def _(sync, ch=ch):
                        for ht in range(32):
                            src = bass.AP(h2_d.ap().tensor,
                                          ht * 128 * NTOK + ch * 512,
                                          [[NTOK, 128], [1, 512]])
                            dma(ctx, sync, hblk[:, ht * 512:(ht + 1) * 512], src)

                    @block.gpsimd
                    def _(gpsimd, ch=ch):
                        for of in range(NFT):
                            gpsimd.wait_ge(ctx.h("vec"), v0 + max(of - 1, 0))
                            c2 = slice((of % 2) * 512, (of % 2) * 512 + 512)
                            dma(ctx, gpsimd, xf[:, c2],
                                xres.ap()[of, :, ch * 512:(ch + 1) * 512],
                                sem="dmg")
                            dma(ctx, gpsimd, x1f[:, c2],
                                x1_d.ap()[of, :, ch * 512:(ch + 1) * 512],
                                sem="dmg")
                            dma(ctx, gpsimd, rrf[:, c2],
                                rr_d.ap()[of, :, ch * 512:(ch + 1) * 512],
                                sem="dmg")
                            gpsimd.wait_ge(ctx.h("vec"), v0 + of + 1)
                            dma(ctx, gpsimd,
                                xres.ap()[of, :, ch * 512:(ch + 1) * 512],
                                of_out[:, c2], sem="dmg")

                    @block.tensor
                    def _(tensor, ch=ch, d0=d0):
                        tensor.wait_ge(ctx.h("dma"), d0 + 16 * 32)
                        for of in range(NFT):
                            ps = p1 if of % 2 == 0 else p2
                            if of >= 2:
                                tensor.wait_ge(ctx.h("vec"), v0 + of - 1)
                            for it in range(32):
                                idx = of * 32 + it
                                i = tensor.matmul(
                                    ps[:],
                                    cw[:, idx * 128:(idx + 1) * 128],
                                    hblk[:, it * 512:(it + 1) * 512],
                                    start=(it == 0), stop=(it == 31))
                            i.then_inc(ctx.inc("pe"), 1)

                    @block.vector
                    def _(vector, ch=ch, b0=b0):
                        for of in range(NFT):
                            ps = p1 if of % 2 == 0 else p2
                            vector.wait_ge(ctx.h("pe"), p0 + of + 1)
                            vector.wait_ge(ctx.h("dmg"), b0 + 16 * (4 * of + 3))
                            c2 = slice((of % 2) * 512, (of % 2) * 512 + 512)
                            vector.tensor_tensor(of_out[:, c2], ps[:],
                                                 rrf[:, c2], ALU.mult)
                            vector.drain()
                            vector.tensor_add(of_out[:, c2], of_out[:, c2],
                                              xf[:, c2])
                            vector.drain()
                            vector.tensor_add(of_out[:, c2], of_out[:, c2],
                                              x1f[:, c2]).then_inc(
                                ctx.inc("vec"), 1)
                        wait_all(ctx, vector, "dmg")
    barrier_block(ctx)


class _UniqueNC:

    def __init__(self, nc):
        self._nc = nc
        self._n = 0

    def __getattr__(self, a):
        return getattr(self._nc, a)

    def sbuf_tensor(self, name, *a, **kw):
        self._n += 1
        return self._nc.sbuf_tensor(f"{name}_{self._n}", *a, **kw)

    def psum_tensor(self, name, *a, **kw):
        self._n += 1
        return self._nc.psum_tensor(f"{name}_{self._n}", *a, **kw)


def build_program(upto=99, debug=False):
    nc0 = bass.Bass("TRN2", target_bir_lowering=False,
                    detect_race_conditions=False)
    nc = _UniqueNC(nc0)
    ctx = Ctx(nc)
    ctx.debug = debug
    for s in ("dma", "dmb", "dmg", "pe", "vec", "scl", "gps", "dq0", "dq1"):
        ctx.new_sem(s)

    xin = nc.dram_tensor("xin", [B, D, T], F32, kind="ExternalInput").ap()
    xout = nc.dram_tensor("xout", [B, D, T], F32, kind="ExternalOutput").ap()
    xres = nc.dram_tensor("xres", [NFT, 128, NTOK], F32)
    W = make_weights_decl(nc)

    def glob(name, shape, dtype):
        hh = nc.sbuf_tensor(name, shape, dtype)
        tt = hh.__enter__()
        ctx._stack.append(hh)
        return tt
    ctx.ones_col = glob("c_ones_col", [128, 1], BF16)
    ctx.ones_row = glob("c_ones_row", [1, 128], F32)
    ctx.ident = glob("c_ident", [128, 128], BF16)
    ctx.utmask = glob("c_utmask", [64, 64], BF16)
    ctx.rep32 = glob("c_rep32", [32, 128], BF16)
    ctx.selhi = glob("c_selhi", [64, 128], BF16)
    ctx.ef_const = glob("c_ef", [16, NFT * 128], BF16)
    ctx.scr_f32 = glob("c_scrf32", [128, 2 * NTOK], F32)
    ctx.scr_f32b = glob("c_scrf32b", [128, NTOK], F32)
    ctx.scr_bf = glob("c_scrbf", [128, 2 * NTOK], BF16)
    ctx.scr_xr = glob("c_scrxr", [128, 2 * NTOK], BF16)
    ctx.stat = glob("c_stat", [1, 2 * T], F32)
    ctx.epscol = glob("c_epscol", [1, 1], F32)

    with nc.Block() as block:
        @block.sync
        def _(sync):
            dma(ctx, sync, ctx.ident[:], W["const_ident"][:])
            dma(ctx, sync, ctx.utmask[:], W["const_utmask"][:])
            dma(ctx, sync, ctx.rep32[:], W["const_rep32"][:])
            dma(ctx, sync, ctx.selhi[:], W["const_selhi"][:])
            dma(ctx, sync, ctx.ef_const[:], W["const_ef"][:])

        @block.vector
        def _(vector):
            vector.memset(ctx.ones_col[:], 1.0)
            vector.memset(ctx.ones_row[:], 1.0)
            vector.memset(ctx.epscol[:], LN_EPS)

    with nc.Block() as block:
        @block.sync
        def _(sync):
            for b in range(B):
                for ft in range(NFT):
                    dma(ctx, sync, xres.ap()[ft, :, b * T:(b + 1) * T],
                        xin[b, ft * 128:(ft + 1) * 128, :])
            wait_all(ctx, sync, "dma")

    if 1 <= upto < 20:
        with nc0.named_scope("gla"):
            build_gla(ctx, xres, W)
    if 1.5 <= upto < 20:
        build_ffn0_prep(ctx, W)
    if 1.8 <= upto < 20:
        with nc0.named_scope("ffn0a"):
            build_ffn0(ctx, xres, W, parts=(2 if upto >= 2 else 1))
    if upto == 30:
        xn_dram = nc.dram_tensor("xn_scr", [NFT, 128, NTOK], BF16)
        scr2 = nc.dram_tensor("scr2", [B, 16, 16, 64, 128], BF16)
        build_attnb(ctx, xres, W, xn_dram, scr2)
    if 2.2 <= upto < 30:
        xn_dram = nc.dram_tensor("xn_scr", [NFT, 128, NTOK], BF16)
        scr2 = nc.dram_tensor("scr2", [B, 16, 16, 64, 128], BF16)
        with nc0.named_scope("attnb"):
            build_attnb(ctx, xres, W, xn_dram, scr2,
                        parts=(4 if upto >= 3
                               else int((upto - 2) * 10 + 0.5) // 2))
    if 4 <= upto < 20:
        with nc0.named_scope("ffn0b"):
            build_ffn0(ctx, xres, W)
    if (5 <= upto < 20) or upto == 31:
        qraw_d = nc.dram_tensor("aa_qraw", [NFT, 128, NTOK], BF16)
        kraw_d = nc.dram_tensor("aa_kraw", [NFT, 128, NTOK], BF16)
        vraw_d = nc.dram_tensor("aa_vraw", [NFT, 128, NTOK], BF16)
        ks_d = nc.dram_tensor("aa_ksd", [1, 2 * D], F32)
        with nc0.named_scope("attna"):
            build_attna(ctx, xres, W, qraw_d, kraw_d, vraw_d, ks_d)
    if (5.5 <= upto < 20) or upto == 32:
        ctx.f1h_dram = nc.dram_tensor("f1_h", [32, 128, NTOK], BF16)
        build_ffn1_prep(ctx, W)
    if (6 <= upto < 20) or upto == 32:
        with nc0.named_scope("ffn1a"):
            build_ffn1(ctx, xres, W, ctx.f1h_dram)
    if (7 <= upto < 20) or upto == 33:
        if upto == 33:
            ctx.f1h_dram = nc.dram_tensor("f1_h", [32, 128, NTOK], BF16)
        with nc0.named_scope("rwkv"):
            build_rwkv(ctx, xres, W, ctx.f1h_dram)
    if 8 <= upto < 20:
        with nc0.named_scope("ffn1b"):
            build_ffn1(ctx, xres, W, ctx.f1h_dram)

    with nc.Block() as block:
        @block.sync
        def _(sync):
            for b in range(B):
                for ft in range(NFT):
                    dma(ctx, sync, xout[b, ft * 128:(ft + 1) * 128, :],
                        xres.ap()[ft, :, b * T:(b + 1) * T])
            wait_all(ctx, sync, "dma")
    return nc


def make_in_maps(inputs, upto=99):
    x = np.asarray(inputs["L0_attn_a_hidden_state"], dtype=np.float32)
    shared = {}
    for n in ("L0_attn_a_Wq", "L0_attn_a_bq", "L0_attn_a_Wk", "L0_attn_a_bk",
              "L0_attn_a_Wg", "L0_attn_a_bg", "L0_attn_a_Wv", "L0_attn_a_bv",
              "L0_attn_a_Wo", "L0_attn_a_bo"):
        a = np.asarray(inputs[n], dtype=np.float32)
        shared[n] = a.reshape(-1, 1) if a.ndim == 1 else np.ascontiguousarray(a)
    shared["L0_ffn_kernel"] = np.ascontiguousarray(np.asarray(inputs["L0_ffn_kernel"]))
    shared["L0_ffn_input_scale"] = np.asarray(inputs["L0_ffn_input_scale"], dtype=np.float32).reshape(1, 1)
    shared["L0_ffn_kernel_scale"] = np.asarray(inputs["L0_ffn_kernel_scale"], dtype=np.float32).reshape(1, 1)
    shared["L0_ffn_bias"] = np.ascontiguousarray(np.asarray(inputs["L0_ffn_bias"]).reshape(-1, 1))
    for n in ("L0_attn_b_Wq", "L0_attn_b_bq", "L0_attn_b_Wk", "L0_attn_b_bk",
              "L0_attn_b_Wv", "L0_attn_b_bv", "L0_attn_b_Wo", "L0_attn_b_bo"):
        a = np.asarray(inputs[n], dtype=np.float32)
        shared[n] = a.reshape(-1, 1) if a.ndim == 1 else np.ascontiguousarray(a)
    shared["L0_attn_b_key_param"] = np.ascontiguousarray(
        np.asarray(inputs["L0_attn_b_key_param"], dtype=np.float32)[0].T)
    shared["const_krow"] = np.ascontiguousarray(
        np.tile(np.arange(1, 33, dtype=np.float32)[None, :], (128, 1)))
    shared["const_rep32"] = np.ascontiguousarray(
        np.tile(np.eye(32, dtype=ml_dtypes.bfloat16), (1, 4)))
    selhi = np.zeros((64, 128), dtype=ml_dtypes.bfloat16)
    for dd in range(64):
        selhi[dd, 64 + dd] = 1
    shared["const_selhi"] = selhi
    ef = np.zeros((16, 8 * 128), dtype=ml_dtypes.bfloat16)
    for f in range(8):
        for p in range(128):
            ef[2 * f + p // 64, f * 128 + p] = 1
    shared["const_ef"] = ef
    for n in ("L1_attn_a_Wq", "L1_attn_a_Wk", "L1_attn_a_Wv", "L1_attn_a_Wo",
              "L1_attn_a_bo", "L1_ffn_w1", "L1_ffn_b1", "L1_ffn_w2", "L1_ffn_b2"):
        a = np.asarray(inputs[n], dtype=np.float32)
        shared[n] = a.reshape(-1, 1) if a.ndim == 1 else np.ascontiguousarray(a)
    shared["L1_ffn_w1T"] = np.ascontiguousarray(
        np.asarray(inputs["L1_ffn_w1"], dtype=np.float32).T)
    shared["L1_ffn_w2T"] = np.ascontiguousarray(
        np.asarray(inputs["L1_ffn_w2"], dtype=np.float32).T)
    shared["const_ident"] = np.eye(128, dtype=ml_dtypes.bfloat16)
    shared["const_utmask"] = np.ascontiguousarray(
        np.triu(np.ones((64, 64), dtype=ml_dtypes.bfloat16)))
    for n in ("L1_attn_b_time_decay", "L1_attn_b_time_first",
              "L1_attn_b_g1", "L1_attn_b_b1", "L1_attn_b_g2", "L1_attn_b_b2",
              "L1_attn_b_tmk", "L1_attn_b_tmv", "L1_attn_b_tmr",
              "L1_attn_b_cmk", "L1_attn_b_cmr"):
        shared[n] = np.ascontiguousarray(
            np.asarray(inputs[n], dtype=np.float32).reshape(-1, 1))
    for n in ("L1_attn_b_W_tm", "L1_attn_b_W_cm", "L1_attn_b_W_cp"):
        shared[n] = np.ascontiguousarray(np.asarray(inputs[n], dtype=np.float32))
    shared["const_tau"] = np.ascontiguousarray(
        np.tile(np.arange(T, dtype=np.float32)[None, :], (128, 1)))
    maps = []
    for c in range(8):
        m = dict(shared)
        m["xin"] = np.ascontiguousarray(x[B * c:B * (c + 1)].transpose(0, 2, 1))
        maps.append(m)
    return maps


def kernel(**inputs):
    nc = build_program(upto=10)
    maps = make_in_maps(inputs)
    res = run_bass_kernel_spmd(nc, maps, core_ids=list(range(8)))
    out = np.empty((16, T, D), dtype=np.float32)
    for c in range(8):
        out[B * c:B * (c + 1)] = res.results[c]["xout"].transpose(0, 2, 1)
    return out
